# version 19
# baseline (speedup 1.0000x reference)
import math
import contextlib
import numpy as np

import concourse.bass as bass
import concourse.mybir as mybir
from concourse.bass_utils import run_bass_kernel_spmd

F32 = mybir.dt.float32
BF16 = mybir.dt.bfloat16
I32 = mybir.dt.int32
AF = mybir.ActivationFunctionType
ALU = mybir.AluOpType
AX = mybir.AxisListType

T = 2048
D = 1024
HF = 4096
NH = 16
HD = 64
NT = T // 128
RS = NT * 32
NCORES = 8


def set_dims(nt):
    global T, NT, RS
    NT = nt
    T = nt * 128
    RS = NT * 32

ROPE_THETA = 10000.0
TWO_PI = 2 * math.pi

ENGINES = ("tensor", "vector", "scalar", "gpsimd", "sync")
DEBUG = False


def bcast_free(ap, n):
    return bass.AP(ap.tensor, ap.offset, [list(ap.ap[0]), [0, n]])


def head_view(ap, col_off, nh, w, stride):
    return bass.AP(ap.tensor, ap.offset + col_off, [list(ap.ap[0]), [stride, nh], [1, w]])


def head_bcast(ap, col_off, nh, w):
    return bass.AP(ap.tensor, ap.offset + col_off, [list(ap.ap[0]), [0, nh], [1, w]])


def pair_view(dram_ap, c, ncols):
    off = c * 128 * ncols
    return bass.AP(dram_ap.tensor, dram_ap.offset + off,
                   [[ncols, 128], [128 * ncols, 2], [1, ncols]])


class Prog:
    def __init__(self, nc):
        self.nc = nc
        self.cmds = {e: [] for e in ENGINES}
        self.sems = {}
        self.cnt = {}
        self.hw = {e: {} for e in ENGINES}
        self.blocks = 0

    def sem(self, name):
        h = self.nc.semaphore(name).__enter__()
        self.sems[name] = h
        self.cnt[name] = 0
        return h

    def emit(self, eng, fn, inc=None, n=1):
        ev = None
        if inc is not None:
            self.cnt[inc] += n
            ev = (inc, self.cnt[inc])
            sem = self.sems[inc]

            def cl(obj, fn=fn, sem=sem, n=n):
                ins = fn(obj)
                assert ins is not None, "emitted fn with inc must return an instruction"
                ins.then_inc(sem, n)
        else:
            def cl(obj, fn=fn):
                fn(obj)
        self.cmds[eng].append(cl)
        return ev

    ENG_SEM = {"tensor": "pe", "vector": "dve", "scalar": "act", "gpsimd": "pool"}

    def gemit(self, eng, fn, n=1):
        ev = self.emit(eng, fn, inc=self.ENG_SEM[eng], n=n)
        self.wait(eng, ev)
        return ev

    def wait(self, eng, ev):
        if ev is None:
            return
        s, v = ev
        assert v <= self.cnt[s], f"wait on future event {s}={v} > {self.cnt[s]}"
        if self.hw[eng].get(s, 0) >= v:
            return
        self.hw[eng][s] = v
        sem = self.sems[s]
        self.cmds[eng].append(lambda obj, sem=sem, v=v: obj.wait_ge(sem, v))

    def wait_all(self, eng, evs):
        best = {}
        for ev in evs:
            if ev is None:
                continue
            s, v = ev
            if best.get(s, 0) < v:
                best[s] = v
        for s, v in best.items():
            self.wait(eng, (s, v))

    def flush_block(self, name):
        cmds = self.cmds
        self.cmds = {e: [] for e in ENGINES}
        nonempty = [e for e in ENGINES if cmds[e]]
        if not nonempty:
            return
        with self.nc.Block(f"b{self.blocks}_{name}") as blk:
            for e in nonempty:
                def replay(obj, lst=cmds[e]):
                    for c in lst:
                        c(obj)
                getattr(blk, e)(replay)
        self.blocks += 1
        for s in ("pe", "dve", "act", "pool"):
            if s in self.cnt:
                for e in ENGINES:
                    self.hw[e][s] = self.cnt[s]


class Res:
    __slots__ = ("ready", "cons")

    def __init__(self):
        self.ready = None
        self.cons = []

    def begin_write(self, p, eng):
        p.wait_all(eng, self.cons)

    def set_ready(self, ev):
        self.ready = ev
        self.cons = []

    def add_consumer(self, ev):
        if ev is not None:
            self.cons.append(ev)


def build_program(phases=("P0", "P1", "P3", "P5", "P6", "P7")):
    terminal = "P7" if "P7" in phases else ("P6" if "P6" in phases else "P3")
    nc = bass.Bass("TRN2", target_bir_lowering=False)
    dbg_tensors = {}

    def dbg_decl(name, shape, dt):
        if not DEBUG:
            return None
        dbg_tensors[name] = nc.dram_tensor(f"dbg_{name}", shape, dt,
                                           kind="ExternalOutput").ap()
        return dbg_tensors[name]

    x_in = nc.dram_tensor("x", [T, D], F32, kind="ExternalInput").ap()
    wqkvT = nc.dram_tensor("wqkvT", [D, 3 * D], F32, kind="ExternalInput").ap()
    woT = nc.dram_tensor("woT", [D, D], F32, kind="ExternalInput").ap()
    wfcT = nc.dram_tensor("wfcT", [D, HF], F32, kind="ExternalInput").ap()
    wpjT = nc.dram_tensor("wpjT", [HF, D], F32, kind="ExternalInput").ap()
    qnw = nc.dram_tensor("qnw", [1, HD], F32, kind="ExternalInput").ap()
    knw = nc.dram_tensor("knw", [1, HD], F32, kind="ExternalInput").ap()
    lnw = nc.dram_tensor("lnw", [32, 128], F32, kind="ExternalInput").ap()
    lnb = nc.dram_tensor("lnb", [32, 128], F32, kind="ExternalInput").ap()
    y_out = nc.dram_tensor("y", [T, D], F32, kind="ExternalOutput").ap()

    xs = nc.dram_tensor("xs", [T, D], F32, kind="Internal").ap()
    wqfc_s = nc.dram_tensor("wqfc_s", [D, HF], BF16, kind="Internal").ap()
    wqpj_s = nc.dram_tensor("wqpj_s", [HF, D], BF16, kind="Internal").ap()
    lrow_d = nc.dram_tensor("lrow_d", [4, 128], F32, kind="Internal").ap()

    es = contextlib.ExitStack()
    with es:
        p = Prog(nc)
        for s in ("dS", "dG", "dO", "pe", "dve", "act", "pool"):
            p.sem(s)

        def dbg_dump(name, src_ap, shape, dt, wait_evs=(), res=None):
            if not DEBUG:
                return
            d = dbg_decl(name, shape, dt)
            p.wait_all("sync", [e for e in wait_evs if e is not None])
            ev = p.emit("sync", lambda o, d=d, s=src_ap: o.dma_start(out=d, in_=s),
                        inc="dO", n=16)
            if res is not None:
                res.add_consumer(ev)

        def sb(name, shape, dt):
            return es.enter_context(nc.sbuf_tensor(name, shape, dt))

        ident_b = sb("ident_b", [128, 128], BF16)
        ident_f = sb("ident_f", [128, 128], F32)
        mask_b = sb("mask_b", [128, 128], BF16)
        ones_row_b = sb("ones_row_b", [1, 128], BF16)
        ones_row_f = sb("ones_row_f", [1, 128], F32)
        ones_col_f = sb("ones_col_f", [128, 1], F32)
        ones_col_b = sb("ones_col_b", [128, 1], BF16)
        ropeq = sb("ropeq", [128, 4 * RS], F32)
        ropek = sb("ropek", [128, 4 * RS], F32)
        halfb = sb("halfb", [128, 1024], F32)
        halfc = sb("halfc", [128, 1024], F32)
        lnwT = sb("lnwT", [128, 32], F32)
        lnbT = sb("lnbT", [128, 32], BF16)
        thr2 = sb("thr2", [128, 2], F32)
        st = sb("st", [128, 64], F32)
        gst = sb("gst", [128, 8], F32)
        cst = sb("cst", [128, 2], F32)

        def init_cst(g):
            g.memset(st[:], 0.0)
            g.memset(gst[:], 0.0)
            g.memset(cst[:, 0:1], 1e-5)
            return g.memset(cst[:, 1:2], 1e-6)
        p.emit("gpsimd", init_cst, inc="pool")
        p.flush_block("cst")

        evs = {}

        if "P0" in phases:
            with contextlib.ExitStack() as ps0:
                qn_sb = ps0.enter_context(nc.sbuf_tensor("qn_sb", [1, HD], F32))
                kn_sb = ps0.enter_context(nc.sbuf_tensor("kn_sb", [1, HD], F32))
                lnw_r = ps0.enter_context(nc.sbuf_tensor("lnw_r", [32, 128], F32))
                lnb_r = ps0.enter_context(nc.sbuf_tensor("lnb_r", [32, 128], BF16))
                wbc = ps0.enter_context(nc.sbuf_tensor("wbc", [128, 128], F32))
                ang = ps0.enter_context(nc.sbuf_tensor("ang", [128, RS], F32))
                ang2 = ps0.enter_context(nc.sbuf_tensor("ang2", [128, RS], F32))
                rt1 = ps0.enter_context(nc.sbuf_tensor("rt1", [128, RS], F32))
                rk32 = ps0.enter_context(nc.sbuf_tensor("rk32", [128, RS], I32))
                rkf = ps0.enter_context(nc.sbuf_tensor("rkf", [128, RS], F32))
                sin_t = ps0.enter_context(nc.sbuf_tensor("sin_t", [128, RS], F32))
                cos_t = ps0.enter_context(nc.sbuf_tensor("cos_t", [128, RS], F32))
                invf = ps0.enter_context(nc.sbuf_tensor("invf", [128, 32], F32))
                pos = ps0.enter_context(nc.sbuf_tensor("pos", [128, NT], F32))
                p0_ps = ps0.enter_context(nc.psum_tensor("p0_ps", [128, 1024], F32))
                _ = None
                p.emit("sync", lambda o: o.dma_start(out=qn_sb[:], in_=qnw), inc="dS", n=16)
                p.emit("sync", lambda o: o.dma_start(out=kn_sb[:], in_=knw), inc="dS", n=16)
                e_qkn = ("dS", p.cnt["dS"])
                e_lnw = p.emit("sync", lambda o: o.dma_start(out=lnw_r[:], in_=lnw), inc="dS", n=16)
                e_lnb = p.emit("gpsimd", lambda o: o.dma_start(out=lnb_r[:], in_=lnb), inc="dG", n=16)

                def g0a(g):
                    g.memset(ones_row_b[:], 1.0)
                    g.memset(ones_row_f[:], 1.0)
                    g.memset(ones_col_f[:], 1.0)
                    g.memset(ones_col_b[:], 1.0)
                    g.memset(ident_b[:], 1.0)
                    g.memset(ident_f[:], 1.0)
                    return g.memset(mask_b[:], 0.0)
                p.gemit("gpsimd", g0a)

                def g0b(g):
                    g.affine_select(mask_b[:], mask_b[:], [[-1, 128]], ALU.is_ge, -1e9,
                                    base=0, channel_multiplier=1)
                    g.affine_select(ident_b[:], ident_b[:], [[-1, 128]], ALU.is_equal, 0.0,
                                    base=0, channel_multiplier=1)
                    g.affine_select(ident_f[:], ident_f[:], [[-1, 128]], ALU.is_equal, 0.0,
                                    base=0, channel_multiplier=1)
                    g.iota(invf[:], [[1, 32]], base=0, channel_multiplier=0,
                           allow_small_or_imprecise_dtypes=True)
                    return g.iota(pos[:], [[128, NT]], base=0, channel_multiplier=1,
                                  allow_small_or_imprecise_dtypes=True)
                e_g0 = p.gemit("gpsimd", g0b)

                p.wait("scalar", e_g0)
                e_invf = p.emit("scalar", lambda o: o.activation(
                    invf[:], invf[:], AF.Exp, bias=0.0,
                    scale=-2.0 * math.log(ROPE_THETA) / HD), inc="act")

                p.wait("vector", e_invf)

                def mk_ang(v):
                    in0 = bass.AP(pos[:].tensor, pos[:].offset,
                                  [list(pos[:].ap[0]), [1, NT], [0, 32]])
                    in1 = bass.AP(invf[:].tensor, invf[:].offset,
                                  [list(invf[:].ap[0]), [0, NT], [1, 32]])
                    v.tensor_tensor(ang[:].rearrange("p (a b) -> p a b", b=32),
                                    in0, in1, ALU.mult)
                    v.tensor_scalar(rt1[:], ang[:], 1.0 / TWO_PI, None, ALU.mult)
                    v.tensor_copy(rk32[:], rt1[:])
                    v.tensor_copy(rkf[:], rk32[:])
                    v.scalar_tensor_tensor(rt1[:], rkf[:], -TWO_PI, ang[:], ALU.mult, ALU.add)
                    v.tensor_scalar(rkf[:], rt1[:], math.pi, None, ALU.is_gt)
                    v.scalar_tensor_tensor(rt1[:], rkf[:], -TWO_PI, rt1[:], ALU.mult, ALU.add)
                    v.tensor_scalar(ang2[:], ang[:], math.pi / 2, None, ALU.add)
                    v.tensor_scalar(ang[:], ang2[:], 1.0 / TWO_PI, None, ALU.mult)
                    v.tensor_copy(rk32[:], ang[:])
                    v.tensor_copy(rkf[:], rk32[:])
                    v.scalar_tensor_tensor(ang2[:], rkf[:], -TWO_PI, ang2[:],
                                           ALU.mult, ALU.add)
                    v.tensor_scalar(rkf[:], ang2[:], math.pi, None, ALU.is_gt)
                    return v.scalar_tensor_tensor(ang2[:], rkf[:], -TWO_PI, ang2[:],
                                                  ALU.mult, ALU.add)
                e_rr = p.emit("vector", mk_ang, inc="dve")

                p.wait("scalar", e_rr)
                p.emit("scalar", lambda o: o.activation(sin_t[:], rt1[:], AF.Sin))
                e_sc = p.emit("scalar", lambda o: o.activation(cos_t[:], ang2[:], AF.Sin),
                              inc="act")

                p.wait("tensor", e_qkn)
                p.wait("tensor", e_lnw)
                p.wait("tensor", e_g0)

                def pe0(t_):
                    t_.matmul(p0_ps[:, 0:64], ones_row_f[0:1, :], qn_sb[0:1, :],
                              start=True, stop=True)
                    t_.matmul(p0_ps[:, 64:128], ones_row_f[0:1, :], kn_sb[0:1, :],
                              start=True, stop=True)
                    return t_.transpose(p0_ps[:, 512:544], lnw_r[:, :], ident_f[0:32, 0:32])
                e_pe0 = p.emit("tensor", pe0, inc="pe")
                p.wait("vector", e_pe0)

                def ev0(v):
                    v.tensor_copy(wbc[:], p0_ps[:, 0:128])
                    return v.tensor_copy(lnwT[:], p0_ps[:, 512:544])
                e_ev0 = p.emit("vector", ev0, inc="dve")

                p.wait("vector", e_sc)

                def tables(v):
                    last = None
                    for dst, wofs, src, scale in (
                        (ropeq[:, 0:RS], 0, cos_t, 0.125),
                        (ropeq[:, RS:2 * RS], 32, sin_t, 0.125),
                        (ropeq[:, 2 * RS:3 * RS], 0, sin_t, 0.125),
                        (ropeq[:, 3 * RS:4 * RS], 32, cos_t, 0.125),
                        (ropek[:, 0:RS], 64, cos_t, 1.0),
                        (ropek[:, RS:2 * RS], 96, sin_t, 1.0),
                        (ropek[:, 2 * RS:3 * RS], 64, sin_t, 1.0),
                        (ropek[:, 3 * RS:4 * RS], 96, cos_t, 1.0),
                    ):
                        last = v.scalar_tensor_tensor(
                            dst.rearrange("p (a b) -> p a b", b=32),
                            src[:].rearrange("p (a b) -> p a b", b=32),
                            scale, head_bcast(wbc[:], wofs, NT, 32), ALU.mult, ALU.mult)
                    return last
                p.emit("vector", tables)
                p.flush_block("p0a")

                with nc.psum_tensor("p0b_ps", [128, 32], BF16) as p0b_ps:
                    p.wait("tensor", e_lnb)
                    e_tb = p.emit("tensor", lambda o: o.transpose(
                        p0b_ps[:, :], lnb_r[:, :], ident_b[0:32, 0:32]), inc="pe")
                    p.wait("vector", e_tb)
                    e_lnbT = p.emit("vector", lambda o: o.tensor_copy(lnbT[:], p0b_ps[:, :]),
                                    inc="dve")
                    dbg_dump("lnwT", lnwT[:], [128, 32], F32, [e_ev0])
                    dbg_dump("lnbT", lnbT[:], [128, 32], BF16, [e_lnbT])
                    p.flush_block("p0b")

        def emit_ffn(tag, src_ap, dst_ap, wqfc, wqpj, src_wait_evs=None,
                     xin_pre=None):
            NSUP = NT // 2
            NCH = HF // 128
            with contextlib.ExitStack() as psf:
                if xin_pre is None:
                    xin0 = psf.enter_context(nc.sbuf_tensor(f"{tag}_xin0", [128, 2048], F32))
                    r_x0 = Res()
                else:
                    xin0, r_x0 = xin_pre
                xin1 = psf.enter_context(nc.sbuf_tensor(f"{tag}_xin1", [128, 2048], F32))
                r_xin = [r_x0, Res()]
                lnx = psf.enter_context(nc.sbuf_tensor(f"{tag}_lnx", [128, 2048], BF16))
                lxT0 = psf.enter_context(nc.sbuf_tensor(f"{tag}_lxT0", [128, 2048], BF16))
                lxT1 = psf.enter_context(nc.sbuf_tensor(f"{tag}_lxT1", [128, 2048], BF16))
                cmb = psf.enter_context(nc.sbuf_tensor(f"{tag}_cmb", [128, 2048], BF16))
                tmpv = psf.enter_context(nc.sbuf_tensor(f"{tag}_tmp", [128, 2048], F32))
                rowb = psf.enter_context(nc.sbuf_tensor(f"{tag}_row", [1, 512], F32))
                stg_ps = psf.enter_context(nc.psum_tensor(f"{tag}_stg", [128, 1024], BF16))
                hps0 = psf.enter_context(nc.psum_tensor(f"{tag}_h0", [128, 512], F32))
                hps1 = psf.enter_context(nc.psum_tensor(f"{tag}_h1", [128, 512], F32))
                o2a_ps = psf.enter_context(nc.psum_tensor(f"{tag}_o2a", [128, 1024], F32))
                o2b_ps = psf.enter_context(nc.psum_tensor(f"{tag}_o2b", [128, 1024], F32))
                sm_ps = psf.enter_context(nc.psum_tensor(f"{tag}_sm", [128, 512], F32))
                xin = [xin0, xin1]
                lxT = [lxT0, lxT1]
                hps = [hps0, hps1]
                o2 = [o2a_ps, o2b_ps]
                sm_f = sm_ps[:, :]

                r_lnx = Res()
                r_lxT = [Res(), Res()]
                r_stg = Res()
                r_hps = [Res(), Res()]
                r_cmb = [Res() for _ in range(4)]
                r_o2 = [Res(), Res()]
                r_sm = Res()
                lxT_evs = [[], []]
                gel_ev = {}
                sq_ev = {}
                mm2_ev = {}
                summ_ev = {}

                def newton_rstd(dst, src, tmp):
                    z = st[:, src:src + 2]
                    y = st[:, dst:dst + 2]
                    t = st[:, tmp:tmp + 2]
                    zi = z.bitcast(I32)
                    yi = y.bitcast(I32)
                    p.gemit("vector", lambda v, z=z: v.tensor_scalar(
                        z, z, 1e-5, None, ALU.add))
                    p.gemit("vector", lambda v, yi=yi, zi=zi: v.tensor_scalar(
                        yi, zi, 1, None, ALU.logical_shift_right))
                    p.gemit("vector", lambda v, yi=yi: v.tensor_scalar(
                        yi, yi, -1, None, ALU.bitwise_xor))
                    p.gemit("vector", lambda v, yi=yi: v.tensor_scalar(
                        yi, yi, 0x5F3759E0, None, ALU.add))
                    last = None
                    for _ in range(2):
                        p.gemit("vector", lambda v, t=t, y=y: v.tensor_tensor(
                            t, y, y, ALU.mult))
                        p.gemit("vector", lambda v, t=t, z=z: v.tensor_tensor(
                            t, t, z, ALU.mult))
                        p.gemit("vector", lambda v, t=t: v.tensor_scalar(
                            t, t, -0.5, 1.5, ALU.mult, ALU.add))
                        last = p.gemit("vector", lambda v, y=y, t=t:
                                       v.tensor_tensor(y, y, t, ALU.mult))
                    return last

                def emit_load(u):
                    par = u % 2
                    r_xin[par].begin_write(p, "sync")
                    if src_wait_evs is not None:
                        p.wait_all("sync",
                                   [src_wait_evs[2 * u], src_wait_evs[2 * u + 1]])
                    ev = p.emit("sync", lambda o, u=u, par=par: o.dma_start(
                        out=xin[par][:].rearrange("p (a n) -> p a n", a=2),
                        in_=pair_view(src_ap, 2 * u, D)), inc="dS", n=16)
                    r_xin[par].set_ready(ev)

                def emit_ln_stats(u):
                    par = u % 2
                    co = 32 * par
                    p.wait("vector", r_xin[par].ready)
                    r_lnx.begin_write(p, "vector")
                    x3 = xin[par][:].rearrange("p (a n) -> p a n", a=2)
                    p.gemit("vector", lambda v, x3=x3, co=co: v.reduce_sum(
                        st[:, co:co + 2], x3, AX.X))
                    for j in range(2):
                        p.gemit("vector", lambda v, par=par, j=j, co=co:
                                v.scalar_tensor_tensor(
                                    lnx[:, j * D:(j + 1) * D],
                                    xin[par][:, j * D:(j + 1) * D], 1.0,
                                    xin[par][:, j * D:(j + 1) * D], ALU.mult,
                                    ALU.mult,
                                    accum_out=st[:, co + 2 + j:co + 3 + j]))
                    p.gemit("vector", lambda v, co=co: v.tensor_scalar(
                        st[:, co + 4:co + 6], st[:, co:co + 2], 1.0 / D, None,
                        ALU.mult))
                    p.gemit("vector", lambda v, co=co: v.tensor_tensor(
                        st[:, co + 20:co + 22], st[:, co + 4:co + 6],
                        st[:, co + 4:co + 6], ALU.mult))
                    p.gemit("vector", lambda v, co=co: v.scalar_tensor_tensor(
                        st[:, co + 6:co + 8], st[:, co + 2:co + 4], 1.0 / D,
                        st[:, co + 20:co + 22], ALU.mult, ALU.subtract))

                def emit_rstd_apply(u):
                    par = u % 2
                    co = 32 * par
                    newton_rstd(co + 8, co + 6, co + 20)
                    ev = None
                    for j in range(2):
                        ev = p.emit("vector", lambda o, par=par, j=j, co=co:
                                    o.tensor_scalar(
                                        lnx[:, j * D:(j + 1) * D],
                                        xin[par][:, j * D:(j + 1) * D],
                                        st[:, co + 4 + j:co + 5 + j],
                                        st[:, co + 8 + j:co + 9 + j],
                                        ALU.subtract, ALU.mult), inc="dve")
                    r_lnx.set_ready(ev)

                def emit_tp(u, half):
                    par = u % 2
                    r_stg.begin_write(p, "tensor")
                    p.wait("tensor", r_lnx.ready)

                    def tp_ln(t_, half=half):
                        last = None
                        for k in range(8):
                            c = half * 4 + k // 2
                            j = k % 2
                            last = t_.transpose(
                                stg_ps[:, k * 128:(k + 1) * 128],
                                lnx[:, j * D + c * 128:j * D + (c + 1) * 128],
                                ident_b[:])
                        return last
                    e_tp = p.emit("tensor", tp_ln, inc="pe")
                    r_stg.set_ready(e_tp)
                    r_lnx.add_consumer(e_tp)
                    if half == 0:
                        r_lxT[par].begin_write(p, "scalar")
                        p.wait("scalar", e_tp)
                        e_te = p.emit("scalar", lambda o, par=par: o.copy(
                            lxT[par][:, 0:1024], stg_ps[:, :]), inc="act")
                        lxT_evs[par] = [e_te]
                    else:
                        r_lxT[par].begin_write(p, "vector")
                        p.wait("vector", e_tp)
                        e_te = p.emit("vector", lambda o, par=par: o.tensor_copy(
                            lxT[par][:, 1024:2048], stg_ps[:, :]), inc="dve")
                        lxT_evs[par] = lxT_evs[par] + [e_te]
                    r_stg.add_consumer(e_te)

                def emit_chunk(u, c):
                    par = u % 2
                    gb = c % 2
                    s = c % 4
                    r_hps[gb].begin_write(p, "tensor")
                    if c == 0:
                        p.wait_all("tensor", lxT_evs[par])
                        if "wqfc_q" in evs:
                            p.wait("tensor", evs["wqfc_q"])
                        if "wqpj_fold" in evs:
                            p.wait("tensor", evs["wqpj_fold"])

                    def mm1(t_, par=par, c=c, gb=gb):
                        last = None
                        for k in range(8):
                            last = t_.matmul(
                                hps[gb][:, 0:256],
                                wqfc[:, k * HF + c * 128:k * HF + (c + 1) * 128],
                                lxT[par][:, k * 256:(k + 1) * 256],
                                start=(k == 0), stop=(k == 7))
                        return last
                    e_mm1 = p.emit("tensor", mm1, inc="pe")
                    r_hps[gb].set_ready(e_mm1)
                    r_lxT[par].add_consumer(e_mm1)

                    r_cmb[s].begin_write(p, "scalar")
                    p.wait("scalar", e_mm1)
                    e_gel = p.emit("scalar", lambda o, s=s, gb=gb: o.activation(
                        cmb[:, s * 512:s * 512 + 256], hps[gb][:, 0:256],
                        AF.Gelu), inc="act")
                    r_hps[gb].add_consumer(e_gel)
                    gel_ev[c] = e_gel

                    p.wait("gpsimd", e_gel)
                    e_sq = p.emit("gpsimd", lambda o, s=s: o.tensor_tensor(
                        cmb[:, s * 512 + 256:s * 512 + 512],
                        cmb[:, s * 512:s * 512 + 256],
                        cmb[:, s * 512:s * 512 + 256], ALU.mult), inc="pool")
                    sq_ev[c] = e_sq

                def emit_mm2(u, c):
                    s = c % 4
                    if c == 0:
                        r_o2[0].begin_write(p, "tensor")
                        r_o2[1].begin_write(p, "tensor")
                        r_sm.begin_write(p, "tensor")
                    p.wait("tensor", gel_ev[c])

                    def mm2(t_, c=c, s=s):
                        last = None
                        for j in range(2):
                            for g in range(2):
                                last = t_.matmul(
                                    o2[j][:, g * 512:(g + 1) * 512],
                                    cmb[:, s * 512 + j * 128:s * 512 + (j + 1) * 128],
                                    wqpj[:, c * D + g * 512:c * D + (g + 1) * 512],
                                    start=(c == 0), stop=(c == NCH - 1))
                        return last
                    e_mm2 = p.emit("tensor", mm2, inc="pe")
                    mm2_ev[c] = e_mm2
                    p.wait("tensor", sq_ev[c])
                    e_summ = p.emit("tensor", lambda t_, s=s, c=c: t_.matmul(
                        sm_ps[0:1, 0:512], ones_col_b[:, 0:1],
                        cmb[:, s * 512:(s + 1) * 512],
                        start=(c == 0), stop=(c == NCH - 1)), inc="pe")
                    summ_ev[c] = e_summ
                    r_cmb[s].add_consumer(e_mm2)
                    r_cmb[s].add_consumer(e_summ)

                def emit_tail(u):
                    par = u % 2
                    co = 32 * par
                    e_s31 = summ_ev[NCH - 1]
                    e_m31 = mm2_ev[NCH - 1]
                    r_o2[0].set_ready(e_m31)
                    r_o2[1].set_ready(e_m31)
                    r_sm.set_ready(e_s31)
                    p.wait("vector", e_s31)
                    e_rows = p.gemit("vector", lambda v: v.tensor_copy(
                        rowb[0:1, :], sm_ps[0:1, 0:512]))
                    r_sm.add_consumer(e_rows)
                    p.wait("tensor", e_rows)

                    def row_tp(t_):
                        ins = None
                        for q in range(4):
                            ins = t_.matmul(sm_f[:, q:q + 1],
                                            rowb[0:1, q * 128:(q + 1) * 128],
                                            ones_row_f[0:1, 0:1],
                                            start=True, stop=True)
                        return ins
                    e_rtp = p.emit("tensor", row_tp, inc="pe")
                    p.wait("scalar", e_m31)
                    ev0a = p.emit("scalar", lambda o: o.activation(
                        tmpv[:, 0:1024], o2[0][:, :], AF.Copy, scale=0.5),
                        inc="act")
                    p.wait("vector", e_m31)
                    ev0b = p.emit("vector", lambda o: o.tensor_scalar(
                        tmpv[:, 1024:2048], o2[1][:, :], 0.5, None, ALU.mult),
                        inc="dve")
                    r_o2[0].add_consumer(ev0a)
                    r_o2[1].add_consumer(ev0b)
                    p.wait("vector", e_rtp)
                    p.gemit("vector", lambda v, co=co: v.tensor_scalar(
                        st[:, co + 12:co + 14], sm_f[:, 0:2], 1.0 / HF, None,
                        ALU.mult))
                    p.gemit("vector", lambda v, co=co: v.tensor_tensor(
                        st[:, co + 20:co + 22], st[:, co + 12:co + 14],
                        st[:, co + 12:co + 14], ALU.mult))
                    e_var = p.gemit("vector", lambda v, co=co:
                                    v.scalar_tensor_tensor(
                                        st[:, co + 14:co + 16], sm_f[:, 2:4],
                                        1.0 / HF, st[:, co + 20:co + 22],
                                        ALU.mult, ALU.subtract))
                    r_sm.add_consumer(e_var)
                    newton_rstd(co + 16, co + 14, co + 20)
                    p.gemit("vector", lambda v, co=co: v.tensor_tensor(
                        st[:, co + 18:co + 20], st[:, co + 12:co + 14],
                        st[:, co + 16:co + 18], ALU.mult))
                    p.gemit("vector", lambda v, co=co: v.tensor_scalar(
                        st[:, co + 18:co + 20], st[:, co + 18:co + 20], -1.0,
                        None, ALU.mult))
                    p.wait("vector", ev0a)
                    for j in range(2):
                        p.gemit("vector", lambda v, j=j, co=co:
                                v.scalar_tensor_tensor(
                                    tmpv[:, j * 1024:(j + 1) * 1024],
                                    tmpv[:, j * 1024:(j + 1) * 1024],
                                    st[:, co + 16 + j:co + 17 + j],
                                    halfb[:], ALU.mult, ALU.add))
                        p.gemit("vector", lambda v, j=j, co=co:
                                v.scalar_tensor_tensor(
                                    tmpv[:, j * 1024:(j + 1) * 1024],
                                    halfc[:],
                                    st[:, co + 18 + j:co + 19 + j],
                                    tmpv[:, j * 1024:(j + 1) * 1024],
                                    ALU.mult, ALU.add))
                        e_ev3 = p.gemit("vector", lambda v, par=par, j=j:
                                        v.tensor_tensor(
                                            xin[par][:, j * D:(j + 1) * D],
                                            xin[par][:, j * D:(j + 1) * D],
                                            tmpv[:, j * 1024:(j + 1) * 1024],
                                            ALU.add))
                        r_xin[par].add_consumer(e_ev3)
                        p.wait("sync", e_ev3)
                        e_sto = p.emit("sync", lambda o, u=u, j=j, par=par:
                                       o.dma_start(
                                           out=dst_ap[(2 * u + j) * 128:
                                                      (2 * u + j + 1) * 128, :],
                                           in_=xin[par][:, j * D:(j + 1) * D]),
                                       inc="dO", n=16)
                        r_xin[par].add_consumer(e_sto)
                        evs[f"{tag}_store_{2 * u + j}"] = e_sto

                if xin_pre is None:
                    emit_load(0)
                if NSUP > 1:
                    emit_load(1)
                emit_ln_stats(0)
                emit_rstd_apply(0)
                emit_tp(0, 0)
                emit_tp(0, 1)

                for u in range(NSUP):
                    nxt = u + 1
                    for c in range(NCH):
                        emit_chunk(u, c)
                        if c == 2 and nxt < NSUP:
                            emit_ln_stats(nxt)
                        if c == 8 and nxt < NSUP:
                            emit_rstd_apply(nxt)
                        if c >= 1:
                            emit_mm2(u, c - 1)
                        if c == 20 and nxt < NSUP:
                            emit_tp(nxt, 0)
                        if c == 26 and nxt < NSUP:
                            emit_tp(nxt, 1)
                    emit_mm2(u, NCH - 1)
                    emit_tail(u)
                    if nxt + 1 < NSUP:
                        emit_load(nxt + 1)
                if DEBUG:
                    dbg_dump("cmb", cmb[:], [128, 2048], BF16,
                             [gel_ev[NCH - 1], sq_ev[NCH - 1]])
                p.flush_block(tag)

        if "P1" in phases or "P3" in phases:
            with (
                nc.sbuf_tensor("wqfc", [128, 8 * HF], BF16) as wqfc,
                nc.sbuf_tensor("wqpj", [128, 32 * D], BF16) as wqpj,
                nc.sbuf_tensor("f1_xin0", [128, 2048], F32) as f1_xin0,
            ):
                r_xin_f1 = Res()
                if "P1" in phases:
                    with contextlib.ExitStack() as ps1:
                        wld0 = ps1.enter_context(nc.sbuf_tensor("wld0", [128, 2048], F32, side="right"))
                        wld1 = ps1.enter_context(nc.sbuf_tensor("wld1", [128, 2048], F32, side="right"))
                        wld2 = ps1.enter_context(nc.sbuf_tensor("wld2", [128, 2048], F32, side="right"))
                        wsq = ps1.enter_context(nc.sbuf_tensor("wsq", [128, 2048], F32, side="right"))
                        wsg = ps1.enter_context(nc.sbuf_tensor("wsg", [128, 2048], BF16, side="right"))
                        bvec_c = ps1.enter_context(nc.sbuf_tensor("bvec_c", [1, 1024], BF16, side="right"))
                        bvec_b = ps1.enter_context(nc.sbuf_tensor("bvec_b", [1, 1024], BF16, side="right"))
                        p1_ps = ps1.enter_context(nc.psum_tensor("p1_ps", [128, 2048], F32))
                        _ = None
                        wld = [wld0, wld1, wld2]
                        e_x = p.emit("scalar", lambda o: o.dma_start(
                            out=f1_xin0[:].rearrange("p (a n) -> p a n", a=2),
                            in_=pair_view(x_in, 0, D)), inc="dO", n=16)
                        r_xin_f1.set_ready(e_x)
                        halves = []
                        for c in range(8):
                            for hh in range(2):
                                halves.append(
                                    (wfcT[c * 128:(c + 1) * 128, hh * 2048:(hh + 1) * 2048],
                                     wqfc, c * HF + hh * 2048, 0))
                        for c in range(0, 32, 2):
                            halves.append((pair_view(wpjT, c, D), wqpj, c * D, 1))

                        p.emit("vector", lambda o: o.memset(gst[:, 0:8], 0.0))
                        buf_free = [Res(), Res(), Res()]
                        e_a = None
                        for idx, (src, _, _, mat) in enumerate(halves):
                            b = idx % 3
                            q = "sync" if idx % 2 == 0 else "gpsimd"
                            qs = "dS" if idx % 2 == 0 else "dG"
                            buf_free[b].begin_write(p, q)
                            e_l = p.emit(q, lambda o, s=src, b=b: o.dma_start(
                                out=wld[b][:], in_=s), inc=qs, n=16)
                            buf_free[b].set_ready(e_l)
                            col = mat * 4 + b
                            p.wait("vector", e_l)
                            p.gemit("vector", lambda v, b=b: v.reduce_sum(
                                st[:, b:b + 1], wld[b][:], AX.X,
                                apply_absolute_value=True))
                            e_a = p.gemit("vector", lambda v, col=col, b=b: v.tensor_tensor(
                                gst[:, col:col + 1], gst[:, col:col + 1],
                                st[:, b:b + 1], ALU.add))
                            buf_free[b].add_consumer(e_a)

                        p.gemit("vector", lambda v: v.reduce_sum(st[:, 0:1], gst[:, 0:4],
                                                                 AX.X))
                        e_a = p.gemit("vector", lambda v: v.reduce_sum(st[:, 1:2],
                                                                       gst[:, 4:8], AX.X))
                        p.wait("tensor", e_a)

                        def psum_gamma(t_):
                            t_.matmul(p1_ps[0:1, 0:1], st[:, 0:1], ones_col_f[:],
                                      start=True, stop=True)
                            return t_.matmul(p1_ps[0:1, 1:2], st[:, 1:2], ones_col_f[:],
                                             start=True, stop=True)
                        e_pg = p.emit("tensor", psum_gamma, inc="pe")
                        p.wait("vector", e_pg)

                        p.gemit("vector", lambda v: v.tensor_scalar(
                            st[0:1, 2:4], p1_ps[0:1, 0:2], 1.0 / (HF * D), None, ALU.mult))
                        p.gemit("vector", lambda v: v.tensor_scalar(
                            st[0:1, 2:4], st[0:1, 2:4], 1e-6, 0.5, ALU.add, ALU.mult))
                        e_thr = p.gemit("vector", lambda v: v.tensor_tensor(
                            st[0:1, 2:4], st[0:1, 2:4], st[0:1, 2:4], ALU.mult))
                        p.wait("tensor", e_thr)
                        e_bthr = p.emit("tensor", lambda o: o.matmul(
                            p1_ps[:, 512:514], ones_row_f[0:1, :], st[0:1, 2:4],
                            start=True, stop=True), inc="pe")
                        p.wait("vector", e_bthr)
                        e_thr2 = p.emit("vector", lambda o: o.tensor_copy(
                            thr2[:], p1_ps[:, 512:514]), inc="dve")
                        dbg_dump("thr2", thr2[:], [128, 2], F32, [e_thr2])
                        dbg_dump("gst", gst[:, 0:2], [128, 2], F32, [e_thr2])

                        wsq2 = [wsq, wsq]
                        wsg2 = [wsg, wsg]
                        _sq = Res()
                        _sg = Res()
                        sq_r = [_sq, _sq]
                        sg_r = [_sg, _sg]
                        for idx, (src, dst, dcol, mat) in enumerate(halves):
                            b = idx % 3
                            b2 = idx % 2
                            q = "sync" if idx % 2 == 0 else "gpsimd"
                            qs = "dS" if idx % 2 == 0 else "dG"
                            buf_free[b].begin_write(p, q)
                            e_l = p.emit(q, lambda o, s=src, b=b: o.dma_start(
                                out=wld[b][:], in_=s), inc=qs, n=16)
                            buf_free[b].set_ready(e_l)

                            sq_r[b2].begin_write(p, "gpsimd")
                            p.wait("gpsimd", e_l)
                            e_sq = p.emit("gpsimd", lambda o, b=b, b2=b2: o.tensor_tensor(
                                wsq2[b2][:], wld[b][:], wld[b][:], ALU.mult), inc="pool")
                            sq_r[b2].set_ready(e_sq)

                            sg_r[b2].begin_write(p, "scalar")
                            p.wait("scalar", e_l)
                            e_sg = p.emit("scalar", lambda o, b=b, b2=b2: o.activation(
                                wsg2[b2][:], wld[b][:], AF.Sign), inc="act")
                            sg_r[b2].set_ready(e_sg)
                            buf_free[b].add_consumer(e_sq)
                            buf_free[b].add_consumer(e_sg)

                            p.wait("vector", e_sq)
                            p.wait("vector", e_sg)
                            e_q = p.emit("vector", lambda o, dst=dst, dcol=dcol, mat=mat,
                                         b2=b2: o.scalar_tensor_tensor(
                                             dst[:, dcol:dcol + 2048], wsq2[b2][:],
                                             thr2[:, mat:mat + 1], wsg2[b2][:],
                                             ALU.is_gt, ALU.mult), inc="dve")
                            sq_r[b2].add_consumer(e_q)
                            sg_r[b2].add_consumer(e_q)
                            if idx == 15:
                                evs["wqfc_q"] = e_q
                        evs["wqpj_q"] = e_q
                        dbg_dump("wqfc0", wqfc[:, 0:1024], [128, 1024], BF16, [e_q])
                        dbg_dump("wqpj0", wqpj[:, 0:1024], [128, 1024], BF16, [e_q])

                        p.wait("tensor", evs["wqpj_q"])
                        p.wait("tensor", e_thr2)
                        e_b_list = []
                        for g in range(2):
                            def bmm(t_, g=g):
                                last = None
                                for c in range(32):
                                    last = t_.matmul(
                                        p1_ps[0:1, g * 512:(g + 1) * 512],
                                        lnbT[:, c:c + 1],
                                        wqpj[:, c * D + g * 512: c * D + (g + 1) * 512],
                                        start=(c == 0), stop=(c == 31))
                                return last
                            e_b = p.emit("tensor", bmm, inc="pe")
                            e_b_list.append(e_b)
                            p.wait("scalar", e_b)
                            e_bc = p.emit("scalar", lambda o, g=g: o.copy(
                                bvec_b[0:1, g * 512:(g + 1) * 512],
                                p1_ps[0:1, g * 512:(g + 1) * 512]), inc="act")

                        p.wait_all("vector", e_b_list)
                        def fold(v):
                            last = None
                            for c in range(32):
                                last = v.tensor_scalar(wqpj[:, c * D:(c + 1) * D],
                                                       wqpj[:, c * D:(c + 1) * D],
                                                       lnwT[:, c:c + 1], None, ALU.mult)
                            return last
                        e_fold = p.emit("vector", fold, inc="dve")
                        evs["wqpj_fold"] = e_fold

                        p.wait("tensor", e_fold)
                        p.wait("tensor", e_bc)

                        def cv_mm(t_):
                            last = None
                            for g in range(2):
                                for c in range(32):
                                    last = t_.matmul(
                                        p1_ps[0:1, g * 512:(g + 1) * 512],
                                        ones_col_b[:, 0:1],
                                        wqpj[:, c * D + g * 512: c * D + (g + 1) * 512],
                                        start=(c == 0), stop=(c == 31))
                            return last
                        e_cv = p.emit("tensor", cv_mm, inc="pe")
                        p.wait("scalar", e_cv)
                        e_cvc = p.emit("scalar", lambda o: o.copy(
                            bvec_c[0:1, :], p1_ps[0:1, 0:1024]), inc="act")

                        p.wait("tensor", e_bc)
                        p.wait("tensor", e_cvc)
                        def hb_mm(t_):
                            t_.matmul(p1_ps[:, 0:512], ones_row_b[0:1, :],
                                      bvec_b[0:1, 0:512], start=True, stop=True)
                            t_.matmul(p1_ps[:, 512:1024], ones_row_b[0:1, :],
                                      bvec_b[0:1, 512:1024], start=True, stop=True)
                            t_.matmul(p1_ps[:, 1024:1536], ones_row_b[0:1, :],
                                      bvec_c[0:1, 0:512], start=True, stop=True)
                            return t_.matmul(p1_ps[:, 1536:2048], ones_row_b[0:1, :],
                                             bvec_c[0:1, 512:1024], start=True, stop=True)
                        e_hb = p.emit("tensor", hb_mm, inc="pe")
                        p.wait("vector", e_hb)
                        e_halfb = p.emit("vector", lambda o: (
                            o.tensor_scalar(halfb[:], p1_ps[:, 0:1024], 0.5, None,
                                            ALU.mult),
                            o.tensor_scalar(halfc[:], p1_ps[:, 1024:2048], 0.5, None,
                                            ALU.mult))[-1], inc="dve")
                        dbg_dump("halfb", halfb[:], [128, 1024], F32, [e_halfb])
                        dbg_dump("wqpjf0", wqpj[:, 0:1024], [128, 1024], BF16, [e_halfb])

                        p.wait("scalar", ("dve", p.cnt["dve"]))
                        for c in range(8):
                            p.emit("scalar", lambda o, c=c: o.dma_start(
                                out=wqfc_s[c * 128:(c + 1) * 128, :],
                                in_=wqfc[:, c * HF:(c + 1) * HF]), inc="dO", n=16)
                        for c in range(0, 32, 2):
                            dstv = pair_view(wqpj_s, c, D)
                            p.emit("scalar", lambda o, d=dstv, c=c: o.dma_start(
                                out=d, in_=wqpj[:, c * D:(c + 2) * D].rearrange(
                                    "p (a n) -> p a n", a=2)), inc="dO", n=16)
                        evs["wq_stored"] = ("dO", p.cnt["dO"])
                        p.flush_block("p1")

                if "P3" in phases:
                    emit_ffn("f1", x_in, y_out if terminal == "P3" else xs,
                             wqfc, wqpj,
                             xin_pre=(f1_xin0, r_xin_f1) if "P1" in phases else None)

        if "P5" in phases or "P6" in phases:
            with (
                nc.sbuf_tensor("qT_b", [128, 8 * T], BF16) as qT_b,
                nc.sbuf_tensor("kT_b", [128, 8 * T], BF16) as kT_b,
                nc.sbuf_tensor("v_b", [128, NT * D], BF16) as v_b,
            ):
                if "P5" in phases:
                    with contextlib.ExitStack() as ps5:
                        wqkv_b = ps5.enter_context(nc.sbuf_tensor("a_wqkv", [128, 8 * 3 * D], BF16))
                        axin0 = ps5.enter_context(nc.sbuf_tensor("a_xin0", [128, D], F32))
                        axin1 = ps5.enter_context(nc.sbuf_tensor("a_xin1", [128, D], F32))
                        alnx = ps5.enter_context(nc.sbuf_tensor("a_lnx", [128, D], BF16))
                        alxT = ps5.enter_context(nc.sbuf_tensor("a_lxT", [128, D], BF16))
                        qr_f = ps5.enter_context(nc.sbuf_tensor("a_qr", [128, D], F32))
                        kr_f = ps5.enter_context(nc.sbuf_tensor("a_kr", [128, D], F32))
                        qrope = ps5.enter_context(nc.sbuf_tensor("a_qrope", [128, D], BF16))
                        krope = ps5.enter_context(nc.sbuf_tensor("a_krope", [128, D], BF16))
                        tq1 = ps5.enter_context(nc.sbuf_tensor("a_tq1", [128, 512], F32))
                        tq2 = ps5.enter_context(nc.sbuf_tensor("a_tq2", [128, 512], F32))
                        tk1 = ps5.enter_context(nc.sbuf_tensor("a_tk1", [128, 512], F32))
                        tk2 = ps5.enter_context(nc.sbuf_tensor("a_tk2", [128, 512], F32))
                        qkv_ps = ps5.enter_context(nc.psum_tensor("a_qkv", [128, 3 * D], F32))
                        stA_ps = ps5.enter_context(nc.psum_tensor("a_stA", [128, 1024], BF16))
                        stB_ps = ps5.enter_context(nc.psum_tensor("a_stB", [128, 1024], BF16))
                        _ = None
                        xin = [axin0, axin1]
                        r_xin = [Res(), Res()]
                        r_lnx = Res()
                        r_lxT = Res()
                        r_stA = Res()
                        r_stB = Res()
                        r_qkv = Res()
                        r_qr = Res()
                        r_kr = Res()
                        r_qrope = Res()
                        r_krope = Res()

                        for c in range(8):
                            p.emit("gpsimd", lambda o, c=c: o.dma_start(
                                out=wqkv_b[:, c * 3 * D:(c + 1) * 3 * D],
                                in_=wqkvT[c * 128:(c + 1) * 128, :]), inc="dG", n=16)
                        e_wqkv = ("dG", p.cnt["dG"])

                        for t in range(NT):
                            par = t % 2
                            sc = 8 * par

                            r_xin[par].begin_write(p, "sync")
                            if "P3" in phases:
                                p.wait("sync", evs[f"f1_store_{t}"])
                            e_ld = p.emit("sync", lambda o, t=t, par=par: o.dma_start(
                                out=xin[par][:], in_=xs[t * 128:(t + 1) * 128, :]),
                                inc="dS", n=16)
                            r_xin[par].set_ready(e_ld)

                            p.wait("vector", e_ld)
                            r_lnx.begin_write(p, "vector")

                            p.gemit("vector", lambda v, par=par, sc=sc: (
                                v.reduce_sum(st[:, sc + 0:sc + 1], xin[par][:], AX.X),
                                v.scalar_tensor_tensor(alnx[:], xin[par][:], 1.0,
                                                       xin[par][:], ALU.mult, ALU.mult,
                                                       accum_out=st[:, sc + 1:sc + 2]))[-1])
                            p.gemit("vector", lambda v, sc=sc: v.tensor_scalar(
                                st[:, sc + 2:sc + 3], st[:, sc + 0:sc + 1],
                                1.0 / D, None, ALU.mult))
                            p.gemit("vector", lambda v, sc=sc: v.tensor_tensor(
                                st[:, sc + 3:sc + 4], st[:, sc + 2:sc + 3],
                                st[:, sc + 2:sc + 3], ALU.mult))
                            e_st = p.gemit("vector", lambda v, sc=sc: v.scalar_tensor_tensor(
                                st[:, sc + 4:sc + 5], st[:, sc + 1:sc + 2], 1.0 / D,
                                st[:, sc + 3:sc + 4], ALU.mult, ALU.subtract))
                            p.wait("scalar", e_st)

                            def rstd(s, sc=sc):
                                s.activation(st[:, sc + 5:sc + 6], st[:, sc + 4:sc + 5],
                                             AF.Ln, bias=cst[:, 0:1])
                                return s.activation(st[:, sc + 6:sc + 7],
                                                    st[:, sc + 5:sc + 6], AF.Exp, scale=-0.5)
                            e_rs = p.emit("scalar", rstd, inc="act")
                            p.wait("vector", e_rs)
                            e_ln = p.emit("vector", lambda o, par=par, sc=sc: o.tensor_scalar(
                                alnx[:], xin[par][:], st[:, sc + 2:sc + 3],
                                st[:, sc + 6:sc + 7], ALU.subtract, ALU.mult), inc="dve")
                            r_lnx.set_ready(e_ln)
                            r_xin[par].add_consumer(e_ln)

                            r_stA.begin_write(p, "tensor")
                            p.wait("tensor", e_ln)

                            def tp_ln(t_):
                                last = None
                                for c in range(8):
                                    last = t_.transpose(stA_ps[:, c * 128:(c + 1) * 128],
                                                        alnx[:, c * 128:(c + 1) * 128],
                                                        ident_b[:])
                                return last
                            e_tp = p.emit("tensor", tp_ln, inc="pe")
                            r_stA.set_ready(e_tp)
                            r_lnx.add_consumer(e_tp)

                            r_lxT.begin_write(p, "scalar")
                            p.wait("scalar", e_tp)
                            e_lxT = p.emit("scalar", lambda o: o.copy(
                                alxT[:], stA_ps[:, :]), inc="act")
                            r_lxT.set_ready(e_lxT)
                            r_stA.add_consumer(e_lxT)

                            p.wait("tensor", e_lxT)
                            p.wait("tensor", e_wqkv)
                            r_qkv.begin_write(p, "tensor")

                            def qkv_mm(t_):
                                last = None
                                for g in range(6):
                                    for c in range(8):
                                        last = t_.matmul(
                                            qkv_ps[:, g * 512:(g + 1) * 512],
                                            alxT[:, c * 128:(c + 1) * 128],
                                            wqkv_b[:, c * 3 * D + g * 512:
                                                   c * 3 * D + (g + 1) * 512],
                                            start=(c == 0), stop=(c == 7))
                                return last
                            e_qkv = p.emit("tensor", qkv_mm, inc="pe")
                            r_qkv.set_ready(e_qkv)
                            r_lxT.add_consumer(e_qkv)

                            p.wait("vector", e_qkv)
                            r_qr.begin_write(p, "vector")
                            r_kr.begin_write(p, "vector")
                            e_cpq = p.gemit("vector", lambda v: v.tensor_copy(
                                qr_f[:], qkv_ps[:, 0:1024]))
                            p.wait("scalar", e_qkv)
                            e_cpk = p.emit("scalar", lambda o: o.copy(
                                kr_f[:], qkv_ps[:, 1024:2048]), inc="act")
                            e_cp = e_cpk
                            p.wait("gpsimd", e_cpk)
                            e_ksq = p.gemit("gpsimd", lambda g: (
                                g.tensor_tensor(tk1[:], kr_f[:, 0:512], kr_f[:, 0:512],
                                                ALU.mult),
                                g.tensor_tensor(tk2[:], kr_f[:, 512:1024],
                                                kr_f[:, 512:1024], ALU.mult))[-1])
                            p.gemit("vector", lambda v: (
                                v.tensor_tensor(tq1[:], qr_f[:, 0:512], qr_f[:, 0:512],
                                                ALU.mult),
                                v.tensor_tensor(tq2[:], qr_f[:, 512:1024],
                                                qr_f[:, 512:1024], ALU.mult))[-1])
                            p.wait("vector", e_ksq)

                            def sq_red(v):
                                v.reduce_sum(st[:, 16:24],
                                             tq1[:].rearrange("p (a b) -> p a b", b=64),
                                             AX.X)
                                v.reduce_sum(st[:, 24:32],
                                             tq2[:].rearrange("p (a b) -> p a b", b=64),
                                             AX.X)
                                v.reduce_sum(st[:, 32:40],
                                             tk1[:].rearrange("p (a b) -> p a b", b=64),
                                             AX.X)
                                return v.reduce_sum(
                                    st[:, 40:48],
                                    tk2[:].rearrange("p (a b) -> p a b", b=64), AX.X)
                            e_sqk = p.gemit("vector", sq_red)
                            r_qkv.add_consumer(e_cpq)
                            r_qkv.add_consumer(e_cpk)

                            p.wait("scalar", e_sqk)

                            def rms_rstd(s):
                                s.activation(st[:, 48:64], st[:, 16:32], AF.Ln,
                                             bias=cst[:, 1:2], scale=1.0 / HD)
                                s.activation(st[:, 48:64], st[:, 48:64], AF.Exp, scale=-0.5)
                                s.activation(st[:, 16:32], st[:, 32:48], AF.Ln,
                                             bias=cst[:, 1:2], scale=1.0 / HD)
                                return s.activation(st[:, 16:32], st[:, 16:32], AF.Exp,
                                                    scale=-0.5)
                            e_rms = p.emit("scalar", rms_rstd, inc="act")

                            p.wait("vector", e_rms)

                            def evq(v, t=t):
                                v.tensor_tensor(
                                    qr_f[:].rearrange("p (a b) -> p a b", b=64),
                                    qr_f[:].rearrange("p (a b) -> p a b", b=64),
                                    bass.AP(st[:].tensor, st[:].offset + 48,
                                            [list(st[:].ap[0]), [1, 16], [0, 64]]),
                                    ALU.mult)
                                return v.tensor_tensor(
                                    kr_f[:].rearrange("p (a b) -> p a b", b=64),
                                    kr_f[:].rearrange("p (a b) -> p a b", b=64),
                                    bass.AP(st[:].tensor, st[:].offset + 16,
                                            [list(st[:].ap[0]), [1, 16], [0, 64]]),
                                    ALU.mult)
                            e_vv = p.emit("scalar", lambda o, t=t: o.copy(
                                v_b[:, t * D:(t + 1) * D], qkv_ps[:, 2048:3072]),
                                inc="act")
                            e_ev = p.emit("vector", evq, inc="dve")
                            r_qr.set_ready(e_ev)
                            r_kr.set_ready(e_ev)
                            r_qkv.add_consumer(e_vv)

                            r_qrope.begin_write(p, "vector")

                            def rope_q(v, t=t):
                                A = head_bcast(ropeq[:], 0 + t * 32, 16, 32)
                                B = head_bcast(ropeq[:], RS + t * 32, 16, 32)
                                C = head_bcast(ropeq[:], 2 * RS + t * 32, 16, 32)
                                Dd = head_bcast(ropeq[:], 3 * RS + t * 32, 16, 32)
                                q1 = head_view(qr_f[:], 0, 16, 32, 64)
                                q2 = head_view(qr_f[:], 32, 16, 32, 64)
                                o1 = head_view(qrope[:], 0, 16, 32, 64)
                                o2 = head_view(qrope[:], 32, 16, 32, 64)
                                t1v = tq1[:].rearrange("p (a b) -> p a b", b=32)
                                t2v = tq2[:].rearrange("p (a b) -> p a b", b=32)
                                v.tensor_tensor(t1v, q1, A, ALU.mult)
                                v.tensor_tensor(t2v, q2, B, ALU.mult)
                                v.tensor_tensor(o1, t1v, t2v, ALU.subtract)
                                v.tensor_tensor(t1v, q1, C, ALU.mult)
                                v.tensor_tensor(t2v, q2, Dd, ALU.mult)
                                return v.tensor_tensor(o2, t1v, t2v, ALU.add)
                            e_rq = p.emit("vector", rope_q, inc="dve")
                            r_qrope.set_ready(e_rq)
                            r_qr.add_consumer(e_rq)

                            r_krope.begin_write(p, "gpsimd")
                            p.wait("gpsimd", e_ev)

                            def rope_k(g, t=t):
                                A = head_bcast(ropek[:], 0 + t * 32, 16, 32)
                                B = head_bcast(ropek[:], RS + t * 32, 16, 32)
                                C = head_bcast(ropek[:], 2 * RS + t * 32, 16, 32)
                                Dd = head_bcast(ropek[:], 3 * RS + t * 32, 16, 32)
                                k1 = head_view(kr_f[:], 0, 16, 32, 64)
                                k2 = head_view(kr_f[:], 32, 16, 32, 64)
                                o1 = head_view(krope[:], 0, 16, 32, 64)
                                o2 = head_view(krope[:], 32, 16, 32, 64)
                                t1v = tk1[:].rearrange("p (a b) -> p a b", b=32)
                                t2v = tk2[:].rearrange("p (a b) -> p a b", b=32)
                                g.tensor_tensor(t1v, k1, A, ALU.mult)
                                g.tensor_tensor(t2v, k2, B, ALU.mult)
                                g.tensor_tensor(o1, t1v, t2v, ALU.subtract)
                                g.tensor_tensor(t1v, k1, C, ALU.mult)
                                g.tensor_tensor(t2v, k2, Dd, ALU.mult)
                                return g.tensor_tensor(o2, t1v, t2v, ALU.add)
                            e_rk = p.emit("gpsimd", rope_k, inc="pool")
                            r_krope.set_ready(e_rk)
                            r_kr.add_consumer(e_rk)

                            r_stA.begin_write(p, "tensor")
                            p.wait("tensor", e_rq)

                            def tp_q(t_):
                                last = None
                                for hh in range(16):
                                    po = 64 * (hh % 2)
                                    pc = hh // 2
                                    last = t_.transpose(
                                        stA_ps[po:po + 64, pc * 128:(pc + 1) * 128],
                                        qrope[:, hh * 64:(hh + 1) * 64], ident_b[:])
                                return last
                            e_tq = p.emit("tensor", tp_q, inc="pe")
                            r_stA.set_ready(e_tq)
                            r_qrope.add_consumer(e_tq)

                            r_stB.begin_write(p, "tensor")
                            p.wait("tensor", e_rk)

                            def tp_k(t_):
                                last = None
                                for hh in range(16):
                                    po = 64 * (hh % 2)
                                    pc = hh // 2
                                    last = t_.transpose(
                                        stB_ps[po:po + 64, pc * 128:(pc + 1) * 128],
                                        krope[:, hh * 64:(hh + 1) * 64], ident_b[:])
                                return last
                            e_tk = p.emit("tensor", tp_k, inc="pe")
                            r_stB.set_ready(e_tk)
                            r_krope.add_consumer(e_tk)

                            p.wait("vector", e_tq)

                            p.wait("scalar", e_tq)

                            def evq2a(o, t=t):
                                iA = stA_ps[:, :].rearrange("p (a b) -> p a b", b=128)
                                oA = bass.AP(qT_b[:].tensor, qT_b[:].offset + t * 128,
                                             [list(qT_b[:].ap[0]), [T, 8], [1, 128]])
                                return o.copy(oA, iA)
                            e_ev2 = p.emit("scalar", evq2a, inc="act")
                            p.wait("scalar", e_tk)

                            def evq2b(o, t=t):
                                iB = stB_ps[:, :].rearrange("p (a b) -> p a b", b=128)
                                oB = bass.AP(kT_b[:].tensor, kT_b[:].offset + t * 128,
                                             [list(kT_b[:].ap[0]), [T, 8], [1, 128]])
                                return o.copy(oB, iB)
                            e_ev2b = p.emit("scalar", evq2b, inc="act")
                            r_stA.add_consumer(e_ev2)
                            r_stB.add_consumer(e_ev2b)
                        p.flush_block("p5")

                if "P6" in phases:
                    with contextlib.ExitStack() as ps6:
                        wo_b = ps6.enter_context(nc.sbuf_tensor("c_wo", [128, 8 * D], BF16))
                        Pb0 = ps6.enter_context(nc.sbuf_tensor("c_P0", [128, T], BF16))
                        Pb1 = ps6.enter_context(nc.sbuf_tensor("c_P1", [128, T], BF16))
                        B_sb = ps6.enter_context(nc.sbuf_tensor("c_B", [128, 512], F32))
                        rowb6 = ps6.enter_context(nc.sbuf_tensor("c_row6", [1, 256], F32))
                        PT0 = ps6.enter_context(nc.sbuf_tensor("c_PT0", [128, T], BF16))
                        PT1 = ps6.enter_context(nc.sbuf_tensor("c_PT1", [128, T], BF16))
                        ot_sb = ps6.enter_context(nc.sbuf_tensor("c_ot", [128, 1024], BF16))
                        cxr0 = ps6.enter_context(nc.sbuf_tensor("c_xr0", [128, D], F32))
                        cxr1 = ps6.enter_context(nc.sbuf_tensor("c_xr1", [128, D], F32))
                        cxo0 = ps6.enter_context(nc.sbuf_tensor("c_xo0", [128, D], F32))
                        cxo1 = ps6.enter_context(nc.sbuf_tensor("c_xo1", [128, D], F32))
                        S_ps = ps6.enter_context(nc.psum_tensor("c_S", [128, min(T, 2048)], F32))
                        PT_ps = ps6.enter_context(nc.psum_tensor("c_PT", [128, 1024], BF16))
                        OT_ps = ps6.enter_context(nc.psum_tensor("c_OT", [128, 1024], F32))
                        WO_ps = ps6.enter_context(nc.psum_tensor("c_WO", [128, 512], F32))
                        _ = None
                        Pb2 = ps6.enter_context(nc.sbuf_tensor("c_P2", [128, T], BF16))
                        Pb3 = ps6.enter_context(nc.sbuf_tensor("c_P3", [128, T], BF16))
                        PT2 = ps6.enter_context(nc.sbuf_tensor("c_PT2", [128, T], BF16))
                        PT3 = ps6.enter_context(nc.sbuf_tensor("c_PT3", [128, T], BF16))
                        Pb = [Pb0, Pb1, Pb2, Pb3]
                        PT = [PT0, PT1, PT2, PT3]
                        xr = [cxr0, cxr1]
                        xo = [cxo0, cxo1]

                        for c in range(8):
                            p.emit("gpsimd", lambda o, c=c: o.dma_start(
                                out=wo_b[:, c * D:(c + 1) * D],
                                in_=woT[c * 128:(c + 1) * 128, :]), inc="dG", n=16)
                        e_wo = ("dG", p.cnt["dG"])

                        r_SH = [Res(), Res()]
                        r_Pb = [Res(), Res(), Res(), Res()]
                        r_B = [Res(), Res(), Res(), Res()]
                        r_row6 = Res()
                        last_psc = [None]
                        r_PTps = Res()
                        r_PTsb = [Res(), Res(), Res(), Res()]
                        r_OT = Res()
                        r_ot = Res()
                        r_WO = Res()
                        r_xr = [Res(), Res()]
                        r_xo = [Res(), Res()]
                        r_linv = [Res(), Res(), Res(), Res()]
                        pend_wo = [None]

                        for i in range(NT):
                            ip = i % 2
                            ncols = (i + 1) * 128
                            r_xr[ip].begin_write(p, "sync")
                            if "P5" not in phases and "P3" in phases:
                                p.wait("sync", evs[f"f1_store_{i}"])
                            e_lr = p.emit("sync", lambda o, i=i, ip=ip: o.dma_start(
                                out=xr[ip][:], in_=xs[i * 128:(i + 1) * 128, :]),
                                inc="dS", n=16)
                            r_xr[ip].set_ready(e_lr)

                            def stage1(h, i=i, ncols=ncols):
                                hp = h % 2
                                bi = h % 4
                                po = 64 * hp
                                pc = h // 2
                                pb = 32 + hp * 16
                                c_nm = pb + 5
                                c_lA, c_lB, c_li = pb + 6, pb + 7, 42 + bi
                                c_l = pb + 8
                                colsA = min(ncols, 1024)
                                colsB = ncols - colsA
                                diag_lo = i * 128
                                nch = (ncols + 511) // 512
                                sbase = 1024 * (h % 2) if ncols <= 1024 else 0

                                lqv = qT_b[po:po + 64,
                                           pc * T + i * 128: pc * T + (i + 1) * 128]
                                cm_evs = []
                                e_s_half = {0: None, 1: None}
                                seen_half = set()
                                for ci in range(nch):
                                    c0 = ci * 512
                                    w = min(512, ncols - c0)
                                    half = (sbase + c0) // 1024
                                    if half not in seen_half:
                                        seen_half.add(half)
                                        r_SH[half].begin_write(p, "tensor")
                                    has_diag = c0 <= diag_lo < c0 + w

                                    def smm(t_, c0=c0, w=w, po=po, pc=pc, sbase=sbase,
                                            has_diag=has_diag, dl=diag_lo):
                                        last = t_.matmul(
                                            S_ps[:, sbase + c0:sbase + c0 + w], lqv,
                                            kT_b[po:po + 64, pc * T + c0: pc * T + c0 + w],
                                            start=True, stop=not has_diag)
                                        if has_diag:
                                            last = t_.matmul(
                                                S_ps[:, sbase + dl:sbase + dl + 128],
                                                ident_b[:], mask_b[:],
                                                start=False, stop=True)
                                        return last
                                    e_sc = p.emit("tensor", smm, inc="pe")
                                    e_s_half[half] = e_sc
                                    p.wait("vector", e_sc)
                                    e_cm = p.emit("vector", lambda v, ci=ci, c0=c0, w=w,
                                                  sbase=sbase:
                                                  v.reduce_max(st[:, pb + ci:pb + ci + 1],
                                                               S_ps[:, sbase + c0:
                                                                    sbase + c0 + w],
                                                               AX.X,
                                                               negate=True), inc="dve")
                                    cm_evs.append(e_cm)
                                for half in seen_half:
                                    r_SH[half].set_ready(e_s_half[half])
                                p.wait("vector", cm_evs[-1])
                                e_nm = p.emit("vector", lambda v, nch=nch: v.tensor_reduce(
                                    st[:, c_nm:c_nm + 1], st[:, pb:pb + nch], AX.X,
                                    ALU.min), inc="dve")

                                r_Pb[bi].begin_write(p, "scalar")
                                p.wait("scalar", e_nm)
                                dstlA = c_lA if colsB else c_l
                                e_eA = p.emit("scalar", lambda o, bi=bi, colsA=colsA,
                                              c_nm=c_nm, d=dstlA, sbase=sbase:
                                              o.activation(
                                                  Pb[bi][:, 0:colsA],
                                                  S_ps[:, sbase:sbase + colsA],
                                                  AF.Exp, bias=st[:, c_nm:c_nm + 1],
                                                  accum_out=st[:, d:d + 1]), inc="act")
                                r_SH[sbase // 1024].add_consumer(e_eA)
                                e_eB = None
                                if colsB:
                                    e_eB = p.emit("scalar", lambda o, bi=bi, colsB=colsB,
                                                  c_nm=c_nm, d=c_lB: o.activation(
                                                      Pb[bi][:, 1024:1024 + colsB],
                                                      S_ps[:, 1024:1024 + colsB],
                                                      AF.Exp, bias=st[:, c_nm:c_nm + 1],
                                                      accum_out=st[:, d:d + 1]), inc="act")
                                    r_SH[1].add_consumer(e_eB)
                                r_Pb[bi].set_ready(e_eB or e_eA)

                                r_linv[bi].begin_write(p, "vector")
                                p.wait("vector", e_eB or e_eA)
                                if colsB:
                                    p.gemit("vector", lambda v, a=c_lA, bcol=c_lB, d=c_l:
                                            v.tensor_tensor(st[:, d:d + 1], st[:, a:a + 1],
                                                            st[:, bcol:bcol + 1], ALU.add))
                                e_li = p.gemit("vector", lambda v, scol=c_l, d=c_li:
                                               v.reciprocal(st[:, d:d + 1],
                                                            st[:, scol:scol + 1]))
                                r_linv[bi].set_ready(e_li)

                                r_B[bi].begin_write(p, "gpsimd")
                                p.wait("gpsimd", e_li)
                                ev1 = p.emit("gpsimd", lambda o, bi=bi, d=c_li:
                                             o.dma_start(
                                                 out=lrow_d[bi:bi + 1, :],
                                                 in_=st[:, d:d + 1]), inc="dG", n=16)
                                r_linv[bi].add_consumer(ev1)
                                p.wait("gpsimd", ev1)
                                bsrc = bass.AP(lrow_d.tensor,
                                               lrow_d.offset + bi * 128,
                                               [[0, 128], [1, 128]])
                                ev2 = p.emit("gpsimd", lambda o, bi=bi, bsrc=bsrc:
                                             o.dma_start(
                                                 out=B_sb[:, bi * 128:(bi + 1) * 128],
                                                 in_=bsrc), inc="dG", n=16)
                                r_B[bi].set_ready(ev2)


                            def stage2pair(hE, i=i, ncols=ncols):
                                nb = i + 1
                                banks = [PT_ps[:, :],
                                         WO_ps[:, :].bitcast(BF16)]
                                bank_res = [r_PTps, r_WO]
                                pv_done = {}
                                for hp in (0, 1):
                                    p.wait("tensor", r_Pb[(hE + hp) % 4].ready)
                                r_OT.begin_write(p, "tensor")
                                e_pv = None
                                for b0 in range(0, nb, 8):
                                    bw = min(8, nb - b0)
                                    ev_evac = {}
                                    for hp in (0, 1):
                                        bi = (hE + hp) % 4
                                        bank_res[hp].begin_write(p, "tensor")

                                        def tp_p(t_, bi=bi, b0=b0, bw=bw, hp=hp):
                                            last = None
                                            for c in range(bw):
                                                last = t_.transpose(
                                                    banks[hp][:, c * 128:(c + 1) * 128],
                                                    Pb[bi][:, (b0 + c) * 128:
                                                           (b0 + c + 1) * 128],
                                                    ident_b[:])
                                            return last
                                        e_t = p.emit("tensor", tp_p, inc="pe")
                                        bank_res[hp].set_ready(e_t)
                                        r_Pb[bi].add_consumer(e_t)
                                        eng = "scalar" if hp == 0 else "vector"
                                        if b0 == 0:
                                            r_PTsb[bi].begin_write(p, eng)
                                        p.wait(eng, e_t)
                                        if hp == 0:
                                            e_e = p.emit("scalar", lambda o, bi=bi, b0=b0,
                                                         bw=bw, hp=hp: o.copy(
                                                             PT[bi][:, b0 * 128:
                                                                    (b0 + bw) * 128],
                                                             banks[hp][:, 0:bw * 128]),
                                                         inc="act")
                                        else:
                                            e_e = p.emit("vector", lambda o, bi=bi, b0=b0,
                                                         bw=bw, hp=hp: o.tensor_copy(
                                                             PT[bi][:, b0 * 128:
                                                                    (b0 + bw) * 128],
                                                             banks[hp][:, 0:bw * 128]),
                                                         inc="dve")
                                        bank_res[hp].add_consumer(e_e)
                                        ev_evac[hp] = e_e
                                        if b0 == 0:
                                            pv_done[hp] = []
                                        pv_done[hp].append(e_e)

                                    p.wait_all("tensor", [ev_evac[0], ev_evac[1]])

                                    def pvb(t_, hE=hE, i=i, b0=b0, bw=bw):
                                        last = None
                                        for b in range(b0, b0 + bw):
                                            for hp in (0, 1):
                                                h = hE + hp
                                                bi = h % 4
                                                po = 64 * hp
                                                off = 512 * hp
                                                last = t_.matmul(
                                                    OT_ps[po:po + 64, off:off + 128],
                                                    v_b[:, b * D + h * HD:
                                                        b * D + (h + 1) * HD],
                                                    PT[bi][:, b * 128:(b + 1) * 128],
                                                    start=(b == 0), stop=(b == i),
                                                    tile_position=(0, po))
                                        return last
                                    e_pv = p.emit("tensor", pvb, inc="pe")
                                    for hp in (0, 1):
                                        r_PTsb[(hE + hp) % 4].add_consumer(e_pv)
                                r_OT.set_ready(e_pv)

                                pc = hE // 2
                                if pc == 0:
                                    r_ot.begin_write(p, "vector")
                                p.wait("vector", e_pv)
                                b0i = hE % 4
                                b1i = (hE + 1) % 4
                                p.wait("vector", r_B[b0i].ready)
                                p.wait("vector", r_B[b1i].ready)
                                e_oe = p.emit("vector", lambda o, pc=pc, b0i=b0i,
                                              b1i=b1i: (
                                    o.tensor_tensor(
                                        ot_sb[0:64, pc * 128:(pc + 1) * 128],
                                        OT_ps[0:64, 0:128],
                                        B_sb[0:64, b0i * 128:(b0i + 1) * 128],
                                        ALU.mult),
                                    o.tensor_tensor(
                                        ot_sb[64:128, pc * 128:(pc + 1) * 128],
                                        OT_ps[64:128, 512:640],
                                        B_sb[64:128, b1i * 128:(b1i + 1) * 128],
                                        ALU.mult))[-1],
                                    inc="dve")
                                r_B[b0i].add_consumer(e_oe)
                                r_B[b1i].add_consumer(e_oe)
                                r_OT.add_consumer(e_oe)
                                r_B[b0i].add_consumer(e_oe)
                                r_B[b1i].add_consumer(e_oe)
                                last_psc[0] = e_oe
                                if pc == 7:
                                    r_ot.set_ready(e_oe)

                            def wo_block(i2):
                                ip2 = i2 % 2
                                p.wait("tensor", r_ot.ready)
                                p.wait("tensor", e_wo)
                                r_xo[ip2].begin_write(p, "vector")
                                e_res = None
                                for g in range(2):
                                    r_WO.begin_write(p, "tensor")

                                    def womm(t_, g=g):
                                        last = None
                                        for c in range(8):
                                            last = t_.matmul(
                                                WO_ps[:, 0:512],
                                                ot_sb[:, c * 128:(c + 1) * 128],
                                                wo_b[:, c * D + g * 512:
                                                     c * D + (g + 1) * 512],
                                                start=(c == 0), stop=(c == 7))
                                        return last
                                    e_wmm = p.emit("tensor", womm, inc="pe")
                                    r_WO.set_ready(e_wmm)
                                    if g == 1:
                                        r_ot.add_consumer(e_wmm)
                                    p.wait("vector", e_wmm)
                                    p.wait("vector", r_xr[ip2].ready)
                                    e_res = p.emit("vector", lambda o, ip2=ip2, g=g:
                                                   o.tensor_tensor(
                                                       xo[ip2][:, g * 512:(g + 1) * 512],
                                                       WO_ps[:, 0:512],
                                                       xr[ip2][:, g * 512:(g + 1) * 512],
                                                       ALU.add), inc="dve")
                                    r_WO.add_consumer(e_res)
                                    r_xr[ip2].add_consumer(e_res)
                                r_xo[ip2].set_ready(e_res)

                                p.wait("scalar", e_res)
                                p6_dst = y_out if terminal == "P6" else xs
                                e_sto = p.emit("scalar", lambda o, i2=i2, ip2=ip2,
                                               d=p6_dst: o.dma_start(
                                                   out=d[i2 * 128:(i2 + 1) * 128, :],
                                                   in_=xo[ip2][:]), inc="dO", n=16)
                                r_xo[ip2].add_consumer(e_sto)
                                evs[f"a_store_{i2}"] = e_sto

                            for pr in range(NH // 2):
                                stage1(2 * pr)
                                stage1(2 * pr + 1)
                                if pr == 0 and pend_wo[0] is not None:
                                    wo_block(pend_wo[0])
                                    pend_wo[0] = None
                                if pr > 0:
                                    stage2pair(2 * pr - 2)
                            stage2pair(NH - 2)
                            pend_wo[0] = i
                        wo_block(pend_wo[0])
                        p.flush_block("p6")

        if "P7" in phases:
            with (
                nc.sbuf_tensor("wqfc2", [128, 8 * HF], BF16) as wqfc2,
                nc.sbuf_tensor("wqpj2", [128, 32 * D], BF16) as wqpj2,
            ):
                if "P1" in phases:
                    p.wait("sync", evs["wq_stored"])
                for c in range(8):
                    p.emit("sync", lambda o, c=c: o.dma_start(
                        out=wqfc2[:, c * HF:(c + 1) * HF],
                        in_=wqfc_s[c * 128:(c + 1) * 128, :]), inc="dS", n=16)
                for c in range(0, 32, 2):
                    srcv = pair_view(wqpj_s, c, D)
                    p.emit("sync", lambda o, s=srcv, c=c: o.dma_start(
                        out=wqpj2[:, c * D:(c + 2) * D].rearrange("p (a n) -> p a n", a=2),
                        in_=s), inc="dS", n=16)
                e_wq2 = ("dS", p.cnt["dS"])
                p.wait("tensor", e_wq2)
                src_evs = ([evs.get(f"a_store_{t}") for t in range(NT)]
                           if "P6" in phases else
                           ([evs.get(f"f1_store_{t}") for t in range(NT)]
                            if "P3" in phases else None))
                emit_ffn("f2", xs, y_out, wqfc2, wqpj2, src_wait_evs=src_evs)

        p.wait("sync", ("dO", p.cnt["dO"]))
        p.flush_block("fin")

    return nc


_NC_CACHE = {}


def _get_nc():
    if "full" not in _NC_CACHE:
        _NC_CACHE["full"] = build_program()
    return _NC_CACHE["full"]


def kernel(L0_attn_x, L0_attn_wqkv, L0_attn_wo, L0_attn_q_norm_w, L0_attn_k_norm_w,
           L0_ffn_W_fc, L0_ffn_ln_w, L0_ffn_ln_b, L0_ffn_W_proj):
    x = np.ascontiguousarray(np.asarray(L0_attn_x, dtype=np.float32))
    wqkvT = np.ascontiguousarray(np.asarray(L0_attn_wqkv, dtype=np.float32).T)
    woT = np.ascontiguousarray(np.asarray(L0_attn_wo, dtype=np.float32).T)
    wfcT = np.ascontiguousarray(np.asarray(L0_ffn_W_fc, dtype=np.float32).T)
    wpjT = np.ascontiguousarray(np.asarray(L0_ffn_W_proj, dtype=np.float32).T)
    qnw = np.ascontiguousarray(np.asarray(L0_attn_q_norm_w, dtype=np.float32).reshape(1, HD))
    knw = np.ascontiguousarray(np.asarray(L0_attn_k_norm_w, dtype=np.float32).reshape(1, HD))
    lnw = np.ascontiguousarray(np.asarray(L0_ffn_ln_w, dtype=np.float32).reshape(32, 128))
    lnb = np.ascontiguousarray(np.asarray(L0_ffn_ln_b, dtype=np.float32).reshape(32, 128))

    nc = _get_nc()
    in_maps = []
    for b in range(NCORES):
        in_maps.append({
            "x": np.ascontiguousarray(x[b]),
            "wqkvT": wqkvT, "woT": woT, "wfcT": wfcT, "wpjT": wpjT,
            "qnw": qnw, "knw": knw, "lnw": lnw, "lnb": lnb,
        })
    res = run_bass_kernel_spmd(nc, in_maps, core_ids=list(range(NCORES)))
    return np.stack([np.asarray(r["y"], dtype=np.float32) for r in res.results], axis=0)



# version 27
# speedup vs baseline: 1.7695x; 1.7695x over previous
import math
import contextlib
import numpy as np

import concourse.bass as bass
import concourse.mybir as mybir
from concourse.bass_utils import run_bass_kernel_spmd

F32 = mybir.dt.float32
BF16 = mybir.dt.bfloat16
I32 = mybir.dt.int32
AF = mybir.ActivationFunctionType
ALU = mybir.AluOpType
AX = mybir.AxisListType

T = 2048
D = 1024
HF = 4096
NH = 16
HD = 64
NT = T // 128
RS = NT * 32
NCORES = 8


def set_dims(nt):
    global T, NT, RS
    NT = nt
    T = nt * 128
    RS = NT * 32

ROPE_THETA = 10000.0
TWO_PI = 2 * math.pi

ENGINES = ("tensor", "vector", "scalar", "gpsimd", "sync")
DEBUG = False


def bcast_free(ap, n):
    return bass.AP(ap.tensor, ap.offset, [list(ap.ap[0]), [0, n]])


def head_view(ap, col_off, nh, w, stride):
    return bass.AP(ap.tensor, ap.offset + col_off, [list(ap.ap[0]), [stride, nh], [1, w]])


def head_bcast(ap, col_off, nh, w):
    return bass.AP(ap.tensor, ap.offset + col_off, [list(ap.ap[0]), [0, nh], [1, w]])


def pair_view(dram_ap, c, ncols):
    off = c * 128 * ncols
    return bass.AP(dram_ap.tensor, dram_ap.offset + off,
                   [[ncols, 128], [128 * ncols, 2], [1, ncols]])


class Prog:
    def __init__(self, nc):
        self.nc = nc
        self.cmds = {e: [] for e in ENGINES}
        self.sems = {}
        self.cnt = {}
        self.hw = {e: {} for e in ENGINES}
        self.blocks = 0

    def sem(self, name):
        h = self.nc.semaphore(name).__enter__()
        self.sems[name] = h
        self.cnt[name] = 0
        return h

    def emit(self, eng, fn, inc=None, n=1):
        ev = None
        if inc is not None:
            self.cnt[inc] += n
            ev = (inc, self.cnt[inc])
            sem = self.sems[inc]

            def cl(obj, fn=fn, sem=sem, n=n):
                ins = fn(obj)
                assert ins is not None, "emitted fn with inc must return an instruction"
                ins.then_inc(sem, n)
        else:
            def cl(obj, fn=fn):
                fn(obj)
        self.cmds[eng].append(cl)
        return ev

    ENG_SEM = {"tensor": "pe", "vector": "dve", "scalar": "act", "gpsimd": "pool"}

    def gemit(self, eng, fn, n=1):
        ev = self.emit(eng, fn, inc=self.ENG_SEM[eng], n=n)
        self.wait(eng, ev)
        return ev

    def wait(self, eng, ev):
        if ev is None:
            return
        s, v = ev
        assert v <= self.cnt[s], f"wait on future event {s}={v} > {self.cnt[s]}"
        if self.hw[eng].get(s, 0) >= v:
            return
        self.hw[eng][s] = v
        sem = self.sems[s]
        self.cmds[eng].append(lambda obj, sem=sem, v=v: obj.wait_ge(sem, v))

    def wait_all(self, eng, evs):
        best = {}
        for ev in evs:
            if ev is None:
                continue
            s, v = ev
            if best.get(s, 0) < v:
                best[s] = v
        for s, v in best.items():
            self.wait(eng, (s, v))

    def flush_block(self, name):
        cmds = self.cmds
        self.cmds = {e: [] for e in ENGINES}
        nonempty = [e for e in ENGINES if cmds[e]]
        if not nonempty:
            return
        with self.nc.Block(f"b{self.blocks}_{name}") as blk:
            for e in nonempty:
                def replay(obj, lst=cmds[e]):
                    for c in lst:
                        c(obj)
                getattr(blk, e)(replay)
        self.blocks += 1
        for s in ("pe", "dve", "act", "pool"):
            if s in self.cnt:
                for e in ENGINES:
                    self.hw[e][s] = self.cnt[s]


class Res:
    __slots__ = ("ready", "cons")

    def __init__(self):
        self.ready = None
        self.cons = []

    def begin_write(self, p, eng):
        p.wait_all(eng, self.cons)

    def set_ready(self, ev):
        self.ready = ev
        self.cons = []

    def add_consumer(self, ev):
        if ev is not None:
            self.cons.append(ev)


def build_program(phases=("P0", "P1", "P3", "P5", "P6", "P7")):
    terminal = "P7" if "P7" in phases else ("P6" if "P6" in phases else "P3")
    nc = bass.Bass("TRN2", target_bir_lowering=False)
    dbg_tensors = {}

    def dbg_decl(name, shape, dt):
        if not DEBUG:
            return None
        dbg_tensors[name] = nc.dram_tensor(f"dbg_{name}", shape, dt,
                                           kind="ExternalOutput").ap()
        return dbg_tensors[name]

    x_in = nc.dram_tensor("x", [T, D], F32, kind="ExternalInput").ap()
    wqkvT = nc.dram_tensor("wqkvT", [D, 3 * D], F32, kind="ExternalInput").ap()
    woT = nc.dram_tensor("woT", [D, D], F32, kind="ExternalInput").ap()
    wfcT = nc.dram_tensor("wfcT", [D, HF], F32, kind="ExternalInput").ap()
    wpjT = nc.dram_tensor("wpjT", [HF, D], F32, kind="ExternalInput").ap()
    qnw = nc.dram_tensor("qnw", [1, HD], F32, kind="ExternalInput").ap()
    knw = nc.dram_tensor("knw", [1, HD], F32, kind="ExternalInput").ap()
    lnw = nc.dram_tensor("lnw", [32, 128], F32, kind="ExternalInput").ap()
    lnb = nc.dram_tensor("lnb", [32, 128], F32, kind="ExternalInput").ap()
    y_out = nc.dram_tensor("y", [T, D], F32, kind="ExternalOutput").ap()

    xs = nc.dram_tensor("xs", [T, D], F32, kind="Internal").ap()
    wqfc_s = nc.dram_tensor("wqfc_s", [D, HF], BF16, kind="Internal").ap()
    wqpj_s = nc.dram_tensor("wqpj_s", [HF, D], BF16, kind="Internal").ap()
    wo_c = nc.dram_tensor("wo_c", [D, D], BF16, kind="Internal").ap()

    es = contextlib.ExitStack()
    with es:
        p = Prog(nc)
        for s in ("dS", "dG", "dO", "pe", "dve", "act", "pool"):
            p.sem(s)

        def dbg_dump(name, src_ap, shape, dt, wait_evs=(), res=None):
            if not DEBUG:
                return
            d = dbg_decl(name, shape, dt)
            p.wait_all("sync", [e for e in wait_evs if e is not None])
            ev = p.emit("sync", lambda o, d=d, s=src_ap: o.dma_start(out=d, in_=s),
                        inc="dO", n=16)
            if res is not None:
                res.add_consumer(ev)

        def sb(name, shape, dt):
            return es.enter_context(nc.sbuf_tensor(name, shape, dt))

        ident_b = sb("ident_b", [128, 128], BF16)
        ident_f = sb("ident_f", [128, 128], F32)
        mask_b = sb("mask_b", [128, 128], BF16)
        ones_row_b = sb("ones_row_b", [1, 128], BF16)
        ones_row_f = sb("ones_row_f", [1, 128], F32)
        ones_col_f = sb("ones_col_f", [128, 1], F32)
        ones_col_b = sb("ones_col_b", [128, 1], BF16)
        ropeq = sb("ropeq", [128, 4 * RS], F32)
        ropek = sb("ropek", [128, 4 * RS], F32)
        halfb = sb("halfb", [128, 1024], F32)
        halfc = sb("halfc", [128, 1024], F32)
        lnwT = sb("lnwT", [128, 32], F32)
        lnbT = sb("lnbT", [128, 32], BF16)
        thr2 = sb("thr2", [128, 2], F32)
        st = sb("st", [128, 64], F32)
        gst = sb("gst", [128, 8], F32)
        cst = sb("cst", [128, 2], F32)

        def init_cst(g):
            g.memset(st[:], 0.0)
            g.memset(gst[:], 0.0)
            g.memset(cst[:, 0:1], 1e-5)
            return g.memset(cst[:, 1:2], 1e-6)
        p.emit("gpsimd", init_cst, inc="pool")
        p.flush_block("cst")

        evs = {}

        if "P0" in phases:
            with contextlib.ExitStack() as ps0:
                qn_sb = ps0.enter_context(nc.sbuf_tensor("qn_sb", [1, HD], F32))
                kn_sb = ps0.enter_context(nc.sbuf_tensor("kn_sb", [1, HD], F32))
                lnw_r = ps0.enter_context(nc.sbuf_tensor("lnw_r", [32, 128], F32))
                lnb_r = ps0.enter_context(nc.sbuf_tensor("lnb_r", [32, 128], BF16))
                wbc = ps0.enter_context(nc.sbuf_tensor("wbc", [128, 128], F32))
                ang = ps0.enter_context(nc.sbuf_tensor("ang", [128, RS], F32))
                ang2 = ps0.enter_context(nc.sbuf_tensor("ang2", [128, RS], F32))
                rt1 = ps0.enter_context(nc.sbuf_tensor("rt1", [128, RS], F32))
                rk32 = ps0.enter_context(nc.sbuf_tensor("rk32", [128, RS], I32))
                rkf = ps0.enter_context(nc.sbuf_tensor("rkf", [128, RS], F32))
                sin_t = ps0.enter_context(nc.sbuf_tensor("sin_t", [128, RS], F32))
                cos_t = ps0.enter_context(nc.sbuf_tensor("cos_t", [128, RS], F32))
                invf = ps0.enter_context(nc.sbuf_tensor("invf", [128, 32], F32))
                pos = ps0.enter_context(nc.sbuf_tensor("pos", [128, NT], F32))
                p0_ps = ps0.enter_context(nc.psum_tensor("p0_ps", [128, 1024], F32))
                _ = None
                p.emit("sync", lambda o: o.dma_start(out=qn_sb[:], in_=qnw), inc="dS", n=16)
                p.emit("sync", lambda o: o.dma_start(out=kn_sb[:], in_=knw), inc="dS", n=16)
                e_qkn = ("dS", p.cnt["dS"])
                e_lnw = p.emit("sync", lambda o: o.dma_start(out=lnw_r[:], in_=lnw), inc="dS", n=16)
                e_lnb = p.emit("gpsimd", lambda o: o.dma_start(out=lnb_r[:], in_=lnb), inc="dG", n=16)

                def g0a(g):
                    g.memset(ones_row_b[:], 1.0)
                    g.memset(ones_row_f[:], 1.0)
                    g.memset(ones_col_f[:], 1.0)
                    g.memset(ones_col_b[:], 1.0)
                    g.memset(ident_b[:], 1.0)
                    g.memset(ident_f[:], 1.0)
                    return g.memset(mask_b[:], 0.0)
                p.gemit("gpsimd", g0a)

                def g0b(g):
                    g.affine_select(mask_b[:], mask_b[:], [[-1, 128]], ALU.is_ge, -1e9,
                                    base=0, channel_multiplier=1)
                    g.affine_select(ident_b[:], ident_b[:], [[-1, 128]], ALU.is_equal, 0.0,
                                    base=0, channel_multiplier=1)
                    g.affine_select(ident_f[:], ident_f[:], [[-1, 128]], ALU.is_equal, 0.0,
                                    base=0, channel_multiplier=1)
                    g.iota(invf[:], [[1, 32]], base=0, channel_multiplier=0,
                           allow_small_or_imprecise_dtypes=True)
                    return g.iota(pos[:], [[128, NT]], base=0, channel_multiplier=1,
                                  allow_small_or_imprecise_dtypes=True)
                e_g0 = p.gemit("gpsimd", g0b)

                p.wait("scalar", e_g0)
                e_invf = p.emit("scalar", lambda o: o.activation(
                    invf[:], invf[:], AF.Exp, bias=0.0,
                    scale=-2.0 * math.log(ROPE_THETA) / HD), inc="act")

                p.wait("vector", e_invf)

                def mk_ang(v):
                    in0 = bass.AP(pos[:].tensor, pos[:].offset,
                                  [list(pos[:].ap[0]), [1, NT], [0, 32]])
                    in1 = bass.AP(invf[:].tensor, invf[:].offset,
                                  [list(invf[:].ap[0]), [0, NT], [1, 32]])
                    v.tensor_tensor(ang[:].rearrange("p (a b) -> p a b", b=32),
                                    in0, in1, ALU.mult)
                    v.tensor_scalar(rt1[:], ang[:], 1.0 / TWO_PI, None, ALU.mult)
                    v.tensor_copy(rk32[:], rt1[:])
                    v.tensor_copy(rkf[:], rk32[:])
                    v.scalar_tensor_tensor(rt1[:], rkf[:], -TWO_PI, ang[:], ALU.mult, ALU.add)
                    v.tensor_scalar(rkf[:], rt1[:], math.pi, None, ALU.is_gt)
                    v.scalar_tensor_tensor(rt1[:], rkf[:], -TWO_PI, rt1[:], ALU.mult, ALU.add)
                    v.tensor_scalar(ang2[:], ang[:], math.pi / 2, None, ALU.add)
                    v.tensor_scalar(ang[:], ang2[:], 1.0 / TWO_PI, None, ALU.mult)
                    v.tensor_copy(rk32[:], ang[:])
                    v.tensor_copy(rkf[:], rk32[:])
                    v.scalar_tensor_tensor(ang2[:], rkf[:], -TWO_PI, ang2[:],
                                           ALU.mult, ALU.add)
                    v.tensor_scalar(rkf[:], ang2[:], math.pi, None, ALU.is_gt)
                    return v.scalar_tensor_tensor(ang2[:], rkf[:], -TWO_PI, ang2[:],
                                                  ALU.mult, ALU.add)
                e_rr = p.emit("vector", mk_ang, inc="dve")

                p.wait("scalar", e_rr)
                p.emit("scalar", lambda o: o.activation(sin_t[:], rt1[:], AF.Sin))
                e_sc = p.emit("scalar", lambda o: o.activation(cos_t[:], ang2[:], AF.Sin),
                              inc="act")

                p.wait("tensor", e_qkn)
                p.wait("tensor", e_lnw)
                p.wait("tensor", e_g0)

                def pe0(t_):
                    t_.matmul(p0_ps[:, 0:64], ones_row_f[0:1, :], qn_sb[0:1, :],
                              start=True, stop=True)
                    t_.matmul(p0_ps[:, 64:128], ones_row_f[0:1, :], kn_sb[0:1, :],
                              start=True, stop=True)
                    return t_.transpose(p0_ps[:, 512:544], lnw_r[:, :], ident_f[0:32, 0:32])
                e_pe0 = p.emit("tensor", pe0, inc="pe")
                p.wait("vector", e_pe0)

                def ev0(v):
                    v.tensor_copy(wbc[:], p0_ps[:, 0:128])
                    return v.tensor_copy(lnwT[:], p0_ps[:, 512:544])
                e_ev0 = p.emit("vector", ev0, inc="dve")

                p.wait("vector", e_sc)

                def tables(v):
                    last = None
                    for dst, wofs, src, scale in (
                        (ropeq[:, 0:RS], 0, cos_t, 0.125),
                        (ropeq[:, RS:2 * RS], 32, sin_t, 0.125),
                        (ropeq[:, 2 * RS:3 * RS], 0, sin_t, 0.125),
                        (ropeq[:, 3 * RS:4 * RS], 32, cos_t, 0.125),
                        (ropek[:, 0:RS], 64, cos_t, 1.0),
                        (ropek[:, RS:2 * RS], 96, sin_t, 1.0),
                        (ropek[:, 2 * RS:3 * RS], 64, sin_t, 1.0),
                        (ropek[:, 3 * RS:4 * RS], 96, cos_t, 1.0),
                    ):
                        last = v.scalar_tensor_tensor(
                            dst.rearrange("p (a b) -> p a b", b=32),
                            src[:].rearrange("p (a b) -> p a b", b=32),
                            scale, head_bcast(wbc[:], wofs, NT, 32), ALU.mult, ALU.mult)
                    return last
                p.emit("vector", tables)
                p.flush_block("p0a")

                with nc.psum_tensor("p0b_ps", [128, 32], BF16) as p0b_ps:
                    p.wait("tensor", e_lnb)
                    e_tb = p.emit("tensor", lambda o: o.transpose(
                        p0b_ps[:, :], lnb_r[:, :], ident_b[0:32, 0:32]), inc="pe")
                    p.wait("vector", e_tb)
                    e_lnbT = p.emit("vector", lambda o: o.tensor_copy(lnbT[:], p0b_ps[:, :]),
                                    inc="dve")
                    dbg_dump("lnwT", lnwT[:], [128, 32], F32, [e_ev0])
                    dbg_dump("lnbT", lnbT[:], [128, 32], BF16, [e_lnbT])
                    p.flush_block("p0b")

        def emit_ffn(tag, src_ap, dst_ap, wqfc, wqpj, src_wait_evs=None,
                     xin_pre=None):
            NSUP = NT // 2
            NCH = HF // 128
            with contextlib.ExitStack() as psf:
                if xin_pre is None:
                    xin0 = psf.enter_context(nc.sbuf_tensor(f"{tag}_xin0", [128, 2048], F32))
                    r_x0 = Res()
                else:
                    xin0, r_x0 = xin_pre
                xin1 = psf.enter_context(nc.sbuf_tensor(f"{tag}_xin1", [128, 2048], F32))
                r_xin = [r_x0, Res()]
                lnx = psf.enter_context(nc.sbuf_tensor(f"{tag}_lnx", [128, 2048], BF16))
                lxT0 = psf.enter_context(nc.sbuf_tensor(f"{tag}_lxT0", [128, 2048], BF16))
                lxT1 = psf.enter_context(nc.sbuf_tensor(f"{tag}_lxT1", [128, 2048], BF16))
                cmb = psf.enter_context(nc.sbuf_tensor(f"{tag}_cmb", [128, 2048], BF16))
                tmpv = psf.enter_context(nc.sbuf_tensor(f"{tag}_tmp", [128, 2048], F32))
                rowb = psf.enter_context(nc.sbuf_tensor(f"{tag}_row", [1, 512], F32))
                stg_ps = psf.enter_context(nc.psum_tensor(f"{tag}_stg", [128, 1024], BF16))
                hps0 = psf.enter_context(nc.psum_tensor(f"{tag}_h0", [128, 512], F32))
                hps1 = psf.enter_context(nc.psum_tensor(f"{tag}_h1", [128, 512], F32))
                o2a_ps = psf.enter_context(nc.psum_tensor(f"{tag}_o2a", [128, 1024], F32))
                o2b_ps = psf.enter_context(nc.psum_tensor(f"{tag}_o2b", [128, 1024], F32))
                sm_ps = psf.enter_context(nc.psum_tensor(f"{tag}_sm", [128, 512], F32))
                xin = [xin0, xin1]
                lxT = [lxT0, lxT1]
                hps = [hps0, hps1]
                o2 = [o2a_ps, o2b_ps]
                sm_f = sm_ps[:, :]

                r_lnx = Res()
                r_lxT = [Res(), Res()]
                r_stg = Res()
                r_hps = [Res(), Res()]
                r_cmb = [Res() for _ in range(4)]
                r_o2 = [Res(), Res()]
                r_sm = Res()
                lxT_evs = [[], []]
                gel_ev = {}
                sq_ev = {}
                mm2_ev = {}
                summ_ev = {}

                def newton_rstd(dst, src, tmp):
                    z = st[:, src:src + 2]
                    y = st[:, dst:dst + 2]
                    t = st[:, tmp:tmp + 2]
                    zi = z.bitcast(I32)
                    yi = y.bitcast(I32)
                    p.gemit("vector", lambda v, z=z: v.tensor_scalar(
                        z, z, 1e-5, None, ALU.add))
                    p.gemit("vector", lambda v, yi=yi, zi=zi: v.tensor_scalar(
                        yi, zi, 1, None, ALU.logical_shift_right))
                    p.gemit("vector", lambda v, yi=yi: v.tensor_scalar(
                        yi, yi, -1, None, ALU.bitwise_xor))
                    p.gemit("vector", lambda v, yi=yi: v.tensor_scalar(
                        yi, yi, 0x5F3759E0, None, ALU.add))
                    last = None
                    for _ in range(2):
                        p.gemit("vector", lambda v, t=t, y=y: v.tensor_tensor(
                            t, y, y, ALU.mult))
                        p.gemit("vector", lambda v, t=t, z=z: v.tensor_tensor(
                            t, t, z, ALU.mult))
                        p.gemit("vector", lambda v, t=t: v.tensor_scalar(
                            t, t, -0.5, 1.5, ALU.mult, ALU.add))
                        last = p.gemit("vector", lambda v, y=y, t=t:
                                       v.tensor_tensor(y, y, t, ALU.mult))
                    return last

                def emit_load(u):
                    par = u % 2
                    r_xin[par].begin_write(p, "sync")
                    if src_wait_evs is not None:
                        p.wait_all("sync",
                                   [src_wait_evs[2 * u], src_wait_evs[2 * u + 1]])
                    ev = p.emit("sync", lambda o, u=u, par=par: o.dma_start(
                        out=xin[par][:].rearrange("p (a n) -> p a n", a=2),
                        in_=pair_view(src_ap, 2 * u, D)), inc="dS", n=16)
                    r_xin[par].set_ready(ev)

                def emit_ln_stats(u):
                    par = u % 2
                    co = 32 * par
                    p.wait("vector", r_xin[par].ready)
                    r_lnx.begin_write(p, "vector")
                    x3 = xin[par][:].rearrange("p (a n) -> p a n", a=2)
                    p.gemit("vector", lambda v, x3=x3, co=co: v.reduce_sum(
                        st[:, co:co + 2], x3, AX.X))
                    for j in range(2):
                        p.gemit("vector", lambda v, par=par, j=j, co=co:
                                v.scalar_tensor_tensor(
                                    lnx[:, j * D:(j + 1) * D],
                                    xin[par][:, j * D:(j + 1) * D], 1.0,
                                    xin[par][:, j * D:(j + 1) * D], ALU.mult,
                                    ALU.mult,
                                    accum_out=st[:, co + 2 + j:co + 3 + j]))
                    p.gemit("vector", lambda v, co=co: v.tensor_scalar(
                        st[:, co + 4:co + 6], st[:, co:co + 2], 1.0 / D, None,
                        ALU.mult))
                    p.gemit("vector", lambda v, co=co: v.tensor_tensor(
                        st[:, co + 20:co + 22], st[:, co + 4:co + 6],
                        st[:, co + 4:co + 6], ALU.mult))
                    p.gemit("vector", lambda v, co=co: v.scalar_tensor_tensor(
                        st[:, co + 6:co + 8], st[:, co + 2:co + 4], 1.0 / D,
                        st[:, co + 20:co + 22], ALU.mult, ALU.subtract))

                def emit_rstd_apply(u):
                    par = u % 2
                    co = 32 * par
                    newton_rstd(co + 8, co + 6, co + 20)
                    ev = None
                    for j in range(2):
                        ev = p.emit("vector", lambda o, par=par, j=j, co=co:
                                    o.tensor_scalar(
                                        lnx[:, j * D:(j + 1) * D],
                                        xin[par][:, j * D:(j + 1) * D],
                                        st[:, co + 4 + j:co + 5 + j],
                                        st[:, co + 8 + j:co + 9 + j],
                                        ALU.subtract, ALU.mult), inc="dve")
                    r_lnx.set_ready(ev)

                def emit_tp(u, half):
                    par = u % 2
                    r_stg.begin_write(p, "tensor")
                    p.wait("tensor", r_lnx.ready)

                    def tp_ln(t_, half=half):
                        last = None
                        for k in range(8):
                            c = half * 4 + k // 2
                            j = k % 2
                            last = t_.transpose(
                                stg_ps[:, k * 128:(k + 1) * 128],
                                lnx[:, j * D + c * 128:j * D + (c + 1) * 128],
                                ident_b[:])
                        return last
                    e_tp = p.emit("tensor", tp_ln, inc="pe")
                    r_stg.set_ready(e_tp)
                    r_lnx.add_consumer(e_tp)
                    if half == 0:
                        r_lxT[par].begin_write(p, "scalar")
                        p.wait("scalar", e_tp)
                        e_te = p.emit("scalar", lambda o, par=par: o.copy(
                            lxT[par][:, 0:1024], stg_ps[:, :]), inc="act")
                        lxT_evs[par] = [e_te]
                    else:
                        r_lxT[par].begin_write(p, "vector")
                        p.wait("vector", e_tp)
                        e_te = p.emit("vector", lambda o, par=par: o.tensor_copy(
                            lxT[par][:, 1024:2048], stg_ps[:, :]), inc="dve")
                        lxT_evs[par] = lxT_evs[par] + [e_te]
                    r_stg.add_consumer(e_te)

                def emit_chunk(u, c):
                    par = u % 2
                    gb = c % 2
                    s = c % 4
                    r_hps[gb].begin_write(p, "tensor")
                    if c == 0:
                        p.wait_all("tensor", lxT_evs[par])
                        if "wqfc_q" in evs:
                            p.wait("tensor", evs["wqfc_q"])
                        if "wqpj_fold" in evs:
                            p.wait("tensor", evs["wqpj_fold"])

                    def mm1(t_, par=par, c=c, gb=gb):
                        last = None
                        for k in range(8):
                            last = t_.matmul(
                                hps[gb][:, 0:256],
                                wqfc[:, k * HF + c * 128:k * HF + (c + 1) * 128],
                                lxT[par][:, k * 256:(k + 1) * 256],
                                start=(k == 0), stop=(k == 7))
                        return last
                    e_mm1 = p.emit("tensor", mm1, inc="pe")
                    r_hps[gb].set_ready(e_mm1)
                    r_lxT[par].add_consumer(e_mm1)

                    r_cmb[s].begin_write(p, "scalar")
                    p.wait("scalar", e_mm1)
                    e_gel = p.emit("scalar", lambda o, s=s, gb=gb: o.activation(
                        cmb[:, s * 512:s * 512 + 256], hps[gb][:, 0:256],
                        AF.Gelu), inc="act")
                    r_hps[gb].add_consumer(e_gel)
                    gel_ev[c] = e_gel

                    p.wait("gpsimd", e_gel)
                    e_sq = p.emit("gpsimd", lambda o, s=s: o.tensor_tensor(
                        cmb[:, s * 512 + 256:s * 512 + 512],
                        cmb[:, s * 512:s * 512 + 256],
                        cmb[:, s * 512:s * 512 + 256], ALU.mult), inc="pool")
                    sq_ev[c] = e_sq

                def emit_mm2(u, c):
                    s = c % 4
                    if c == 0:
                        r_o2[0].begin_write(p, "tensor")
                        r_o2[1].begin_write(p, "tensor")
                        r_sm.begin_write(p, "tensor")
                    p.wait("tensor", gel_ev[c])

                    def mm2(t_, c=c, s=s):
                        last = None
                        for j in range(2):
                            for g in range(2):
                                last = t_.matmul(
                                    o2[j][:, g * 512:(g + 1) * 512],
                                    cmb[:, s * 512 + j * 128:s * 512 + (j + 1) * 128],
                                    wqpj[:, c * D + g * 512:c * D + (g + 1) * 512],
                                    start=(c == 0), stop=(c == NCH - 1))
                        return last
                    e_mm2 = p.emit("tensor", mm2, inc="pe")
                    mm2_ev[c] = e_mm2
                    p.wait("tensor", sq_ev[c])
                    e_summ = p.emit("tensor", lambda t_, s=s, c=c: t_.matmul(
                        sm_ps[0:1, 0:512], ones_col_b[:, 0:1],
                        cmb[:, s * 512:(s + 1) * 512],
                        start=(c == 0), stop=(c == NCH - 1)), inc="pe")
                    summ_ev[c] = e_summ
                    r_cmb[s].add_consumer(e_mm2)
                    r_cmb[s].add_consumer(e_summ)

                def emit_tail(u):
                    par = u % 2
                    co = 32 * par
                    e_s31 = summ_ev[NCH - 1]
                    e_m31 = mm2_ev[NCH - 1]
                    r_o2[0].set_ready(e_m31)
                    r_o2[1].set_ready(e_m31)
                    r_sm.set_ready(e_s31)
                    p.wait("vector", e_s31)
                    e_rows = p.gemit("vector", lambda v: v.tensor_copy(
                        rowb[0:1, :], sm_ps[0:1, 0:512]))
                    r_sm.add_consumer(e_rows)
                    p.wait("tensor", e_rows)

                    def row_tp(t_):
                        ins = None
                        for q in range(4):
                            ins = t_.matmul(sm_f[:, q:q + 1],
                                            rowb[0:1, q * 128:(q + 1) * 128],
                                            ones_row_f[0:1, 0:1],
                                            start=True, stop=True)
                        return ins
                    e_rtp = p.emit("tensor", row_tp, inc="pe")
                    p.wait("scalar", e_m31)
                    ev0a = p.emit("scalar", lambda o: o.activation(
                        tmpv[:, 0:1024], o2[0][:, :], AF.Copy, scale=0.5),
                        inc="act")
                    p.wait("vector", e_m31)
                    ev0b = p.emit("vector", lambda o: o.tensor_scalar(
                        tmpv[:, 1024:2048], o2[1][:, :], 0.5, None, ALU.mult),
                        inc="dve")
                    r_o2[0].add_consumer(ev0a)
                    r_o2[1].add_consumer(ev0b)
                    p.wait("vector", e_rtp)
                    p.gemit("vector", lambda v, co=co: v.tensor_scalar(
                        st[:, co + 12:co + 14], sm_f[:, 0:2], 1.0 / HF, None,
                        ALU.mult))
                    p.gemit("vector", lambda v, co=co: v.tensor_tensor(
                        st[:, co + 20:co + 22], st[:, co + 12:co + 14],
                        st[:, co + 12:co + 14], ALU.mult))
                    e_var = p.gemit("vector", lambda v, co=co:
                                    v.scalar_tensor_tensor(
                                        st[:, co + 14:co + 16], sm_f[:, 2:4],
                                        1.0 / HF, st[:, co + 20:co + 22],
                                        ALU.mult, ALU.subtract))
                    r_sm.add_consumer(e_var)
                    newton_rstd(co + 16, co + 14, co + 20)
                    p.gemit("vector", lambda v, co=co: v.tensor_tensor(
                        st[:, co + 18:co + 20], st[:, co + 12:co + 14],
                        st[:, co + 16:co + 18], ALU.mult))
                    p.gemit("vector", lambda v, co=co: v.tensor_scalar(
                        st[:, co + 18:co + 20], st[:, co + 18:co + 20], -1.0,
                        None, ALU.mult))
                    p.wait("vector", ev0a)
                    for j in range(2):
                        p.gemit("vector", lambda v, j=j, co=co:
                                v.scalar_tensor_tensor(
                                    tmpv[:, j * 1024:(j + 1) * 1024],
                                    tmpv[:, j * 1024:(j + 1) * 1024],
                                    st[:, co + 16 + j:co + 17 + j],
                                    halfb[:], ALU.mult, ALU.add))
                        p.gemit("vector", lambda v, j=j, co=co:
                                v.scalar_tensor_tensor(
                                    tmpv[:, j * 1024:(j + 1) * 1024],
                                    halfc[:],
                                    st[:, co + 18 + j:co + 19 + j],
                                    tmpv[:, j * 1024:(j + 1) * 1024],
                                    ALU.mult, ALU.add))
                        e_ev3 = p.gemit("vector", lambda v, par=par, j=j:
                                        v.tensor_tensor(
                                            xin[par][:, j * D:(j + 1) * D],
                                            xin[par][:, j * D:(j + 1) * D],
                                            tmpv[:, j * 1024:(j + 1) * 1024],
                                            ALU.add))
                        r_xin[par].add_consumer(e_ev3)
                        p.wait("sync", e_ev3)
                        e_sto = p.emit("sync", lambda o, u=u, j=j, par=par:
                                       o.dma_start(
                                           out=dst_ap[(2 * u + j) * 128:
                                                      (2 * u + j + 1) * 128, :],
                                           in_=xin[par][:, j * D:(j + 1) * D]),
                                       inc="dO", n=16)
                        r_xin[par].add_consumer(e_sto)
                        evs[f"{tag}_store_{2 * u + j}"] = e_sto

                if xin_pre is None:
                    emit_load(0)
                if NSUP > 1:
                    emit_load(1)
                emit_ln_stats(0)
                emit_rstd_apply(0)
                emit_tp(0, 0)
                emit_tp(0, 1)

                for u in range(NSUP):
                    nxt = u + 1
                    for c in range(NCH):
                        emit_chunk(u, c)
                        if c == 2 and nxt < NSUP:
                            emit_ln_stats(nxt)
                        if c == 8 and nxt < NSUP:
                            emit_rstd_apply(nxt)
                        if c >= 1:
                            emit_mm2(u, c - 1)
                        if c == 20 and nxt < NSUP:
                            emit_tp(nxt, 0)
                        if c == 26 and nxt < NSUP:
                            emit_tp(nxt, 1)
                    emit_mm2(u, NCH - 1)
                    emit_tail(u)
                    if nxt + 1 < NSUP:
                        emit_load(nxt + 1)
                if DEBUG:
                    dbg_dump("cmb", cmb[:], [128, 2048], BF16,
                             [gel_ev[NCH - 1], sq_ev[NCH - 1]])
                p.flush_block(tag)

        if "P1" in phases or "P3" in phases:
            with (
                nc.sbuf_tensor("wqfc", [128, 8 * HF], BF16) as wqfc,
                nc.sbuf_tensor("wqpj", [128, 32 * D], BF16) as wqpj,
                nc.sbuf_tensor("f1_xin0", [128, 2048], F32) as f1_xin0,
            ):
                r_xin_f1 = Res()
                if "P1" in phases:
                    with contextlib.ExitStack() as ps1:
                        wld0 = ps1.enter_context(nc.sbuf_tensor("wld0", [128, 2048], F32, side="right"))
                        wld1 = ps1.enter_context(nc.sbuf_tensor("wld1", [128, 2048], F32, side="right"))
                        wld2 = ps1.enter_context(nc.sbuf_tensor("wld2", [128, 2048], F32, side="right"))
                        wsq = ps1.enter_context(nc.sbuf_tensor("wsq", [128, 2048], F32, side="right"))
                        wsg = ps1.enter_context(nc.sbuf_tensor("wsg", [128, 2048], BF16, side="right"))
                        bvec_c = ps1.enter_context(nc.sbuf_tensor("bvec_c", [1, 1024], BF16, side="right"))
                        bvec_b = ps1.enter_context(nc.sbuf_tensor("bvec_b", [1, 1024], BF16, side="right"))
                        p1_ps = ps1.enter_context(nc.psum_tensor("p1_ps", [128, 2048], F32))
                        _ = None
                        wld = [wld0, wld1, wld2]
                        e_x = p.emit("scalar", lambda o: o.dma_start(
                            out=f1_xin0[:].rearrange("p (a n) -> p a n", a=2),
                            in_=pair_view(x_in, 0, D)), inc="dO", n=16)
                        r_xin_f1.set_ready(e_x)
                        halves = []
                        for c in range(8):
                            for hh in range(2):
                                halves.append(
                                    (wfcT[c * 128:(c + 1) * 128, hh * 2048:(hh + 1) * 2048],
                                     wqfc, c * HF + hh * 2048, 0))
                        for c in range(0, 32, 2):
                            halves.append((pair_view(wpjT, c, D), wqpj, c * D, 1))

                        p.emit("vector", lambda o: o.memset(gst[:, 0:8], 0.0))
                        buf_free = [Res(), Res(), Res()]
                        e_a = None
                        for idx, (src, _, _, mat) in enumerate(halves):
                            b = idx % 3
                            q = "sync" if idx % 2 == 0 else "scalar"
                            qs = "dS" if idx % 2 == 0 else "dO"
                            buf_free[b].begin_write(p, q)
                            e_l = p.emit(q, lambda o, s=src, b=b: o.dma_start(
                                out=wld[b][:], in_=s), inc=qs, n=16)
                            buf_free[b].set_ready(e_l)
                            col = mat * 4 + b
                            p.wait("vector", e_l)
                            p.gemit("vector", lambda v, b=b: v.reduce_sum(
                                st[:, b:b + 1], wld[b][:], AX.X,
                                apply_absolute_value=True))
                            e_a = p.gemit("vector", lambda v, col=col, b=b: v.tensor_tensor(
                                gst[:, col:col + 1], gst[:, col:col + 1],
                                st[:, b:b + 1], ALU.add))
                            buf_free[b].add_consumer(e_a)

                        p.gemit("vector", lambda v: v.reduce_sum(st[:, 0:1], gst[:, 0:4],
                                                                 AX.X))
                        e_a = p.gemit("vector", lambda v: v.reduce_sum(st[:, 1:2],
                                                                       gst[:, 4:8], AX.X))
                        p.wait("tensor", e_a)

                        def psum_gamma(t_):
                            t_.matmul(p1_ps[0:1, 0:1], st[:, 0:1], ones_col_f[:],
                                      start=True, stop=True)
                            return t_.matmul(p1_ps[0:1, 1:2], st[:, 1:2], ones_col_f[:],
                                             start=True, stop=True)
                        e_pg = p.emit("tensor", psum_gamma, inc="pe")
                        p.wait("vector", e_pg)

                        p.gemit("vector", lambda v: v.tensor_scalar(
                            st[0:1, 2:4], p1_ps[0:1, 0:2], 1.0 / (HF * D), None, ALU.mult))
                        p.gemit("vector", lambda v: v.tensor_scalar(
                            st[0:1, 2:4], st[0:1, 2:4], 1e-6, 0.5, ALU.add, ALU.mult))
                        e_thr = p.gemit("vector", lambda v: v.tensor_tensor(
                            st[0:1, 2:4], st[0:1, 2:4], st[0:1, 2:4], ALU.mult))
                        p.wait("tensor", e_thr)
                        e_bthr = p.emit("tensor", lambda o: o.matmul(
                            p1_ps[:, 512:514], ones_row_f[0:1, :], st[0:1, 2:4],
                            start=True, stop=True), inc="pe")
                        p.wait("vector", e_bthr)
                        e_thr2 = p.emit("vector", lambda o: o.tensor_copy(
                            thr2[:], p1_ps[:, 512:514]), inc="dve")
                        dbg_dump("thr2", thr2[:], [128, 2], F32, [e_thr2])
                        dbg_dump("gst", gst[:, 0:2], [128, 2], F32, [e_thr2])

                        wsq2 = [wsq, wsq]
                        wsg2 = [wsg, wsg]
                        _sq = Res()
                        _sg = Res()
                        sq_r = [_sq, _sq]
                        sg_r = [_sg, _sg]
                        for idx, (src, dst, dcol, mat) in enumerate(halves):
                            b = idx % 3
                            b2 = idx % 2
                            q = "sync" if idx % 2 == 0 else "gpsimd"
                            qs = "dS" if idx % 2 == 0 else "dG"
                            buf_free[b].begin_write(p, q)
                            e_l = p.emit(q, lambda o, s=src, b=b: o.dma_start(
                                out=wld[b][:], in_=s), inc=qs, n=16)
                            buf_free[b].set_ready(e_l)

                            sq_r[b2].begin_write(p, "gpsimd")
                            p.wait("gpsimd", e_l)
                            e_sq = p.emit("gpsimd", lambda o, b=b, b2=b2: o.tensor_tensor(
                                wsq2[b2][:], wld[b][:], wld[b][:], ALU.mult), inc="pool")
                            sq_r[b2].set_ready(e_sq)

                            sg_r[b2].begin_write(p, "scalar")
                            p.wait("scalar", e_l)
                            e_sg = p.emit("scalar", lambda o, b=b, b2=b2: o.activation(
                                wsg2[b2][:], wld[b][:], AF.Sign), inc="act")
                            sg_r[b2].set_ready(e_sg)
                            buf_free[b].add_consumer(e_sq)
                            buf_free[b].add_consumer(e_sg)

                            p.wait("vector", e_sq)
                            p.wait("vector", e_sg)
                            e_q = p.emit("vector", lambda o, dst=dst, dcol=dcol, mat=mat,
                                         b2=b2: o.scalar_tensor_tensor(
                                             dst[:, dcol:dcol + 2048], wsq2[b2][:],
                                             thr2[:, mat:mat + 1], wsg2[b2][:],
                                             ALU.is_gt, ALU.mult), inc="dve")
                            sq_r[b2].add_consumer(e_q)
                            sg_r[b2].add_consumer(e_q)
                            if idx == 15:
                                evs["wqfc_q"] = e_q
                        evs["wqpj_q"] = e_q
                        dbg_dump("wqfc0", wqfc[:, 0:1024], [128, 1024], BF16, [e_q])
                        dbg_dump("wqpj0", wqpj[:, 0:1024], [128, 1024], BF16, [e_q])

                        p.wait("tensor", evs["wqpj_q"])
                        p.wait("tensor", e_thr2)
                        e_b_list = []
                        for g in range(2):
                            def bmm(t_, g=g):
                                last = None
                                for c in range(32):
                                    last = t_.matmul(
                                        p1_ps[0:1, g * 512:(g + 1) * 512],
                                        lnbT[:, c:c + 1],
                                        wqpj[:, c * D + g * 512: c * D + (g + 1) * 512],
                                        start=(c == 0), stop=(c == 31))
                                return last
                            e_b = p.emit("tensor", bmm, inc="pe")
                            e_b_list.append(e_b)
                            p.wait("scalar", e_b)
                            e_bc = p.emit("scalar", lambda o, g=g: o.copy(
                                bvec_b[0:1, g * 512:(g + 1) * 512],
                                p1_ps[0:1, g * 512:(g + 1) * 512]), inc="act")

                        p.wait_all("vector", e_b_list)
                        def fold(v):
                            last = None
                            for c in range(32):
                                last = v.tensor_scalar(wqpj[:, c * D:(c + 1) * D],
                                                       wqpj[:, c * D:(c + 1) * D],
                                                       lnwT[:, c:c + 1], None, ALU.mult)
                            return last
                        e_fold = p.emit("vector", fold, inc="dve")
                        evs["wqpj_fold"] = e_fold

                        p.wait("tensor", e_fold)
                        p.wait("tensor", e_bc)

                        def cv_mm(t_):
                            last = None
                            for g in range(2):
                                for c in range(32):
                                    last = t_.matmul(
                                        p1_ps[0:1, g * 512:(g + 1) * 512],
                                        ones_col_b[:, 0:1],
                                        wqpj[:, c * D + g * 512: c * D + (g + 1) * 512],
                                        start=(c == 0), stop=(c == 31))
                            return last
                        e_cv = p.emit("tensor", cv_mm, inc="pe")
                        p.wait("scalar", e_cv)
                        e_cvc = p.emit("scalar", lambda o: o.copy(
                            bvec_c[0:1, :], p1_ps[0:1, 0:1024]), inc="act")

                        p.wait("tensor", e_bc)
                        p.wait("tensor", e_cvc)
                        def hb_mm(t_):
                            t_.matmul(p1_ps[:, 0:512], ones_row_b[0:1, :],
                                      bvec_b[0:1, 0:512], start=True, stop=True)
                            t_.matmul(p1_ps[:, 512:1024], ones_row_b[0:1, :],
                                      bvec_b[0:1, 512:1024], start=True, stop=True)
                            t_.matmul(p1_ps[:, 1024:1536], ones_row_b[0:1, :],
                                      bvec_c[0:1, 0:512], start=True, stop=True)
                            return t_.matmul(p1_ps[:, 1536:2048], ones_row_b[0:1, :],
                                             bvec_c[0:1, 512:1024], start=True, stop=True)
                        e_hb = p.emit("tensor", hb_mm, inc="pe")
                        p.wait("vector", e_hb)
                        e_halfb = p.emit("vector", lambda o: (
                            o.tensor_scalar(halfb[:], p1_ps[:, 0:1024], 0.5, None,
                                            ALU.mult),
                            o.tensor_scalar(halfc[:], p1_ps[:, 1024:2048], 0.5, None,
                                            ALU.mult))[-1], inc="dve")
                        dbg_dump("halfb", halfb[:], [128, 1024], F32, [e_halfb])
                        dbg_dump("wqpjf0", wqpj[:, 0:1024], [128, 1024], BF16, [e_halfb])

                        p.wait("scalar", ("dve", p.cnt["dve"]))
                        for c in range(8):
                            p.emit("scalar", lambda o, c=c: o.dma_start(
                                out=wqfc_s[c * 128:(c + 1) * 128, :],
                                in_=wqfc[:, c * HF:(c + 1) * HF]), inc="dO", n=16)
                        for c in range(0, 32, 2):
                            dstv = pair_view(wqpj_s, c, D)
                            p.emit("scalar", lambda o, d=dstv, c=c: o.dma_start(
                                out=d, in_=wqpj[:, c * D:(c + 2) * D].rearrange(
                                    "p (a n) -> p a n", a=2)), inc="dO", n=16)
                        evs["wq_stored"] = ("dO", p.cnt["dO"])
                        p.flush_block("p1")

                if "P3" in phases:
                    for c in range(8):
                        p.emit("gpsimd", lambda o, c=c: o.dma_start(
                            out=wo_c[c * 128:(c + 1) * 128, :],
                            in_=woT[c * 128:(c + 1) * 128, :]), inc="dG", n=16)
                    evs["wo_c"] = ("dG", p.cnt["dG"])
                    emit_ffn("f1", x_in, y_out if terminal == "P3" else xs,
                             wqfc, wqpj,
                             xin_pre=(f1_xin0, r_xin_f1) if "P1" in phases else None)

        if "P5" in phases or "P6" in phases:
            with (
                nc.sbuf_tensor("qT_b", [128, 8 * T], BF16) as qT_b,
                nc.sbuf_tensor("kT_b", [128, 8 * T], BF16) as kT_b,
                nc.sbuf_tensor("v_b", [128, NT * D], BF16) as v_b,
            ):
                if "P5" in phases:
                    with contextlib.ExitStack() as ps5:
                        wqkv_b = ps5.enter_context(nc.sbuf_tensor("a_wqkv", [128, 8 * 3 * D], BF16))
                        axin0 = ps5.enter_context(nc.sbuf_tensor("a_xin0", [128, D], F32))
                        axin1 = ps5.enter_context(nc.sbuf_tensor("a_xin1", [128, D], F32))
                        alnx = ps5.enter_context(nc.sbuf_tensor("a_lnx", [128, D], BF16))
                        alxT = ps5.enter_context(nc.sbuf_tensor("a_lxT", [128, D], BF16))
                        qr_f = ps5.enter_context(nc.sbuf_tensor("a_qr", [128, D], F32))
                        kr_f = ps5.enter_context(nc.sbuf_tensor("a_kr", [128, D], F32))
                        qrope = ps5.enter_context(nc.sbuf_tensor("a_qrope", [128, D], BF16))
                        krope = ps5.enter_context(nc.sbuf_tensor("a_krope", [128, D], BF16))
                        tq1 = ps5.enter_context(nc.sbuf_tensor("a_tq1", [128, 512], F32))
                        tq2 = ps5.enter_context(nc.sbuf_tensor("a_tq2", [128, 512], F32))
                        tk1 = ps5.enter_context(nc.sbuf_tensor("a_tk1", [128, 512], F32))
                        tk2 = ps5.enter_context(nc.sbuf_tensor("a_tk2", [128, 512], F32))
                        qkv_ps = ps5.enter_context(nc.psum_tensor("a_qkv", [128, 3 * D], F32))
                        stA_ps = ps5.enter_context(nc.psum_tensor("a_stA", [128, 1024], BF16))
                        stB_ps = ps5.enter_context(nc.psum_tensor("a_stB", [128, 1024], BF16))
                        _ = None
                        xin = [axin0, axin1]
                        r_xin = [Res(), Res()]
                        r_lnx = Res()
                        r_lxT = Res()
                        r_stA = Res()
                        r_stB = Res()
                        r_qkv = Res()
                        r_qr = Res()
                        r_kr = Res()
                        r_qrope = Res()
                        r_krope = Res()

                        for c in range(8):
                            p.emit("gpsimd", lambda o, c=c: o.dma_start(
                                out=wqkv_b[:, c * 3 * D:(c + 1) * 3 * D],
                                in_=wqkvT[c * 128:(c + 1) * 128, :]), inc="dG", n=16)
                        e_wqkv = ("dG", p.cnt["dG"])

                        for t in range(NT):
                            par = t % 2
                            sc = 8 * par

                            r_xin[par].begin_write(p, "sync")
                            if "P3" in phases:
                                p.wait("sync", evs[f"f1_store_{t}"])
                            e_ld = p.emit("sync", lambda o, t=t, par=par: o.dma_start(
                                out=xin[par][:], in_=xs[t * 128:(t + 1) * 128, :]),
                                inc="dS", n=16)
                            r_xin[par].set_ready(e_ld)

                            p.wait("vector", e_ld)
                            r_lnx.begin_write(p, "vector")

                            p.gemit("vector", lambda v, par=par, sc=sc: (
                                v.reduce_sum(st[:, sc + 0:sc + 1], xin[par][:], AX.X),
                                v.scalar_tensor_tensor(alnx[:], xin[par][:], 1.0,
                                                       xin[par][:], ALU.mult, ALU.mult,
                                                       accum_out=st[:, sc + 1:sc + 2]))[-1])
                            p.gemit("vector", lambda v, sc=sc: v.tensor_scalar(
                                st[:, sc + 2:sc + 3], st[:, sc + 0:sc + 1],
                                1.0 / D, None, ALU.mult))
                            p.gemit("vector", lambda v, sc=sc: v.tensor_tensor(
                                st[:, sc + 3:sc + 4], st[:, sc + 2:sc + 3],
                                st[:, sc + 2:sc + 3], ALU.mult))
                            e_st = p.gemit("vector", lambda v, sc=sc: v.scalar_tensor_tensor(
                                st[:, sc + 4:sc + 5], st[:, sc + 1:sc + 2], 1.0 / D,
                                st[:, sc + 3:sc + 4], ALU.mult, ALU.subtract))
                            p.wait("scalar", e_st)

                            def rstd(s, sc=sc):
                                s.activation(st[:, sc + 5:sc + 6], st[:, sc + 4:sc + 5],
                                             AF.Ln, bias=cst[:, 0:1])
                                return s.activation(st[:, sc + 6:sc + 7],
                                                    st[:, sc + 5:sc + 6], AF.Exp, scale=-0.5)
                            e_rs = p.emit("scalar", rstd, inc="act")
                            p.wait("vector", e_rs)
                            e_ln = p.emit("vector", lambda o, par=par, sc=sc: o.tensor_scalar(
                                alnx[:], xin[par][:], st[:, sc + 2:sc + 3],
                                st[:, sc + 6:sc + 7], ALU.subtract, ALU.mult), inc="dve")
                            r_lnx.set_ready(e_ln)
                            r_xin[par].add_consumer(e_ln)

                            r_stA.begin_write(p, "tensor")
                            p.wait("tensor", e_ln)

                            def tp_ln(t_):
                                last = None
                                for c in range(8):
                                    last = t_.transpose(stA_ps[:, c * 128:(c + 1) * 128],
                                                        alnx[:, c * 128:(c + 1) * 128],
                                                        ident_b[:])
                                return last
                            e_tp = p.emit("tensor", tp_ln, inc="pe")
                            r_stA.set_ready(e_tp)
                            r_lnx.add_consumer(e_tp)

                            r_lxT.begin_write(p, "scalar")
                            p.wait("scalar", e_tp)
                            e_lxT = p.emit("scalar", lambda o: o.copy(
                                alxT[:], stA_ps[:, :]), inc="act")
                            r_lxT.set_ready(e_lxT)
                            r_stA.add_consumer(e_lxT)

                            p.wait("tensor", e_lxT)
                            p.wait("tensor", e_wqkv)
                            r_qkv.begin_write(p, "tensor")

                            def qkv_mm(t_):
                                last = None
                                for g in range(6):
                                    for c in range(8):
                                        last = t_.matmul(
                                            qkv_ps[:, g * 512:(g + 1) * 512],
                                            alxT[:, c * 128:(c + 1) * 128],
                                            wqkv_b[:, c * 3 * D + g * 512:
                                                   c * 3 * D + (g + 1) * 512],
                                            start=(c == 0), stop=(c == 7))
                                return last
                            e_qkv = p.emit("tensor", qkv_mm, inc="pe")
                            r_qkv.set_ready(e_qkv)
                            r_lxT.add_consumer(e_qkv)

                            p.wait("vector", e_qkv)
                            r_qr.begin_write(p, "vector")
                            r_kr.begin_write(p, "vector")
                            e_cpq = p.gemit("vector", lambda v: v.tensor_copy(
                                qr_f[:], qkv_ps[:, 0:1024]))
                            p.wait("scalar", e_qkv)
                            e_cpk = p.emit("scalar", lambda o: o.copy(
                                kr_f[:], qkv_ps[:, 1024:2048]), inc="act")
                            e_cp = e_cpk
                            p.wait("gpsimd", e_cpk)
                            e_ksq = p.gemit("gpsimd", lambda g: (
                                g.tensor_tensor(tk1[:], kr_f[:, 0:512], kr_f[:, 0:512],
                                                ALU.mult),
                                g.tensor_tensor(tk2[:], kr_f[:, 512:1024],
                                                kr_f[:, 512:1024], ALU.mult))[-1])
                            p.gemit("vector", lambda v: (
                                v.tensor_tensor(tq1[:], qr_f[:, 0:512], qr_f[:, 0:512],
                                                ALU.mult),
                                v.tensor_tensor(tq2[:], qr_f[:, 512:1024],
                                                qr_f[:, 512:1024], ALU.mult))[-1])
                            p.wait("vector", e_ksq)

                            def sq_red(v):
                                v.reduce_sum(st[:, 16:24],
                                             tq1[:].rearrange("p (a b) -> p a b", b=64),
                                             AX.X)
                                v.reduce_sum(st[:, 24:32],
                                             tq2[:].rearrange("p (a b) -> p a b", b=64),
                                             AX.X)
                                v.reduce_sum(st[:, 32:40],
                                             tk1[:].rearrange("p (a b) -> p a b", b=64),
                                             AX.X)
                                return v.reduce_sum(
                                    st[:, 40:48],
                                    tk2[:].rearrange("p (a b) -> p a b", b=64), AX.X)
                            e_sqk = p.gemit("vector", sq_red)
                            r_qkv.add_consumer(e_cpq)
                            r_qkv.add_consumer(e_cpk)

                            p.wait("scalar", e_sqk)

                            def rms_rstd(s):
                                s.activation(st[:, 48:64], st[:, 16:32], AF.Ln,
                                             bias=cst[:, 1:2], scale=1.0 / HD)
                                s.activation(st[:, 48:64], st[:, 48:64], AF.Exp, scale=-0.5)
                                s.activation(st[:, 16:32], st[:, 32:48], AF.Ln,
                                             bias=cst[:, 1:2], scale=1.0 / HD)
                                return s.activation(st[:, 16:32], st[:, 16:32], AF.Exp,
                                                    scale=-0.5)
                            e_rms = p.emit("scalar", rms_rstd, inc="act")

                            p.wait("vector", e_rms)

                            def evq(v, t=t):
                                v.tensor_tensor(
                                    qr_f[:].rearrange("p (a b) -> p a b", b=64),
                                    qr_f[:].rearrange("p (a b) -> p a b", b=64),
                                    bass.AP(st[:].tensor, st[:].offset + 48,
                                            [list(st[:].ap[0]), [1, 16], [0, 64]]),
                                    ALU.mult)
                                return v.tensor_tensor(
                                    kr_f[:].rearrange("p (a b) -> p a b", b=64),
                                    kr_f[:].rearrange("p (a b) -> p a b", b=64),
                                    bass.AP(st[:].tensor, st[:].offset + 16,
                                            [list(st[:].ap[0]), [1, 16], [0, 64]]),
                                    ALU.mult)
                            e_vv = p.emit("scalar", lambda o, t=t: o.copy(
                                v_b[:, t * D:(t + 1) * D], qkv_ps[:, 2048:3072]),
                                inc="act")
                            e_ev = p.emit("vector", evq, inc="dve")
                            r_qr.set_ready(e_ev)
                            r_kr.set_ready(e_ev)
                            r_qkv.add_consumer(e_vv)

                            r_qrope.begin_write(p, "vector")

                            def rope_q(v, t=t):
                                A = head_bcast(ropeq[:], 0 + t * 32, 16, 32)
                                B = head_bcast(ropeq[:], RS + t * 32, 16, 32)
                                C = head_bcast(ropeq[:], 2 * RS + t * 32, 16, 32)
                                Dd = head_bcast(ropeq[:], 3 * RS + t * 32, 16, 32)
                                q1 = head_view(qr_f[:], 0, 16, 32, 64)
                                q2 = head_view(qr_f[:], 32, 16, 32, 64)
                                o1 = head_view(qrope[:], 0, 16, 32, 64)
                                o2 = head_view(qrope[:], 32, 16, 32, 64)
                                t1v = tq1[:].rearrange("p (a b) -> p a b", b=32)
                                t2v = tq2[:].rearrange("p (a b) -> p a b", b=32)
                                v.tensor_tensor(t1v, q1, A, ALU.mult)
                                v.tensor_tensor(t2v, q2, B, ALU.mult)
                                v.tensor_tensor(o1, t1v, t2v, ALU.subtract)
                                v.tensor_tensor(t1v, q1, C, ALU.mult)
                                v.tensor_tensor(t2v, q2, Dd, ALU.mult)
                                return v.tensor_tensor(o2, t1v, t2v, ALU.add)
                            e_rq = p.emit("vector", rope_q, inc="dve")
                            r_qrope.set_ready(e_rq)
                            r_qr.add_consumer(e_rq)

                            r_krope.begin_write(p, "gpsimd")
                            p.wait("gpsimd", e_ev)

                            def rope_k(g, t=t):
                                A = head_bcast(ropek[:], 0 + t * 32, 16, 32)
                                B = head_bcast(ropek[:], RS + t * 32, 16, 32)
                                C = head_bcast(ropek[:], 2 * RS + t * 32, 16, 32)
                                Dd = head_bcast(ropek[:], 3 * RS + t * 32, 16, 32)
                                k1 = head_view(kr_f[:], 0, 16, 32, 64)
                                k2 = head_view(kr_f[:], 32, 16, 32, 64)
                                o1 = head_view(krope[:], 0, 16, 32, 64)
                                o2 = head_view(krope[:], 32, 16, 32, 64)
                                t1v = tk1[:].rearrange("p (a b) -> p a b", b=32)
                                t2v = tk2[:].rearrange("p (a b) -> p a b", b=32)
                                g.tensor_tensor(t1v, k1, A, ALU.mult)
                                g.tensor_tensor(t2v, k2, B, ALU.mult)
                                g.tensor_tensor(o1, t1v, t2v, ALU.subtract)
                                g.tensor_tensor(t1v, k1, C, ALU.mult)
                                g.tensor_tensor(t2v, k2, Dd, ALU.mult)
                                return g.tensor_tensor(o2, t1v, t2v, ALU.add)
                            e_rk = p.emit("gpsimd", rope_k, inc="pool")
                            r_krope.set_ready(e_rk)
                            r_kr.add_consumer(e_rk)

                            r_stA.begin_write(p, "tensor")
                            p.wait("tensor", e_rq)

                            def tp_q(t_):
                                last = None
                                for hh in range(16):
                                    po = 64 * (hh % 2)
                                    pc = hh // 2
                                    last = t_.transpose(
                                        stA_ps[po:po + 64, pc * 128:(pc + 1) * 128],
                                        qrope[:, hh * 64:(hh + 1) * 64], ident_b[:])
                                return last
                            e_tq = p.emit("tensor", tp_q, inc="pe")
                            r_stA.set_ready(e_tq)
                            r_qrope.add_consumer(e_tq)

                            r_stB.begin_write(p, "tensor")
                            p.wait("tensor", e_rk)

                            def tp_k(t_):
                                last = None
                                for hh in range(16):
                                    po = 64 * (hh % 2)
                                    pc = hh // 2
                                    last = t_.transpose(
                                        stB_ps[po:po + 64, pc * 128:(pc + 1) * 128],
                                        krope[:, hh * 64:(hh + 1) * 64], ident_b[:])
                                return last
                            e_tk = p.emit("tensor", tp_k, inc="pe")
                            r_stB.set_ready(e_tk)
                            r_krope.add_consumer(e_tk)

                            p.wait("vector", e_tq)

                            p.wait("scalar", e_tq)

                            def evq2a(o, t=t):
                                iA = stA_ps[:, :].rearrange("p (a b) -> p a b", b=128)
                                oA = bass.AP(qT_b[:].tensor, qT_b[:].offset + t * 128,
                                             [list(qT_b[:].ap[0]), [T, 8], [1, 128]])
                                return o.copy(oA, iA)
                            e_ev2 = p.emit("scalar", evq2a, inc="act")
                            p.wait("scalar", e_tk)

                            def evq2b(o, t=t):
                                iB = stB_ps[:, :].rearrange("p (a b) -> p a b", b=128)
                                oB = bass.AP(kT_b[:].tensor, kT_b[:].offset + t * 128,
                                             [list(kT_b[:].ap[0]), [T, 8], [1, 128]])
                                return o.copy(oB, iB)
                            e_ev2b = p.emit("scalar", evq2b, inc="act")
                            r_stA.add_consumer(e_ev2)
                            r_stB.add_consumer(e_ev2b)
                        p.flush_block("p5")

                if "P6" in phases:
                    with contextlib.ExitStack() as ps6:
                        wo_b = ps6.enter_context(nc.sbuf_tensor("c_wo", [128, 8 * D], BF16))
                        Pb0 = ps6.enter_context(nc.sbuf_tensor("c_P0", [128, T], BF16))
                        Pb1 = ps6.enter_context(nc.sbuf_tensor("c_P1", [128, T], BF16))
                        Pn0 = ps6.enter_context(nc.sbuf_tensor("c_Pn0", [128, T], BF16))
                        Pn1 = ps6.enter_context(nc.sbuf_tensor("c_Pn1", [128, T], BF16))
                        PT0 = ps6.enter_context(nc.sbuf_tensor("c_PT0", [128, T], BF16))
                        PT1 = ps6.enter_context(nc.sbuf_tensor("c_PT1", [128, T], BF16))
                        ot_sb = ps6.enter_context(nc.sbuf_tensor("c_ot", [128, 1024], BF16))
                        cxr0 = ps6.enter_context(nc.sbuf_tensor("c_xr0", [128, D], F32))
                        cxr1 = ps6.enter_context(nc.sbuf_tensor("c_xr1", [128, D], F32))
                        cxo0 = ps6.enter_context(nc.sbuf_tensor("c_xo0", [128, D], F32))
                        cxo1 = ps6.enter_context(nc.sbuf_tensor("c_xo1", [128, D], F32))
                        S_ps = ps6.enter_context(nc.psum_tensor("c_S", [128, min(T, 2048)], F32))
                        PT_ps = ps6.enter_context(nc.psum_tensor("c_PT", [128, 1024], BF16))
                        OT_ps = ps6.enter_context(nc.psum_tensor("c_OT", [128, 1024], F32))
                        WO_ps = ps6.enter_context(nc.psum_tensor("c_WO", [128, 512], F32))
                        _ = None
                        Pb2 = ps6.enter_context(nc.sbuf_tensor("c_P2", [128, T], BF16))
                        Pb3 = ps6.enter_context(nc.sbuf_tensor("c_P3", [128, T], BF16))
                        Pn2 = ps6.enter_context(nc.sbuf_tensor("c_Pn2", [128, T], BF16))
                        Pn3 = ps6.enter_context(nc.sbuf_tensor("c_Pn3", [128, T], BF16))
                        PT2 = ps6.enter_context(nc.sbuf_tensor("c_PT2", [128, T], BF16))
                        PT3 = ps6.enter_context(nc.sbuf_tensor("c_PT3", [128, T], BF16))
                        Pb = [Pb0, Pb1, Pb2, Pb3]
                        Pn = [Pn0, Pn1, Pn2, Pn3]
                        PT = [PT0, PT1, PT2, PT3]
                        xr = [cxr0, cxr1]
                        xo = [cxo0, cxo1]

                        if "wo_c" in evs:
                            p.wait("sync", evs["wo_c"])
                            for c in range(8):
                                p.emit("sync", lambda o, c=c: o.dma_start(
                                    out=wo_b[:, c * D:(c + 1) * D],
                                    in_=wo_c[c * 128:(c + 1) * 128, :]),
                                    inc="dS", n=16)
                            e_wo = ("dS", p.cnt["dS"])
                        else:
                            for c in range(8):
                                p.emit("gpsimd", lambda o, c=c: o.dma_start(
                                    out=wo_b[:, c * D:(c + 1) * D],
                                    in_=woT[c * 128:(c + 1) * 128, :]), inc="dG", n=16)
                            e_wo = ("dG", p.cnt["dG"])

                        r_SH = [Res(), Res()]
                        r_Pb = [Res(), Res(), Res(), Res()]
                        r_Pn = [Res(), Res(), Res(), Res()]
                        r_PTps = Res()
                        r_PTsb = [Res(), Res(), Res(), Res()]
                        r_OT = Res()
                        r_ot = Res()
                        r_WO = Res()
                        r_xr = [Res(), Res()]
                        r_xo = [Res(), Res()]
                        r_linv = [Res(), Res()]
                        pend_wo = [None]

                        for i in range(NT):
                            ip = i % 2
                            ncols = (i + 1) * 128
                            r_xr[ip].begin_write(p, "sync")
                            if "P5" not in phases and "P3" in phases:
                                p.wait("sync", evs[f"f1_store_{i}"])
                            e_lr = p.emit("sync", lambda o, i=i, ip=ip: o.dma_start(
                                out=xr[ip][:], in_=xs[i * 128:(i + 1) * 128, :]),
                                inc="dS", n=16)
                            r_xr[ip].set_ready(e_lr)

                            def stage1(h, pump=None, i=i, ncols=ncols):
                                hp = h % 2
                                bi = h % 4
                                po = 64 * hp
                                pc = h // 2
                                pb = 32 + hp * 16
                                c_nm = pb + 5
                                c_lA, c_lB, c_li = pb + 6, pb + 7, pb + 9
                                c_l = pb + 8
                                colsA = min(ncols, 1024)
                                colsB = ncols - colsA
                                diag_lo = i * 128
                                nch = (ncols + 511) // 512
                                sbase = 1024 * (h % 2) if ncols <= 1024 else 0

                                lqv = qT_b[po:po + 64,
                                           pc * T + i * 128: pc * T + (i + 1) * 128]
                                cm_evs = []
                                e_s_half = {0: None, 1: None}
                                seen_half = set()
                                for ci in range(nch):
                                    c0 = ci * 512
                                    w = min(512, ncols - c0)
                                    half = (sbase + c0) // 1024
                                    if half not in seen_half:
                                        seen_half.add(half)
                                        r_SH[half].begin_write(p, "tensor")
                                    has_diag = c0 <= diag_lo < c0 + w

                                    def smm(t_, c0=c0, w=w, po=po, pc=pc, sbase=sbase,
                                            has_diag=has_diag, dl=diag_lo):
                                        last = t_.matmul(
                                            S_ps[:, sbase + c0:sbase + c0 + w], lqv,
                                            kT_b[po:po + 64, pc * T + c0: pc * T + c0 + w],
                                            start=True, stop=not has_diag)
                                        if has_diag:
                                            last = t_.matmul(
                                                S_ps[:, sbase + dl:sbase + dl + 128],
                                                ident_b[:], mask_b[:],
                                                start=False, stop=True)
                                        return last
                                    e_sc = p.emit("tensor", smm, inc="pe")
                                    e_s_half[half] = e_sc
                                    p.wait("vector", e_sc)
                                    e_cm = p.emit("vector", lambda v, ci=ci, c0=c0, w=w,
                                                  sbase=sbase:
                                                  v.reduce_max(st[:, pb + ci:pb + ci + 1],
                                                               S_ps[:, sbase + c0:
                                                                    sbase + c0 + w],
                                                               AX.X,
                                                               negate=True), inc="dve")
                                    cm_evs.append(e_cm)
                                for half in seen_half:
                                    r_SH[half].set_ready(e_s_half[half])
                                if pump is not None:
                                    pump()
                                p.wait("vector", cm_evs[-1])
                                e_nm = p.emit("vector", lambda v, nch=nch: v.tensor_reduce(
                                    st[:, c_nm:c_nm + 1], st[:, pb:pb + nch], AX.X,
                                    ALU.min), inc="dve")

                                r_Pb[bi].begin_write(p, "scalar")
                                p.wait("scalar", e_nm)
                                dstlA = c_lA if colsB else c_l
                                e_eA = p.emit("scalar", lambda o, bi=bi, colsA=colsA,
                                              c_nm=c_nm, d=dstlA, sbase=sbase:
                                              o.activation(
                                                  Pb[bi][:, 0:colsA],
                                                  S_ps[:, sbase:sbase + colsA],
                                                  AF.Exp, bias=st[:, c_nm:c_nm + 1],
                                                  accum_out=st[:, d:d + 1]), inc="act")
                                r_SH[sbase // 1024].add_consumer(e_eA)
                                e_eB = None
                                if colsB:
                                    e_eB = p.emit("scalar", lambda o, bi=bi, colsB=colsB,
                                                  c_nm=c_nm, d=c_lB: o.activation(
                                                      Pb[bi][:, 1024:1024 + colsB],
                                                      S_ps[:, 1024:1024 + colsB],
                                                      AF.Exp, bias=st[:, c_nm:c_nm + 1],
                                                      accum_out=st[:, d:d + 1]), inc="act")
                                    r_SH[1].add_consumer(e_eB)
                                r_Pb[bi].set_ready(e_eB or e_eA)

                                r_linv[hp].begin_write(p, "vector")
                                p.wait("vector", e_eB or e_eA)
                                if colsB:
                                    p.gemit("vector", lambda v, a=c_lA, bcol=c_lB, d=c_l:
                                            v.tensor_tensor(st[:, d:d + 1], st[:, a:a + 1],
                                                            st[:, bcol:bcol + 1], ALU.add))
                                e_li = p.gemit("vector", lambda v, scol=c_l, d=c_li:
                                               v.reciprocal(st[:, d:d + 1],
                                                            st[:, scol:scol + 1]))
                                r_linv[hp].set_ready(e_li)

                                r_Pn[bi].begin_write(p, "gpsimd")
                                p.wait("gpsimd", e_li)
                                e_pn = p.emit("gpsimd", lambda g, bi=bi, ncols=ncols,
                                              d=c_li: g.tensor_tensor(
                                                  Pn[bi][:, 0:ncols], Pb[bi][:, 0:ncols],
                                                  bcast_free(st[:, d:d + 1], ncols),
                                                  ALU.mult), inc="pool")
                                r_Pn[bi].set_ready(e_pn)
                                r_Pb[bi].add_consumer(e_pn)
                                r_linv[hp].add_consumer(e_pn)
                                if pump is not None:
                                    pump()

                            def stage2pair(hE, i=i, ncols=ncols):
                                nb = i + 1
                                banks = [PT_ps[:, :],
                                         WO_ps[:, :].bitcast(BF16)]
                                bank_res = [r_PTps, r_WO]
                                pv_done = {}
                                for hp in (0, 1):
                                    p.wait("tensor", r_Pn[(hE + hp) % 4].ready)
                                r_OT.begin_write(p, "tensor")
                                e_pv = None
                                for b0 in range(0, nb, 8):
                                    yield
                                    bw = min(8, nb - b0)
                                    ev_evac = {}
                                    for hp in (0, 1):
                                        bi = (hE + hp) % 4
                                        bank_res[hp].begin_write(p, "tensor")

                                        def tp_p(t_, bi=bi, b0=b0, bw=bw, hp=hp):
                                            last = None
                                            for c in range(bw):
                                                last = t_.transpose(
                                                    banks[hp][:, c * 128:(c + 1) * 128],
                                                    Pn[bi][:, (b0 + c) * 128:
                                                            (b0 + c + 1) * 128],
                                                    ident_b[:])
                                            return last
                                        e_t = p.emit("tensor", tp_p, inc="pe")
                                        bank_res[hp].set_ready(e_t)
                                        r_Pn[bi].add_consumer(e_t)
                                        eng = "scalar" if hp == 0 else "vector"
                                        if b0 == 0:
                                            r_PTsb[bi].begin_write(p, eng)
                                        p.wait(eng, e_t)
                                        if hp == 0:
                                            e_e = p.emit("scalar", lambda o, bi=bi, b0=b0,
                                                         bw=bw, hp=hp: o.copy(
                                                             PT[bi][:, b0 * 128:
                                                                    (b0 + bw) * 128],
                                                             banks[hp][:, 0:bw * 128]),
                                                         inc="act")
                                        else:
                                            e_e = p.emit("vector", lambda o, bi=bi, b0=b0,
                                                         bw=bw, hp=hp: o.tensor_copy(
                                                             PT[bi][:, b0 * 128:
                                                                    (b0 + bw) * 128],
                                                             banks[hp][:, 0:bw * 128]),
                                                         inc="dve")
                                        bank_res[hp].add_consumer(e_e)
                                        ev_evac[hp] = e_e
                                        if b0 == 0:
                                            pv_done[hp] = []
                                        pv_done[hp].append(e_e)

                                    p.wait_all("tensor", [ev_evac[0], ev_evac[1]])

                                    def pvb(t_, hE=hE, i=i, b0=b0, bw=bw):
                                        last = None
                                        for b in range(b0, b0 + bw):
                                            for hp in (0, 1):
                                                h = hE + hp
                                                bi = h % 4
                                                po = 64 * hp
                                                off = 512 * hp
                                                last = t_.matmul(
                                                    OT_ps[po:po + 64, off:off + 128],
                                                    v_b[:, b * D + h * HD:
                                                        b * D + (h + 1) * HD],
                                                    PT[bi][:, b * 128:(b + 1) * 128],
                                                    start=(b == 0), stop=(b == i),
                                                    tile_position=(0, po))
                                        return last
                                    e_pv = p.emit("tensor", pvb, inc="pe")
                                    for hp in (0, 1):
                                        r_PTsb[(hE + hp) % 4].add_consumer(e_pv)
                                r_OT.set_ready(e_pv)

                                pc = hE // 2
                                if pc == 0:
                                    r_ot.begin_write(p, "vector")
                                p.wait("vector", e_pv)
                                e_oe = p.emit("vector", lambda o, pc=pc: (
                                    o.tensor_copy(ot_sb[0:64, pc * 128:(pc + 1) * 128],
                                                  OT_ps[0:64, 0:128]),
                                    o.tensor_copy(ot_sb[64:128, pc * 128:(pc + 1) * 128],
                                                  OT_ps[64:128, 512:640]))[-1],
                                    inc="dve")
                                r_OT.add_consumer(e_oe)
                                if pc == 7:
                                    r_ot.set_ready(e_oe)

                            def wo_block(i2):
                                ip2 = i2 % 2
                                p.wait("tensor", r_ot.ready)
                                p.wait("tensor", e_wo)
                                r_xo[ip2].begin_write(p, "vector")
                                e_res = None
                                for g in range(2):
                                    r_WO.begin_write(p, "tensor")

                                    def womm(t_, g=g):
                                        last = None
                                        for c in range(8):
                                            last = t_.matmul(
                                                WO_ps[:, 0:512],
                                                ot_sb[:, c * 128:(c + 1) * 128],
                                                wo_b[:, c * D + g * 512:
                                                     c * D + (g + 1) * 512],
                                                start=(c == 0), stop=(c == 7))
                                        return last
                                    e_wmm = p.emit("tensor", womm, inc="pe")
                                    r_WO.set_ready(e_wmm)
                                    if g == 1:
                                        r_ot.add_consumer(e_wmm)
                                    p.wait("vector", e_wmm)
                                    p.wait("vector", r_xr[ip2].ready)
                                    e_res = p.emit("vector", lambda o, ip2=ip2, g=g:
                                                   o.tensor_tensor(
                                                       xo[ip2][:, g * 512:(g + 1) * 512],
                                                       WO_ps[:, 0:512],
                                                       xr[ip2][:, g * 512:(g + 1) * 512],
                                                       ALU.add), inc="dve")
                                    r_WO.add_consumer(e_res)
                                    r_xr[ip2].add_consumer(e_res)
                                r_xo[ip2].set_ready(e_res)

                                p.wait("scalar", e_res)
                                p6_dst = y_out if terminal == "P6" else xs
                                e_sto = p.emit("scalar", lambda o, i2=i2, ip2=ip2,
                                               d=p6_dst: o.dma_start(
                                                   out=d[i2 * 128:(i2 + 1) * 128, :],
                                                   in_=xo[ip2][:]), inc="dO", n=16)
                                r_xo[ip2].add_consumer(e_sto)
                                evs[f"a_store_{i2}"] = e_sto

                            for pr in range(NH // 2):
                                gen = stage2pair(2 * pr - 2) if pr > 0 else None

                                def pump(g=gen):
                                    if g is not None:
                                        next(g, None)
                                stage1(2 * pr, pump)
                                stage1(2 * pr + 1, pump)
                                if pr == 0 and pend_wo[0] is not None:
                                    wo_block(pend_wo[0])
                                    pend_wo[0] = None
                                if gen is not None:
                                    for _ in gen:
                                        pass
                            for _ in stage2pair(NH - 2):
                                pass
                            pend_wo[0] = i
                        wo_block(pend_wo[0])
                        p.flush_block("p6")

        if "P7" in phases:
            with (
                nc.sbuf_tensor("wqfc2", [128, 8 * HF], BF16) as wqfc2,
                nc.sbuf_tensor("wqpj2", [128, 32 * D], BF16) as wqpj2,
            ):
                if "P1" in phases:
                    p.wait("sync", evs["wq_stored"])
                    p.wait("scalar", evs["wq_stored"])
                for c in range(8):
                    q, qs = ("sync", "dS") if c % 2 == 0 else ("scalar", "dO")
                    p.emit(q, lambda o, c=c: o.dma_start(
                        out=wqfc2[:, c * HF:(c + 1) * HF],
                        in_=wqfc_s[c * 128:(c + 1) * 128, :]), inc=qs, n=16)
                for ci, c in enumerate(range(0, 32, 2)):
                    srcv = pair_view(wqpj_s, c, D)
                    q, qs = ("sync", "dS") if ci % 2 == 0 else ("scalar", "dO")
                    p.emit(q, lambda o, s=srcv, c=c: o.dma_start(
                        out=wqpj2[:, c * D:(c + 2) * D].rearrange("p (a n) -> p a n", a=2),
                        in_=s), inc=qs, n=16)
                p.wait("tensor", ("dS", p.cnt["dS"]))
                p.wait("tensor", ("dO", p.cnt["dO"]))
                src_evs = ([evs.get(f"a_store_{t}") for t in range(NT)]
                           if "P6" in phases else
                           ([evs.get(f"f1_store_{t}") for t in range(NT)]
                            if "P3" in phases else None))
                emit_ffn("f2", xs, y_out, wqfc2, wqpj2, src_wait_evs=src_evs)

        p.wait("sync", ("dO", p.cnt["dO"]))
        p.flush_block("fin")

    return nc


_NC_CACHE = {}


def _get_nc():
    if "full" not in _NC_CACHE:
        _NC_CACHE["full"] = build_program()
    return _NC_CACHE["full"]


def kernel(L0_attn_x, L0_attn_wqkv, L0_attn_wo, L0_attn_q_norm_w, L0_attn_k_norm_w,
           L0_ffn_W_fc, L0_ffn_ln_w, L0_ffn_ln_b, L0_ffn_W_proj):
    x = np.ascontiguousarray(np.asarray(L0_attn_x, dtype=np.float32))
    wqkvT = np.ascontiguousarray(np.asarray(L0_attn_wqkv, dtype=np.float32).T)
    woT = np.ascontiguousarray(np.asarray(L0_attn_wo, dtype=np.float32).T)
    wfcT = np.ascontiguousarray(np.asarray(L0_ffn_W_fc, dtype=np.float32).T)
    wpjT = np.ascontiguousarray(np.asarray(L0_ffn_W_proj, dtype=np.float32).T)
    qnw = np.ascontiguousarray(np.asarray(L0_attn_q_norm_w, dtype=np.float32).reshape(1, HD))
    knw = np.ascontiguousarray(np.asarray(L0_attn_k_norm_w, dtype=np.float32).reshape(1, HD))
    lnw = np.ascontiguousarray(np.asarray(L0_ffn_ln_w, dtype=np.float32).reshape(32, 128))
    lnb = np.ascontiguousarray(np.asarray(L0_ffn_ln_b, dtype=np.float32).reshape(32, 128))

    nc = _get_nc()
    in_maps = []
    for b in range(NCORES):
        in_maps.append({
            "x": np.ascontiguousarray(x[b]),
            "wqkvT": wqkvT, "woT": woT, "wfcT": wfcT, "wpjT": wpjT,
            "qnw": qnw, "knw": knw, "lnw": lnw, "lnb": lnb,
        })
    res = run_bass_kernel_spmd(nc, in_maps, core_ids=list(range(NCORES)))
    return np.stack([np.asarray(r["y"], dtype=np.float32) for r in res.results], axis=0)



# version 30
# speedup vs baseline: 1.7835x; 1.0079x over previous
import math
import contextlib
import numpy as np

import concourse.bass as bass
import concourse.mybir as mybir
from concourse.bass_utils import run_bass_kernel_spmd

F32 = mybir.dt.float32
BF16 = mybir.dt.bfloat16
I32 = mybir.dt.int32
AF = mybir.ActivationFunctionType
ALU = mybir.AluOpType
AX = mybir.AxisListType

T = 2048
D = 1024
HF = 4096
NH = 16
HD = 64
NT = T // 128
RS = NT * 32
NCORES = 8


def set_dims(nt):
    global T, NT, RS
    NT = nt
    T = nt * 128
    RS = NT * 32

ROPE_THETA = 10000.0
TWO_PI = 2 * math.pi

ENGINES = ("tensor", "vector", "scalar", "gpsimd", "sync")
DEBUG = False


def bcast_free(ap, n):
    return bass.AP(ap.tensor, ap.offset, [list(ap.ap[0]), [0, n]])


def head_view(ap, col_off, nh, w, stride):
    return bass.AP(ap.tensor, ap.offset + col_off, [list(ap.ap[0]), [stride, nh], [1, w]])


def head_bcast(ap, col_off, nh, w):
    return bass.AP(ap.tensor, ap.offset + col_off, [list(ap.ap[0]), [0, nh], [1, w]])


def pair_view(dram_ap, c, ncols):
    off = c * 128 * ncols
    return bass.AP(dram_ap.tensor, dram_ap.offset + off,
                   [[ncols, 128], [128 * ncols, 2], [1, ncols]])


class Prog:
    def __init__(self, nc):
        self.nc = nc
        self.cmds = {e: [] for e in ENGINES}
        self.sems = {}
        self.cnt = {}
        self.hw = {e: {} for e in ENGINES}
        self.blocks = 0

    def sem(self, name):
        h = self.nc.semaphore(name).__enter__()
        self.sems[name] = h
        self.cnt[name] = 0
        return h

    def emit(self, eng, fn, inc=None, n=1):
        ev = None
        if inc is not None:
            self.cnt[inc] += n
            ev = (inc, self.cnt[inc])
            sem = self.sems[inc]

            def cl(obj, fn=fn, sem=sem, n=n):
                ins = fn(obj)
                assert ins is not None, "emitted fn with inc must return an instruction"
                ins.then_inc(sem, n)
        else:
            def cl(obj, fn=fn):
                fn(obj)
        self.cmds[eng].append(cl)
        return ev

    ENG_SEM = {"tensor": "pe", "vector": "dve", "scalar": "act", "gpsimd": "pool"}

    def gemit(self, eng, fn, n=1):
        ev = self.emit(eng, fn, inc=self.ENG_SEM[eng], n=n)
        self.wait(eng, ev)
        return ev

    def wait(self, eng, ev):
        if ev is None:
            return
        s, v = ev
        assert v <= self.cnt[s], f"wait on future event {s}={v} > {self.cnt[s]}"
        if self.hw[eng].get(s, 0) >= v:
            return
        self.hw[eng][s] = v
        sem = self.sems[s]
        self.cmds[eng].append(lambda obj, sem=sem, v=v: obj.wait_ge(sem, v))

    def wait_all(self, eng, evs):
        best = {}
        for ev in evs:
            if ev is None:
                continue
            s, v = ev
            if best.get(s, 0) < v:
                best[s] = v
        for s, v in best.items():
            self.wait(eng, (s, v))

    def flush_block(self, name):
        cmds = self.cmds
        self.cmds = {e: [] for e in ENGINES}
        nonempty = [e for e in ENGINES if cmds[e]]
        if not nonempty:
            return
        with self.nc.Block(f"b{self.blocks}_{name}") as blk:
            for e in nonempty:
                def replay(obj, lst=cmds[e]):
                    for c in lst:
                        c(obj)
                getattr(blk, e)(replay)
        self.blocks += 1
        for s in ("pe", "dve", "act", "pool"):
            if s in self.cnt:
                for e in ENGINES:
                    self.hw[e][s] = self.cnt[s]


class Res:
    __slots__ = ("ready", "cons")

    def __init__(self):
        self.ready = None
        self.cons = []

    def begin_write(self, p, eng):
        p.wait_all(eng, self.cons)

    def set_ready(self, ev):
        self.ready = ev
        self.cons = []

    def add_consumer(self, ev):
        if ev is not None:
            self.cons.append(ev)


def build_program(phases=("P0", "P1", "P3", "P5", "P6", "P7")):
    terminal = "P7" if "P7" in phases else ("P6" if "P6" in phases else "P3")
    nc = bass.Bass("TRN2", target_bir_lowering=False)
    dbg_tensors = {}

    def dbg_decl(name, shape, dt):
        if not DEBUG:
            return None
        dbg_tensors[name] = nc.dram_tensor(f"dbg_{name}", shape, dt,
                                           kind="ExternalOutput").ap()
        return dbg_tensors[name]

    x_in = nc.dram_tensor("x", [T, D], F32, kind="ExternalInput").ap()
    wqkvT = nc.dram_tensor("wqkvT", [D, 3 * D], F32, kind="ExternalInput").ap()
    woT = nc.dram_tensor("woT", [D, D], F32, kind="ExternalInput").ap()
    wfcT = nc.dram_tensor("wfcT", [D, HF], F32, kind="ExternalInput").ap()
    wpjT = nc.dram_tensor("wpjT", [HF, D], F32, kind="ExternalInput").ap()
    qnw = nc.dram_tensor("qnw", [1, HD], F32, kind="ExternalInput").ap()
    knw = nc.dram_tensor("knw", [1, HD], F32, kind="ExternalInput").ap()
    lnw = nc.dram_tensor("lnw", [32, 128], F32, kind="ExternalInput").ap()
    lnb = nc.dram_tensor("lnb", [32, 128], F32, kind="ExternalInput").ap()
    y_out = nc.dram_tensor("y", [T, D], F32, kind="ExternalOutput").ap()

    xs = nc.dram_tensor("xs", [T, D], F32, kind="Internal").ap()
    wqfc_s = nc.dram_tensor("wqfc_s", [D, HF], BF16, kind="Internal").ap()
    wqpj_s = nc.dram_tensor("wqpj_s", [HF, D], BF16, kind="Internal").ap()
    wo_c = nc.dram_tensor("wo_c", [D, D], BF16, kind="Internal").ap()

    es = contextlib.ExitStack()
    with es:
        p = Prog(nc)
        for s in ("dS", "dG", "dO", "pe", "dve", "act", "pool"):
            p.sem(s)

        def dbg_dump(name, src_ap, shape, dt, wait_evs=(), res=None):
            if not DEBUG:
                return
            d = dbg_decl(name, shape, dt)
            p.wait_all("sync", [e for e in wait_evs if e is not None])
            ev = p.emit("sync", lambda o, d=d, s=src_ap: o.dma_start(out=d, in_=s),
                        inc="dO", n=16)
            if res is not None:
                res.add_consumer(ev)

        def sb(name, shape, dt):
            return es.enter_context(nc.sbuf_tensor(name, shape, dt))

        ident_b = sb("ident_b", [128, 128], BF16)
        ident_f = sb("ident_f", [128, 128], F32)
        mask_b = sb("mask_b", [128, 128], BF16)
        ones_row_b = sb("ones_row_b", [1, 128], BF16)
        ones_row_f = sb("ones_row_f", [1, 128], F32)
        ones_col_f = sb("ones_col_f", [128, 1], F32)
        ones_col_b = sb("ones_col_b", [128, 1], BF16)
        ropeq = sb("ropeq", [128, 4 * RS], F32)
        ropek = sb("ropek", [128, 4 * RS], F32)
        halfb = sb("halfb", [128, 1024], F32)
        halfc = sb("halfc", [128, 1024], F32)
        lnwT = sb("lnwT", [128, 32], F32)
        lnbT = sb("lnbT", [128, 32], BF16)
        thr2 = sb("thr2", [128, 2], F32)
        st = sb("st", [128, 64], F32)
        gst = sb("gst", [128, 8], F32)
        cst = sb("cst", [128, 2], F32)

        def init_cst(g):
            g.memset(st[:], 0.0)
            g.memset(gst[:], 0.0)
            g.memset(cst[:, 0:1], 1e-5)
            return g.memset(cst[:, 1:2], 1e-6)
        p.emit("gpsimd", init_cst, inc="pool")
        p.flush_block("cst")

        evs = {}

        if "P0" in phases:
            with contextlib.ExitStack() as ps0:
                qn_sb = ps0.enter_context(nc.sbuf_tensor("qn_sb", [1, HD], F32))
                kn_sb = ps0.enter_context(nc.sbuf_tensor("kn_sb", [1, HD], F32))
                lnw_r = ps0.enter_context(nc.sbuf_tensor("lnw_r", [32, 128], F32))
                lnb_r = ps0.enter_context(nc.sbuf_tensor("lnb_r", [32, 128], BF16))
                wbc = ps0.enter_context(nc.sbuf_tensor("wbc", [128, 128], F32))
                ang = ps0.enter_context(nc.sbuf_tensor("ang", [128, RS], F32))
                ang2 = ps0.enter_context(nc.sbuf_tensor("ang2", [128, RS], F32))
                rt1 = ps0.enter_context(nc.sbuf_tensor("rt1", [128, RS], F32))
                rk32 = ps0.enter_context(nc.sbuf_tensor("rk32", [128, RS], I32))
                rkf = ps0.enter_context(nc.sbuf_tensor("rkf", [128, RS], F32))
                sin_t = ps0.enter_context(nc.sbuf_tensor("sin_t", [128, RS], F32))
                cos_t = ps0.enter_context(nc.sbuf_tensor("cos_t", [128, RS], F32))
                invf = ps0.enter_context(nc.sbuf_tensor("invf", [128, 32], F32))
                pos = ps0.enter_context(nc.sbuf_tensor("pos", [128, NT], F32))
                p0_ps = ps0.enter_context(nc.psum_tensor("p0_ps", [128, 1024], F32))
                _ = None
                p.emit("sync", lambda o: o.dma_start(out=qn_sb[:], in_=qnw), inc="dS", n=16)
                p.emit("sync", lambda o: o.dma_start(out=kn_sb[:], in_=knw), inc="dS", n=16)
                e_lnw = p.emit("sync", lambda o: o.dma_start(out=lnw_r[:], in_=lnw), inc="dS", n=16)
                e_qkn = e_lnw
                e_lnb = p.emit("gpsimd", lambda o: o.dma_start(out=lnb_r[:], in_=lnb), inc="dG", n=16)

                def g0a(g):
                    g.memset(ones_row_b[:], 1.0)
                    g.memset(ones_row_f[:], 1.0)
                    g.memset(ones_col_f[:], 1.0)
                    g.memset(ones_col_b[:], 1.0)
                    g.memset(ident_b[:], 1.0)
                    g.memset(ident_f[:], 1.0)
                    return g.memset(mask_b[:], 0.0)
                p.gemit("gpsimd", g0a)

                def g0b(g):
                    g.affine_select(mask_b[:], mask_b[:], [[-1, 128]], ALU.is_ge, -1e9,
                                    base=0, channel_multiplier=1)
                    g.affine_select(ident_b[:], ident_b[:], [[-1, 128]], ALU.is_equal, 0.0,
                                    base=0, channel_multiplier=1)
                    g.affine_select(ident_f[:], ident_f[:], [[-1, 128]], ALU.is_equal, 0.0,
                                    base=0, channel_multiplier=1)
                    g.iota(invf[:], [[1, 32]], base=0, channel_multiplier=0,
                           allow_small_or_imprecise_dtypes=True)
                    return g.iota(pos[:], [[128, NT]], base=0, channel_multiplier=1,
                                  allow_small_or_imprecise_dtypes=True)
                e_g0 = p.gemit("gpsimd", g0b)

                p.wait("scalar", e_g0)
                e_invf = p.emit("scalar", lambda o: o.activation(
                    invf[:], invf[:], AF.Exp, bias=0.0,
                    scale=-2.0 * math.log(ROPE_THETA) / HD), inc="act")

                p.wait("vector", e_invf)

                def mk_ang(v):
                    in0 = bass.AP(pos[:].tensor, pos[:].offset,
                                  [list(pos[:].ap[0]), [1, NT], [0, 32]])
                    in1 = bass.AP(invf[:].tensor, invf[:].offset,
                                  [list(invf[:].ap[0]), [0, NT], [1, 32]])
                    v.tensor_tensor(ang[:].rearrange("p (a b) -> p a b", b=32),
                                    in0, in1, ALU.mult)
                    v.tensor_scalar(rt1[:], ang[:], 1.0 / TWO_PI, None, ALU.mult)
                    v.tensor_copy(rk32[:], rt1[:])
                    v.tensor_copy(rkf[:], rk32[:])
                    v.scalar_tensor_tensor(rt1[:], rkf[:], -TWO_PI, ang[:], ALU.mult, ALU.add)
                    v.tensor_scalar(rkf[:], rt1[:], math.pi, None, ALU.is_gt)
                    v.scalar_tensor_tensor(rt1[:], rkf[:], -TWO_PI, rt1[:], ALU.mult, ALU.add)
                    v.tensor_scalar(ang2[:], ang[:], math.pi / 2, None, ALU.add)
                    v.tensor_scalar(ang[:], ang2[:], 1.0 / TWO_PI, None, ALU.mult)
                    v.tensor_copy(rk32[:], ang[:])
                    v.tensor_copy(rkf[:], rk32[:])
                    v.scalar_tensor_tensor(ang2[:], rkf[:], -TWO_PI, ang2[:],
                                           ALU.mult, ALU.add)
                    v.tensor_scalar(rkf[:], ang2[:], math.pi, None, ALU.is_gt)
                    return v.scalar_tensor_tensor(ang2[:], rkf[:], -TWO_PI, ang2[:],
                                                  ALU.mult, ALU.add)
                e_rr = p.emit("vector", mk_ang, inc="dve")

                p.wait("scalar", e_rr)
                p.emit("scalar", lambda o: o.activation(sin_t[:], rt1[:], AF.Sin))
                e_sc = p.emit("scalar", lambda o: o.activation(cos_t[:], ang2[:], AF.Sin),
                              inc="act")

                p.wait("tensor", e_qkn)
                p.wait("tensor", e_lnw)
                p.wait("tensor", e_g0)

                def pe0(t_):
                    t_.matmul(p0_ps[:, 0:64], ones_row_f[0:1, :], qn_sb[0:1, :],
                              start=True, stop=True)
                    t_.matmul(p0_ps[:, 64:128], ones_row_f[0:1, :], kn_sb[0:1, :],
                              start=True, stop=True)
                    return t_.transpose(p0_ps[:, 512:544], lnw_r[:, :], ident_f[0:32, 0:32])
                e_pe0 = p.emit("tensor", pe0, inc="pe")
                p.wait("vector", e_pe0)

                def ev0(v):
                    v.tensor_copy(wbc[:], p0_ps[:, 0:128])
                    return v.tensor_copy(lnwT[:], p0_ps[:, 512:544])
                e_ev0 = p.emit("vector", ev0, inc="dve")

                p.wait("vector", e_sc)

                def tables(v):
                    last = None
                    for dst, wofs, src, scale in (
                        (ropeq[:, 0:RS], 0, cos_t, 0.125),
                        (ropeq[:, RS:2 * RS], 32, sin_t, 0.125),
                        (ropeq[:, 2 * RS:3 * RS], 0, sin_t, 0.125),
                        (ropeq[:, 3 * RS:4 * RS], 32, cos_t, 0.125),
                        (ropek[:, 0:RS], 64, cos_t, 1.0),
                        (ropek[:, RS:2 * RS], 96, sin_t, 1.0),
                        (ropek[:, 2 * RS:3 * RS], 64, sin_t, 1.0),
                        (ropek[:, 3 * RS:4 * RS], 96, cos_t, 1.0),
                    ):
                        last = v.scalar_tensor_tensor(
                            dst.rearrange("p (a b) -> p a b", b=32),
                            src[:].rearrange("p (a b) -> p a b", b=32),
                            scale, head_bcast(wbc[:], wofs, NT, 32), ALU.mult, ALU.mult)
                    return last
                p.emit("vector", tables)
                p.flush_block("p0a")

                with nc.psum_tensor("p0b_ps", [128, 32], BF16) as p0b_ps:
                    p.wait("tensor", e_lnb)
                    e_tb = p.emit("tensor", lambda o: o.transpose(
                        p0b_ps[:, :], lnb_r[:, :], ident_b[0:32, 0:32]), inc="pe")
                    p.wait("vector", e_tb)
                    e_lnbT = p.emit("vector", lambda o: o.tensor_copy(lnbT[:], p0b_ps[:, :]),
                                    inc="dve")
                    dbg_dump("lnwT", lnwT[:], [128, 32], F32, [e_ev0])
                    dbg_dump("lnbT", lnbT[:], [128, 32], BF16, [e_lnbT])
                    p.flush_block("p0b")

        def emit_ffn(tag, src_ap, dst_ap, wqfc, wqpj, src_wait_evs=None,
                     xin_pre=None):
            NSUP = NT // 2
            NCH = HF // 128
            with contextlib.ExitStack() as psf:
                if xin_pre is None:
                    xin0 = psf.enter_context(nc.sbuf_tensor(f"{tag}_xin0", [128, 2048], F32))
                    r_x0 = Res()
                else:
                    xin0, r_x0 = xin_pre
                xin1 = psf.enter_context(nc.sbuf_tensor(f"{tag}_xin1", [128, 2048], F32))
                r_xin = [r_x0, Res()]
                lnx = psf.enter_context(nc.sbuf_tensor(f"{tag}_lnx", [128, 2048], BF16))
                lxT0 = psf.enter_context(nc.sbuf_tensor(f"{tag}_lxT0", [128, 2048], BF16))
                lxT1 = psf.enter_context(nc.sbuf_tensor(f"{tag}_lxT1", [128, 2048], BF16))
                cmb = psf.enter_context(nc.sbuf_tensor(f"{tag}_cmb", [128, 2048], BF16))
                tmpv = psf.enter_context(nc.sbuf_tensor(f"{tag}_tmp", [128, 2048], F32))
                rowb = psf.enter_context(nc.sbuf_tensor(f"{tag}_row", [1, 512], F32))
                stg_ps = psf.enter_context(nc.psum_tensor(f"{tag}_stg", [128, 1024], BF16))
                hps0 = psf.enter_context(nc.psum_tensor(f"{tag}_h0", [128, 512], F32))
                hps1 = psf.enter_context(nc.psum_tensor(f"{tag}_h1", [128, 512], F32))
                o2a_ps = psf.enter_context(nc.psum_tensor(f"{tag}_o2a", [128, 1024], F32))
                o2b_ps = psf.enter_context(nc.psum_tensor(f"{tag}_o2b", [128, 1024], F32))
                sm_ps = psf.enter_context(nc.psum_tensor(f"{tag}_sm", [128, 512], F32))
                xin = [xin0, xin1]
                lxT = [lxT0, lxT1]
                hps = [hps0, hps1]
                o2 = [o2a_ps, o2b_ps]
                sm_f = sm_ps[:, :]

                r_lnx = Res()
                r_lxT = [Res(), Res()]
                r_stg = Res()
                r_hps = [Res(), Res()]
                r_cmb = [Res() for _ in range(4)]
                r_o2 = [Res(), Res()]
                r_sm = Res()
                lxT_evs = [[], []]
                gel_ev = {}
                sq_ev = {}
                mm2_ev = {}
                summ_ev = {}

                def newton_rstd(dst, src, tmp):
                    z = st[:, src:src + 2]
                    y = st[:, dst:dst + 2]
                    t = st[:, tmp:tmp + 2]
                    zi = z.bitcast(I32)
                    yi = y.bitcast(I32)
                    p.gemit("vector", lambda v, z=z: v.tensor_scalar(
                        z, z, 1e-5, None, ALU.add))
                    p.gemit("vector", lambda v, yi=yi, zi=zi: v.tensor_scalar(
                        yi, zi, 1, None, ALU.logical_shift_right))
                    p.gemit("vector", lambda v, yi=yi: v.tensor_scalar(
                        yi, yi, -1, None, ALU.bitwise_xor))
                    p.gemit("vector", lambda v, yi=yi: v.tensor_scalar(
                        yi, yi, 0x5F3759E0, None, ALU.add))
                    last = None
                    for _ in range(2):
                        p.gemit("vector", lambda v, t=t, y=y: v.tensor_tensor(
                            t, y, y, ALU.mult))
                        p.gemit("vector", lambda v, t=t, z=z: v.tensor_tensor(
                            t, t, z, ALU.mult))
                        p.gemit("vector", lambda v, t=t: v.tensor_scalar(
                            t, t, -0.5, 1.5, ALU.mult, ALU.add))
                        last = p.gemit("vector", lambda v, y=y, t=t:
                                       v.tensor_tensor(y, y, t, ALU.mult))
                    return last

                def emit_load(u):
                    par = u % 2
                    r_xin[par].begin_write(p, "sync")
                    if src_wait_evs is not None:
                        p.wait_all("sync",
                                   [src_wait_evs[2 * u], src_wait_evs[2 * u + 1]])
                    ev = p.emit("sync", lambda o, u=u, par=par: o.dma_start(
                        out=xin[par][:].rearrange("p (a n) -> p a n", a=2),
                        in_=pair_view(src_ap, 2 * u, D)), inc="dS", n=16)
                    r_xin[par].set_ready(ev)

                def emit_ln_stats(u):
                    par = u % 2
                    co = 32 * par
                    p.wait("vector", r_xin[par].ready)
                    r_lnx.begin_write(p, "vector")
                    x3 = xin[par][:].rearrange("p (a n) -> p a n", a=2)
                    p.gemit("vector", lambda v, x3=x3, co=co: v.reduce_sum(
                        st[:, co:co + 2], x3, AX.X))
                    for j in range(2):
                        p.gemit("vector", lambda v, par=par, j=j, co=co:
                                v.scalar_tensor_tensor(
                                    lnx[:, j * D:(j + 1) * D],
                                    xin[par][:, j * D:(j + 1) * D], 1.0,
                                    xin[par][:, j * D:(j + 1) * D], ALU.mult,
                                    ALU.mult,
                                    accum_out=st[:, co + 2 + j:co + 3 + j]))
                    p.gemit("vector", lambda v, co=co: v.tensor_scalar(
                        st[:, co + 4:co + 6], st[:, co:co + 2], 1.0 / D, None,
                        ALU.mult))
                    p.gemit("vector", lambda v, co=co: v.tensor_tensor(
                        st[:, co + 20:co + 22], st[:, co + 4:co + 6],
                        st[:, co + 4:co + 6], ALU.mult))
                    p.gemit("vector", lambda v, co=co: v.scalar_tensor_tensor(
                        st[:, co + 6:co + 8], st[:, co + 2:co + 4], 1.0 / D,
                        st[:, co + 20:co + 22], ALU.mult, ALU.subtract))

                def emit_rstd_apply(u):
                    par = u % 2
                    co = 32 * par
                    newton_rstd(co + 8, co + 6, co + 20)
                    ev = None
                    for j in range(2):
                        ev = p.emit("vector", lambda o, par=par, j=j, co=co:
                                    o.tensor_scalar(
                                        lnx[:, j * D:(j + 1) * D],
                                        xin[par][:, j * D:(j + 1) * D],
                                        st[:, co + 4 + j:co + 5 + j],
                                        st[:, co + 8 + j:co + 9 + j],
                                        ALU.subtract, ALU.mult), inc="dve")
                    r_lnx.set_ready(ev)

                def emit_tp(u, half):
                    par = u % 2
                    r_stg.begin_write(p, "tensor")
                    p.wait("tensor", r_lnx.ready)

                    def tp_ln(t_, half=half):
                        last = None
                        for k in range(8):
                            c = half * 4 + k // 2
                            j = k % 2
                            last = t_.transpose(
                                stg_ps[:, k * 128:(k + 1) * 128],
                                lnx[:, j * D + c * 128:j * D + (c + 1) * 128],
                                ident_b[:])
                        return last
                    e_tp = p.emit("tensor", tp_ln, inc="pe")
                    r_stg.set_ready(e_tp)
                    r_lnx.add_consumer(e_tp)
                    if half == 0:
                        r_lxT[par].begin_write(p, "scalar")
                        p.wait("scalar", e_tp)
                        e_te = p.emit("scalar", lambda o, par=par: o.copy(
                            lxT[par][:, 0:1024], stg_ps[:, :]), inc="act")
                        lxT_evs[par] = [e_te]
                    else:
                        r_lxT[par].begin_write(p, "vector")
                        p.wait("vector", e_tp)
                        e_te = p.emit("vector", lambda o, par=par: o.tensor_copy(
                            lxT[par][:, 1024:2048], stg_ps[:, :]), inc="dve")
                        lxT_evs[par] = lxT_evs[par] + [e_te]
                    r_stg.add_consumer(e_te)

                def emit_chunk(u, c):
                    par = u % 2
                    gb = c % 2
                    s = c % 4
                    r_hps[gb].begin_write(p, "tensor")
                    if c == 0:
                        p.wait_all("tensor", lxT_evs[par])
                        if "wqfc_q" in evs:
                            p.wait("tensor", evs["wqfc_q"])
                        if "wqpj_fold" in evs:
                            p.wait("tensor", evs["wqpj_fold"])

                    def mm1(t_, par=par, c=c, gb=gb):
                        last = None
                        for k in range(8):
                            last = t_.matmul(
                                hps[gb][:, 0:256],
                                wqfc[:, k * HF + c * 128:k * HF + (c + 1) * 128],
                                lxT[par][:, k * 256:(k + 1) * 256],
                                start=(k == 0), stop=(k == 7))
                        return last
                    e_mm1 = p.emit("tensor", mm1, inc="pe")
                    r_hps[gb].set_ready(e_mm1)
                    r_lxT[par].add_consumer(e_mm1)

                    r_cmb[s].begin_write(p, "scalar")
                    p.wait("scalar", e_mm1)
                    e_gel = p.emit("scalar", lambda o, s=s, gb=gb: o.activation(
                        cmb[:, s * 512:s * 512 + 256], hps[gb][:, 0:256],
                        AF.Gelu), inc="act")
                    r_hps[gb].add_consumer(e_gel)
                    gel_ev[c] = e_gel

                    p.wait("gpsimd", e_gel)
                    e_sq = p.emit("gpsimd", lambda o, s=s: o.tensor_tensor(
                        cmb[:, s * 512 + 256:s * 512 + 512],
                        cmb[:, s * 512:s * 512 + 256],
                        cmb[:, s * 512:s * 512 + 256], ALU.mult), inc="pool")
                    sq_ev[c] = e_sq

                def emit_mm2(u, c):
                    s = c % 4
                    if c == 0:
                        r_o2[0].begin_write(p, "tensor")
                        r_o2[1].begin_write(p, "tensor")
                        r_sm.begin_write(p, "tensor")
                    p.wait("tensor", gel_ev[c])

                    def mm2(t_, c=c, s=s):
                        last = None
                        for j in range(2):
                            for g in range(2):
                                last = t_.matmul(
                                    o2[j][:, g * 512:(g + 1) * 512],
                                    cmb[:, s * 512 + j * 128:s * 512 + (j + 1) * 128],
                                    wqpj[:, c * D + g * 512:c * D + (g + 1) * 512],
                                    start=(c == 0), stop=(c == NCH - 1))
                        return last
                    e_mm2 = p.emit("tensor", mm2, inc="pe")
                    mm2_ev[c] = e_mm2
                    p.wait("tensor", sq_ev[c])
                    e_summ = p.emit("tensor", lambda t_, s=s, c=c: t_.matmul(
                        sm_ps[0:1, 0:512], ones_col_b[:, 0:1],
                        cmb[:, s * 512:(s + 1) * 512],
                        start=(c == 0), stop=(c == NCH - 1)), inc="pe")
                    summ_ev[c] = e_summ
                    r_cmb[s].add_consumer(e_mm2)
                    r_cmb[s].add_consumer(e_summ)

                def emit_tail(u):
                    par = u % 2
                    co = 32 * par
                    e_s31 = summ_ev[NCH - 1]
                    e_m31 = mm2_ev[NCH - 1]
                    r_o2[0].set_ready(e_m31)
                    r_o2[1].set_ready(e_m31)
                    r_sm.set_ready(e_s31)
                    p.wait("vector", e_s31)
                    e_rows = p.gemit("vector", lambda v: v.tensor_copy(
                        rowb[0:1, :], sm_ps[0:1, 0:512]))
                    r_sm.add_consumer(e_rows)
                    p.wait("tensor", e_rows)

                    def row_tp(t_):
                        ins = None
                        for q in range(4):
                            ins = t_.matmul(sm_f[:, q:q + 1],
                                            rowb[0:1, q * 128:(q + 1) * 128],
                                            ones_row_f[0:1, 0:1],
                                            start=True, stop=True)
                        return ins
                    e_rtp = p.emit("tensor", row_tp, inc="pe")
                    p.wait("scalar", e_m31)
                    ev0a = p.emit("scalar", lambda o: o.activation(
                        tmpv[:, 0:1024], o2[0][:, :], AF.Copy, scale=0.5),
                        inc="act")
                    p.wait("vector", e_m31)
                    ev0b = p.emit("vector", lambda o: o.tensor_scalar(
                        tmpv[:, 1024:2048], o2[1][:, :], 0.5, None, ALU.mult),
                        inc="dve")
                    r_o2[0].add_consumer(ev0a)
                    r_o2[1].add_consumer(ev0b)
                    p.wait("vector", e_rtp)
                    p.gemit("vector", lambda v, co=co: v.tensor_scalar(
                        st[:, co + 12:co + 14], sm_f[:, 0:2], 1.0 / HF, None,
                        ALU.mult))
                    p.gemit("vector", lambda v, co=co: v.tensor_tensor(
                        st[:, co + 20:co + 22], st[:, co + 12:co + 14],
                        st[:, co + 12:co + 14], ALU.mult))
                    e_var = p.gemit("vector", lambda v, co=co:
                                    v.scalar_tensor_tensor(
                                        st[:, co + 14:co + 16], sm_f[:, 2:4],
                                        1.0 / HF, st[:, co + 20:co + 22],
                                        ALU.mult, ALU.subtract))
                    r_sm.add_consumer(e_var)
                    newton_rstd(co + 16, co + 14, co + 20)
                    p.gemit("vector", lambda v, co=co: v.tensor_tensor(
                        st[:, co + 18:co + 20], st[:, co + 12:co + 14],
                        st[:, co + 16:co + 18], ALU.mult))
                    p.gemit("vector", lambda v, co=co: v.tensor_scalar(
                        st[:, co + 18:co + 20], st[:, co + 18:co + 20], -1.0,
                        None, ALU.mult))
                    p.wait("vector", ev0a)
                    for j in range(2):
                        p.gemit("vector", lambda v, j=j, co=co:
                                v.scalar_tensor_tensor(
                                    tmpv[:, j * 1024:(j + 1) * 1024],
                                    tmpv[:, j * 1024:(j + 1) * 1024],
                                    st[:, co + 16 + j:co + 17 + j],
                                    halfb[:], ALU.mult, ALU.add))
                        p.gemit("vector", lambda v, j=j, co=co:
                                v.scalar_tensor_tensor(
                                    tmpv[:, j * 1024:(j + 1) * 1024],
                                    halfc[:],
                                    st[:, co + 18 + j:co + 19 + j],
                                    tmpv[:, j * 1024:(j + 1) * 1024],
                                    ALU.mult, ALU.add))
                        e_ev3 = p.gemit("vector", lambda v, par=par, j=j:
                                        v.tensor_tensor(
                                            xin[par][:, j * D:(j + 1) * D],
                                            xin[par][:, j * D:(j + 1) * D],
                                            tmpv[:, j * 1024:(j + 1) * 1024],
                                            ALU.add))
                        r_xin[par].add_consumer(e_ev3)
                        p.wait("sync", e_ev3)
                        e_sto = p.emit("sync", lambda o, u=u, j=j, par=par:
                                       o.dma_start(
                                           out=dst_ap[(2 * u + j) * 128:
                                                      (2 * u + j + 1) * 128, :],
                                           in_=xin[par][:, j * D:(j + 1) * D]),
                                       inc="dO", n=16)
                        r_xin[par].add_consumer(e_sto)
                        evs[f"{tag}_store_{2 * u + j}"] = e_sto

                if xin_pre is None:
                    emit_load(0)
                if NSUP > 1:
                    emit_load(1)
                emit_ln_stats(0)
                emit_rstd_apply(0)
                emit_tp(0, 0)
                emit_tp(0, 1)

                for u in range(NSUP):
                    nxt = u + 1
                    for c in range(NCH):
                        emit_chunk(u, c)
                        if c == 2 and nxt < NSUP:
                            emit_ln_stats(nxt)
                        if c == 8 and nxt < NSUP:
                            emit_rstd_apply(nxt)
                        if c >= 1:
                            emit_mm2(u, c - 1)
                        if c == 20 and nxt < NSUP:
                            emit_tp(nxt, 0)
                        if c == 26 and nxt < NSUP:
                            emit_tp(nxt, 1)
                    emit_mm2(u, NCH - 1)
                    emit_tail(u)
                    if nxt + 1 < NSUP:
                        emit_load(nxt + 1)
                if DEBUG:
                    dbg_dump("cmb", cmb[:], [128, 2048], BF16,
                             [gel_ev[NCH - 1], sq_ev[NCH - 1]])
                p.flush_block(tag)

        if "P1" in phases or "P3" in phases:
            with (
                nc.sbuf_tensor("wqfc", [128, 8 * HF], BF16) as wqfc,
                nc.sbuf_tensor("wqpj", [128, 32 * D], BF16) as wqpj,
                nc.sbuf_tensor("f1_xin0", [128, 2048], F32) as f1_xin0,
            ):
                r_xin_f1 = Res()
                if "P1" in phases:
                    with contextlib.ExitStack() as ps1:
                        wld0 = ps1.enter_context(nc.sbuf_tensor("wld0", [128, 2048], F32, side="right"))
                        wld1 = ps1.enter_context(nc.sbuf_tensor("wld1", [128, 2048], F32, side="right"))
                        wld2 = ps1.enter_context(nc.sbuf_tensor("wld2", [128, 2048], F32, side="right"))
                        wsq = ps1.enter_context(nc.sbuf_tensor("wsq", [128, 2048], F32, side="right"))
                        wsg = ps1.enter_context(nc.sbuf_tensor("wsg", [128, 2048], BF16, side="right"))
                        bvec_c = ps1.enter_context(nc.sbuf_tensor("bvec_c", [1, 1024], BF16, side="right"))
                        bvec_b = ps1.enter_context(nc.sbuf_tensor("bvec_b", [1, 1024], BF16, side="right"))
                        p1_ps = ps1.enter_context(nc.psum_tensor("p1_ps", [128, 2048], F32))
                        _ = None
                        wld = [wld0, wld1, wld2]
                        e_x = p.emit("scalar", lambda o: o.dma_start(
                            out=f1_xin0[:].rearrange("p (a n) -> p a n", a=2),
                            in_=pair_view(x_in, 0, D)), inc="dO", n=16)
                        r_xin_f1.set_ready(e_x)
                        halves = []
                        for c in range(8):
                            for hh in range(2):
                                halves.append(
                                    (wfcT[c * 128:(c + 1) * 128, hh * 2048:(hh + 1) * 2048],
                                     wqfc, c * HF + hh * 2048, 0))
                        for c in range(0, 32, 2):
                            halves.append((pair_view(wpjT, c, D), wqpj, c * D, 1))

                        p.emit("vector", lambda o: o.memset(gst[:, 0:8], 0.0))
                        buf_free = [Res(), Res(), Res()]
                        e_a = None
                        for idx, (src, _, _, mat) in enumerate(halves):
                            b = idx % 3
                            q = "sync" if idx % 2 == 0 else "scalar"
                            qs = "dS" if idx % 2 == 0 else "dO"
                            buf_free[b].begin_write(p, q)
                            e_l = p.emit(q, lambda o, s=src, b=b: o.dma_start(
                                out=wld[b][:], in_=s), inc=qs, n=16)
                            buf_free[b].set_ready(e_l)
                            col = mat * 4 + b
                            p.wait("vector", e_l)
                            p.gemit("vector", lambda v, b=b: v.reduce_sum(
                                st[:, b:b + 1], wld[b][:], AX.X,
                                apply_absolute_value=True))
                            e_a = p.gemit("vector", lambda v, col=col, b=b: v.tensor_tensor(
                                gst[:, col:col + 1], gst[:, col:col + 1],
                                st[:, b:b + 1], ALU.add))
                            buf_free[b].add_consumer(e_a)

                        p.gemit("vector", lambda v: v.reduce_sum(st[:, 0:1], gst[:, 0:4],
                                                                 AX.X))
                        e_a = p.gemit("vector", lambda v: v.reduce_sum(st[:, 1:2],
                                                                       gst[:, 4:8], AX.X))
                        p.wait("tensor", e_a)

                        def psum_gamma(t_):
                            t_.matmul(p1_ps[0:1, 0:1], st[:, 0:1], ones_col_f[:],
                                      start=True, stop=True)
                            return t_.matmul(p1_ps[0:1, 1:2], st[:, 1:2], ones_col_f[:],
                                             start=True, stop=True)
                        e_pg = p.emit("tensor", psum_gamma, inc="pe")
                        p.wait("vector", e_pg)

                        p.gemit("vector", lambda v: v.tensor_scalar(
                            st[0:1, 2:4], p1_ps[0:1, 0:2], 1.0 / (HF * D), None, ALU.mult))
                        p.gemit("vector", lambda v: v.tensor_scalar(
                            st[0:1, 2:4], st[0:1, 2:4], 1e-6, 0.5, ALU.add, ALU.mult))
                        e_thr = p.gemit("vector", lambda v: v.tensor_tensor(
                            st[0:1, 2:4], st[0:1, 2:4], st[0:1, 2:4], ALU.mult))
                        p.wait("tensor", e_thr)
                        e_bthr = p.emit("tensor", lambda o: o.matmul(
                            p1_ps[:, 512:514], ones_row_f[0:1, :], st[0:1, 2:4],
                            start=True, stop=True), inc="pe")
                        p.wait("vector", e_bthr)
                        e_thr2 = p.emit("vector", lambda o: o.tensor_copy(
                            thr2[:], p1_ps[:, 512:514]), inc="dve")
                        dbg_dump("thr2", thr2[:], [128, 2], F32, [e_thr2])
                        dbg_dump("gst", gst[:, 0:2], [128, 2], F32, [e_thr2])

                        wsq2 = [wsq, wsq]
                        wsg2 = [wsg, wsg]
                        _sq = Res()
                        _sg = Res()
                        sq_r = [_sq, _sq]
                        sg_r = [_sg, _sg]
                        for idx, (src, dst, dcol, mat) in enumerate(halves):
                            b = idx % 3
                            b2 = idx % 2
                            q = "sync" if idx % 2 == 0 else "gpsimd"
                            qs = "dS" if idx % 2 == 0 else "dG"
                            buf_free[b].begin_write(p, q)
                            e_l = p.emit(q, lambda o, s=src, b=b: o.dma_start(
                                out=wld[b][:], in_=s), inc=qs, n=16)
                            buf_free[b].set_ready(e_l)

                            sq_r[b2].begin_write(p, "gpsimd")
                            p.wait("gpsimd", e_l)
                            e_sq = p.emit("gpsimd", lambda o, b=b, b2=b2: o.tensor_tensor(
                                wsq2[b2][:], wld[b][:], wld[b][:], ALU.mult), inc="pool")
                            sq_r[b2].set_ready(e_sq)

                            sg_r[b2].begin_write(p, "scalar")
                            p.wait("scalar", e_l)
                            e_sg = p.emit("scalar", lambda o, b=b, b2=b2: o.activation(
                                wsg2[b2][:], wld[b][:], AF.Sign), inc="act")
                            sg_r[b2].set_ready(e_sg)
                            buf_free[b].add_consumer(e_sq)
                            buf_free[b].add_consumer(e_sg)

                            p.wait("vector", e_sq)
                            p.wait("vector", e_sg)
                            e_q = p.emit("vector", lambda o, dst=dst, dcol=dcol, mat=mat,
                                         b2=b2: o.scalar_tensor_tensor(
                                             dst[:, dcol:dcol + 2048], wsq2[b2][:],
                                             thr2[:, mat:mat + 1], wsg2[b2][:],
                                             ALU.is_gt, ALU.mult), inc="dve")
                            sq_r[b2].add_consumer(e_q)
                            sg_r[b2].add_consumer(e_q)
                            if idx == 15:
                                evs["wqfc_q"] = e_q
                        evs["wqpj_q"] = e_q
                        dbg_dump("wqfc0", wqfc[:, 0:1024], [128, 1024], BF16, [e_q])
                        dbg_dump("wqpj0", wqpj[:, 0:1024], [128, 1024], BF16, [e_q])

                        p.wait("tensor", evs["wqpj_q"])
                        p.wait("tensor", e_thr2)
                        e_b_list = []
                        for g in range(2):
                            def bmm(t_, g=g):
                                last = None
                                for c in range(32):
                                    last = t_.matmul(
                                        p1_ps[0:1, g * 512:(g + 1) * 512],
                                        lnbT[:, c:c + 1],
                                        wqpj[:, c * D + g * 512: c * D + (g + 1) * 512],
                                        start=(c == 0), stop=(c == 31))
                                return last
                            e_b = p.emit("tensor", bmm, inc="pe")
                            e_b_list.append(e_b)
                            p.wait("scalar", e_b)
                            e_bc = p.emit("scalar", lambda o, g=g: o.copy(
                                bvec_b[0:1, g * 512:(g + 1) * 512],
                                p1_ps[0:1, g * 512:(g + 1) * 512]), inc="act")

                        p.wait_all("vector", e_b_list)
                        def fold(v):
                            last = None
                            for c in range(32):
                                last = v.tensor_scalar(wqpj[:, c * D:(c + 1) * D],
                                                       wqpj[:, c * D:(c + 1) * D],
                                                       lnwT[:, c:c + 1], None, ALU.mult)
                            return last
                        e_fold = p.emit("vector", fold, inc="dve")
                        evs["wqpj_fold"] = e_fold

                        p.wait("tensor", e_fold)
                        p.wait("tensor", e_bc)

                        def cv_mm(t_):
                            last = None
                            for g in range(2):
                                for c in range(32):
                                    last = t_.matmul(
                                        p1_ps[0:1, g * 512:(g + 1) * 512],
                                        ones_col_b[:, 0:1],
                                        wqpj[:, c * D + g * 512: c * D + (g + 1) * 512],
                                        start=(c == 0), stop=(c == 31))
                            return last
                        e_cv = p.emit("tensor", cv_mm, inc="pe")
                        p.wait("scalar", e_cv)
                        e_cvc = p.emit("scalar", lambda o: o.copy(
                            bvec_c[0:1, :], p1_ps[0:1, 0:1024]), inc="act")

                        p.wait("tensor", e_bc)
                        p.wait("tensor", e_cvc)
                        def hb_mm(t_):
                            t_.matmul(p1_ps[:, 0:512], ones_row_b[0:1, :],
                                      bvec_b[0:1, 0:512], start=True, stop=True)
                            t_.matmul(p1_ps[:, 512:1024], ones_row_b[0:1, :],
                                      bvec_b[0:1, 512:1024], start=True, stop=True)
                            t_.matmul(p1_ps[:, 1024:1536], ones_row_b[0:1, :],
                                      bvec_c[0:1, 0:512], start=True, stop=True)
                            return t_.matmul(p1_ps[:, 1536:2048], ones_row_b[0:1, :],
                                             bvec_c[0:1, 512:1024], start=True, stop=True)
                        e_hb = p.emit("tensor", hb_mm, inc="pe")
                        p.wait("vector", e_hb)
                        e_halfb = p.emit("vector", lambda o: (
                            o.tensor_scalar(halfb[:], p1_ps[:, 0:1024], 0.5, None,
                                            ALU.mult),
                            o.tensor_scalar(halfc[:], p1_ps[:, 1024:2048], 0.5, None,
                                            ALU.mult))[-1], inc="dve")
                        dbg_dump("halfb", halfb[:], [128, 1024], F32, [e_halfb])
                        dbg_dump("wqpjf0", wqpj[:, 0:1024], [128, 1024], BF16, [e_halfb])

                        p.wait("scalar", ("dve", p.cnt["dve"]))
                        for c in range(8):
                            p.emit("scalar", lambda o, c=c: o.dma_start(
                                out=wqfc_s[c * 128:(c + 1) * 128, :],
                                in_=wqfc[:, c * HF:(c + 1) * HF]), inc="dO", n=16)
                        for c in range(0, 32, 2):
                            dstv = pair_view(wqpj_s, c, D)
                            p.emit("scalar", lambda o, d=dstv, c=c: o.dma_start(
                                out=d, in_=wqpj[:, c * D:(c + 2) * D].rearrange(
                                    "p (a n) -> p a n", a=2)), inc="dO", n=16)
                        evs["wq_stored"] = ("dO", p.cnt["dO"])
                        p.flush_block("p1")

                if "P3" in phases:
                    for c in range(8):
                        p.emit("gpsimd", lambda o, c=c: o.dma_start(
                            out=wo_c[c * 128:(c + 1) * 128, :],
                            in_=woT[c * 128:(c + 1) * 128, :]), inc="dG", n=16)
                    evs["wo_c"] = ("dG", p.cnt["dG"])
                    emit_ffn("f1", x_in, y_out if terminal == "P3" else xs,
                             wqfc, wqpj,
                             xin_pre=(f1_xin0, r_xin_f1) if "P1" in phases else None)

        if "P5" in phases or "P6" in phases:
            with (
                nc.sbuf_tensor("qT_b", [128, 8 * T], BF16) as qT_b,
                nc.sbuf_tensor("kT_b", [128, 8 * T], BF16) as kT_b,
                nc.sbuf_tensor("v_b", [128, NT * D], BF16) as v_b,
            ):
                if "P5" in phases:
                    with contextlib.ExitStack() as ps5:
                        wqkv_b = ps5.enter_context(nc.sbuf_tensor("a_wqkv", [128, 8 * 3 * D], BF16))
                        axin0 = ps5.enter_context(nc.sbuf_tensor("a_xin0", [128, D], F32))
                        axin1 = ps5.enter_context(nc.sbuf_tensor("a_xin1", [128, D], F32))
                        alnx = ps5.enter_context(nc.sbuf_tensor("a_lnx", [128, D], BF16))
                        alxT = ps5.enter_context(nc.sbuf_tensor("a_lxT", [128, D], BF16))
                        qr_f = ps5.enter_context(nc.sbuf_tensor("a_qr", [128, D], F32))
                        kr_f = ps5.enter_context(nc.sbuf_tensor("a_kr", [128, D], F32))
                        qrope = ps5.enter_context(nc.sbuf_tensor("a_qrope", [128, D], BF16))
                        krope = ps5.enter_context(nc.sbuf_tensor("a_krope", [128, D], BF16))
                        tq1 = ps5.enter_context(nc.sbuf_tensor("a_tq1", [128, 512], F32))
                        tq2 = ps5.enter_context(nc.sbuf_tensor("a_tq2", [128, 512], F32))
                        tk1 = ps5.enter_context(nc.sbuf_tensor("a_tk1", [128, 512], F32))
                        tk2 = ps5.enter_context(nc.sbuf_tensor("a_tk2", [128, 512], F32))
                        qkv_ps = ps5.enter_context(nc.psum_tensor("a_qkv", [128, 3 * D], F32))
                        stA_ps = ps5.enter_context(nc.psum_tensor("a_stA", [128, 1024], BF16))
                        stB_ps = ps5.enter_context(nc.psum_tensor("a_stB", [128, 1024], BF16))
                        _ = None
                        xin = [axin0, axin1]
                        r_xin = [Res(), Res()]
                        r_lnx = Res()
                        r_lxT = Res()
                        r_stA = Res()
                        r_stB = Res()
                        r_qkv = Res()
                        r_qr = Res()
                        r_kr = Res()
                        r_qrope = Res()
                        r_krope = Res()

                        for c in range(8):
                            p.emit("gpsimd", lambda o, c=c: o.dma_start(
                                out=wqkv_b[:, c * 3 * D:(c + 1) * 3 * D],
                                in_=wqkvT[c * 128:(c + 1) * 128, :]), inc="dG", n=16)
                        e_wqkv = ("dG", p.cnt["dG"])

                        for t in range(NT):
                            par = t % 2
                            sc = 8 * par

                            r_xin[par].begin_write(p, "sync")
                            if "P3" in phases:
                                p.wait("sync", evs[f"f1_store_{t}"])
                            e_ld = p.emit("sync", lambda o, t=t, par=par: o.dma_start(
                                out=xin[par][:], in_=xs[t * 128:(t + 1) * 128, :]),
                                inc="dS", n=16)
                            r_xin[par].set_ready(e_ld)

                            p.wait("vector", e_ld)
                            r_lnx.begin_write(p, "vector")

                            p.gemit("vector", lambda v, par=par, sc=sc: (
                                v.reduce_sum(st[:, sc + 0:sc + 1], xin[par][:], AX.X),
                                v.scalar_tensor_tensor(alnx[:], xin[par][:], 1.0,
                                                       xin[par][:], ALU.mult, ALU.mult,
                                                       accum_out=st[:, sc + 1:sc + 2]))[-1])
                            p.gemit("vector", lambda v, sc=sc: v.tensor_scalar(
                                st[:, sc + 2:sc + 3], st[:, sc + 0:sc + 1],
                                1.0 / D, None, ALU.mult))
                            p.gemit("vector", lambda v, sc=sc: v.tensor_tensor(
                                st[:, sc + 3:sc + 4], st[:, sc + 2:sc + 3],
                                st[:, sc + 2:sc + 3], ALU.mult))
                            e_st = p.gemit("vector", lambda v, sc=sc: v.scalar_tensor_tensor(
                                st[:, sc + 4:sc + 5], st[:, sc + 1:sc + 2], 1.0 / D,
                                st[:, sc + 3:sc + 4], ALU.mult, ALU.subtract))
                            p.wait("scalar", e_st)

                            def rstd(s, sc=sc):
                                s.activation(st[:, sc + 5:sc + 6], st[:, sc + 4:sc + 5],
                                             AF.Ln, bias=cst[:, 0:1])
                                return s.activation(st[:, sc + 6:sc + 7],
                                                    st[:, sc + 5:sc + 6], AF.Exp, scale=-0.5)
                            e_rs = p.emit("scalar", rstd, inc="act")
                            p.wait("vector", e_rs)
                            e_ln = p.emit("vector", lambda o, par=par, sc=sc: o.tensor_scalar(
                                alnx[:], xin[par][:], st[:, sc + 2:sc + 3],
                                st[:, sc + 6:sc + 7], ALU.subtract, ALU.mult), inc="dve")
                            r_lnx.set_ready(e_ln)
                            r_xin[par].add_consumer(e_ln)

                            r_stA.begin_write(p, "tensor")
                            p.wait("tensor", e_ln)

                            def tp_ln(t_):
                                last = None
                                for c in range(8):
                                    last = t_.transpose(stA_ps[:, c * 128:(c + 1) * 128],
                                                        alnx[:, c * 128:(c + 1) * 128],
                                                        ident_b[:])
                                return last
                            e_tp = p.emit("tensor", tp_ln, inc="pe")
                            r_stA.set_ready(e_tp)
                            r_lnx.add_consumer(e_tp)

                            r_lxT.begin_write(p, "scalar")
                            p.wait("scalar", e_tp)
                            e_lxT = p.emit("scalar", lambda o: o.copy(
                                alxT[:], stA_ps[:, :]), inc="act")
                            r_lxT.set_ready(e_lxT)
                            r_stA.add_consumer(e_lxT)

                            p.wait("tensor", e_lxT)
                            p.wait("tensor", e_wqkv)
                            r_qkv.begin_write(p, "tensor")

                            def qkv_mm(t_):
                                last = None
                                for g in range(6):
                                    for c in range(8):
                                        last = t_.matmul(
                                            qkv_ps[:, g * 512:(g + 1) * 512],
                                            alxT[:, c * 128:(c + 1) * 128],
                                            wqkv_b[:, c * 3 * D + g * 512:
                                                   c * 3 * D + (g + 1) * 512],
                                            start=(c == 0), stop=(c == 7))
                                return last
                            e_qkv = p.emit("tensor", qkv_mm, inc="pe")
                            r_qkv.set_ready(e_qkv)
                            r_lxT.add_consumer(e_qkv)

                            p.wait("vector", e_qkv)
                            r_qr.begin_write(p, "vector")
                            r_kr.begin_write(p, "vector")
                            e_cpq = p.gemit("vector", lambda v: v.tensor_copy(
                                qr_f[:], qkv_ps[:, 0:1024]))
                            p.wait("scalar", e_qkv)
                            e_cpk = p.emit("scalar", lambda o: o.copy(
                                kr_f[:], qkv_ps[:, 1024:2048]), inc="act")
                            e_cp = e_cpk
                            p.wait("gpsimd", e_cpk)
                            e_ksq = p.gemit("gpsimd", lambda g: (
                                g.tensor_tensor(tk1[:], kr_f[:, 0:512], kr_f[:, 0:512],
                                                ALU.mult),
                                g.tensor_tensor(tk2[:], kr_f[:, 512:1024],
                                                kr_f[:, 512:1024], ALU.mult))[-1])
                            p.gemit("vector", lambda v: (
                                v.tensor_tensor(tq1[:], qr_f[:, 0:512], qr_f[:, 0:512],
                                                ALU.mult),
                                v.tensor_tensor(tq2[:], qr_f[:, 512:1024],
                                                qr_f[:, 512:1024], ALU.mult))[-1])
                            p.wait("vector", e_ksq)

                            def sq_red(v):
                                v.reduce_sum(st[:, 16:24],
                                             tq1[:].rearrange("p (a b) -> p a b", b=64),
                                             AX.X)
                                v.reduce_sum(st[:, 24:32],
                                             tq2[:].rearrange("p (a b) -> p a b", b=64),
                                             AX.X)
                                v.reduce_sum(st[:, 32:40],
                                             tk1[:].rearrange("p (a b) -> p a b", b=64),
                                             AX.X)
                                return v.reduce_sum(
                                    st[:, 40:48],
                                    tk2[:].rearrange("p (a b) -> p a b", b=64), AX.X)
                            e_sqk = p.gemit("vector", sq_red)
                            r_qkv.add_consumer(e_cpq)
                            r_qkv.add_consumer(e_cpk)

                            p.wait("scalar", e_sqk)

                            def rms_rstd(s):
                                s.activation(st[:, 48:64], st[:, 16:32], AF.Ln,
                                             bias=cst[:, 1:2], scale=1.0 / HD)
                                s.activation(st[:, 48:64], st[:, 48:64], AF.Exp, scale=-0.5)
                                s.activation(st[:, 16:32], st[:, 32:48], AF.Ln,
                                             bias=cst[:, 1:2], scale=1.0 / HD)
                                return s.activation(st[:, 16:32], st[:, 16:32], AF.Exp,
                                                    scale=-0.5)
                            e_rms = p.emit("scalar", rms_rstd, inc="act")

                            p.wait("vector", e_rms)

                            def evq(v, t=t):
                                v.tensor_tensor(
                                    qr_f[:].rearrange("p (a b) -> p a b", b=64),
                                    qr_f[:].rearrange("p (a b) -> p a b", b=64),
                                    bass.AP(st[:].tensor, st[:].offset + 48,
                                            [list(st[:].ap[0]), [1, 16], [0, 64]]),
                                    ALU.mult)
                                return v.tensor_tensor(
                                    kr_f[:].rearrange("p (a b) -> p a b", b=64),
                                    kr_f[:].rearrange("p (a b) -> p a b", b=64),
                                    bass.AP(st[:].tensor, st[:].offset + 16,
                                            [list(st[:].ap[0]), [1, 16], [0, 64]]),
                                    ALU.mult)
                            e_vv = p.emit("scalar", lambda o, t=t: o.copy(
                                v_b[:, t * D:(t + 1) * D], qkv_ps[:, 2048:3072]),
                                inc="act")
                            e_ev = p.emit("vector", evq, inc="dve")
                            r_qr.set_ready(e_ev)
                            r_kr.set_ready(e_ev)
                            r_qkv.add_consumer(e_vv)

                            r_qrope.begin_write(p, "vector")

                            def rope_q(v, t=t):
                                A = head_bcast(ropeq[:], 0 + t * 32, 16, 32)
                                B = head_bcast(ropeq[:], RS + t * 32, 16, 32)
                                C = head_bcast(ropeq[:], 2 * RS + t * 32, 16, 32)
                                Dd = head_bcast(ropeq[:], 3 * RS + t * 32, 16, 32)
                                q1 = head_view(qr_f[:], 0, 16, 32, 64)
                                q2 = head_view(qr_f[:], 32, 16, 32, 64)
                                o1 = head_view(qrope[:], 0, 16, 32, 64)
                                o2 = head_view(qrope[:], 32, 16, 32, 64)
                                t1v = tq1[:].rearrange("p (a b) -> p a b", b=32)
                                t2v = tq2[:].rearrange("p (a b) -> p a b", b=32)
                                v.tensor_tensor(t1v, q1, A, ALU.mult)
                                v.tensor_tensor(t2v, q2, B, ALU.mult)
                                v.tensor_tensor(o1, t1v, t2v, ALU.subtract)
                                v.tensor_tensor(t1v, q1, C, ALU.mult)
                                v.tensor_tensor(t2v, q2, Dd, ALU.mult)
                                return v.tensor_tensor(o2, t1v, t2v, ALU.add)
                            e_rq = p.emit("vector", rope_q, inc="dve")
                            r_qrope.set_ready(e_rq)
                            r_qr.add_consumer(e_rq)

                            r_krope.begin_write(p, "gpsimd")
                            p.wait("gpsimd", e_ev)

                            def rope_k(g, t=t):
                                A = head_bcast(ropek[:], 0 + t * 32, 16, 32)
                                B = head_bcast(ropek[:], RS + t * 32, 16, 32)
                                C = head_bcast(ropek[:], 2 * RS + t * 32, 16, 32)
                                Dd = head_bcast(ropek[:], 3 * RS + t * 32, 16, 32)
                                k1 = head_view(kr_f[:], 0, 16, 32, 64)
                                k2 = head_view(kr_f[:], 32, 16, 32, 64)
                                o1 = head_view(krope[:], 0, 16, 32, 64)
                                o2 = head_view(krope[:], 32, 16, 32, 64)
                                t1v = tk1[:].rearrange("p (a b) -> p a b", b=32)
                                t2v = tk2[:].rearrange("p (a b) -> p a b", b=32)
                                g.tensor_tensor(t1v, k1, A, ALU.mult)
                                g.tensor_tensor(t2v, k2, B, ALU.mult)
                                g.tensor_tensor(o1, t1v, t2v, ALU.subtract)
                                g.tensor_tensor(t1v, k1, C, ALU.mult)
                                g.tensor_tensor(t2v, k2, Dd, ALU.mult)
                                return g.tensor_tensor(o2, t1v, t2v, ALU.add)
                            e_rk = p.emit("gpsimd", rope_k, inc="pool")
                            r_krope.set_ready(e_rk)
                            r_kr.add_consumer(e_rk)

                            r_stA.begin_write(p, "tensor")
                            p.wait("tensor", e_rq)

                            def tp_q(t_):
                                last = None
                                for hh in range(16):
                                    po = 64 * (hh % 2)
                                    pc = hh // 2
                                    last = t_.transpose(
                                        stA_ps[po:po + 64, pc * 128:(pc + 1) * 128],
                                        qrope[:, hh * 64:(hh + 1) * 64], ident_b[:])
                                return last
                            e_tq = p.emit("tensor", tp_q, inc="pe")
                            r_stA.set_ready(e_tq)
                            r_qrope.add_consumer(e_tq)

                            r_stB.begin_write(p, "tensor")
                            p.wait("tensor", e_rk)

                            def tp_k(t_):
                                last = None
                                for hh in range(16):
                                    po = 64 * (hh % 2)
                                    pc = hh // 2
                                    last = t_.transpose(
                                        stB_ps[po:po + 64, pc * 128:(pc + 1) * 128],
                                        krope[:, hh * 64:(hh + 1) * 64], ident_b[:])
                                return last
                            e_tk = p.emit("tensor", tp_k, inc="pe")
                            r_stB.set_ready(e_tk)
                            r_krope.add_consumer(e_tk)

                            p.wait("vector", e_tq)

                            p.wait("scalar", e_tq)

                            def evq2a(o, t=t):
                                iA = stA_ps[:, :].rearrange("p (a b) -> p a b", b=128)
                                oA = bass.AP(qT_b[:].tensor, qT_b[:].offset + t * 128,
                                             [list(qT_b[:].ap[0]), [T, 8], [1, 128]])
                                return o.copy(oA, iA)
                            e_ev2 = p.emit("scalar", evq2a, inc="act")
                            p.wait("scalar", e_tk)

                            def evq2b(o, t=t):
                                iB = stB_ps[:, :].rearrange("p (a b) -> p a b", b=128)
                                oB = bass.AP(kT_b[:].tensor, kT_b[:].offset + t * 128,
                                             [list(kT_b[:].ap[0]), [T, 8], [1, 128]])
                                return o.copy(oB, iB)
                            e_ev2b = p.emit("scalar", evq2b, inc="act")
                            r_stA.add_consumer(e_ev2)
                            r_stB.add_consumer(e_ev2b)
                        p.flush_block("p5")

                if "P6" in phases:
                    with contextlib.ExitStack() as ps6:
                        wo_b = ps6.enter_context(nc.sbuf_tensor("c_wo", [128, 8 * D], BF16))
                        Pb0 = ps6.enter_context(nc.sbuf_tensor("c_P0", [128, T], BF16))
                        Pb1 = ps6.enter_context(nc.sbuf_tensor("c_P1", [128, T], BF16))
                        Pn0 = ps6.enter_context(nc.sbuf_tensor("c_Pn0", [128, T], BF16))
                        Pn1 = ps6.enter_context(nc.sbuf_tensor("c_Pn1", [128, T], BF16))
                        PT0 = ps6.enter_context(nc.sbuf_tensor("c_PT0", [128, T], BF16))
                        PT1 = ps6.enter_context(nc.sbuf_tensor("c_PT1", [128, T], BF16))
                        ot_sb = ps6.enter_context(nc.sbuf_tensor("c_ot", [128, 1024], BF16))
                        cxr0 = ps6.enter_context(nc.sbuf_tensor("c_xr0", [128, D], F32))
                        cxr1 = ps6.enter_context(nc.sbuf_tensor("c_xr1", [128, D], F32))
                        cxo0 = ps6.enter_context(nc.sbuf_tensor("c_xo0", [128, D], F32))
                        cxo1 = ps6.enter_context(nc.sbuf_tensor("c_xo1", [128, D], F32))
                        S_ps = ps6.enter_context(nc.psum_tensor("c_S", [128, min(T, 2048)], F32))
                        PT_ps = ps6.enter_context(nc.psum_tensor("c_PT", [128, 1024], BF16))
                        OT_ps = ps6.enter_context(nc.psum_tensor("c_OT", [128, 1024], F32))
                        WO_ps = ps6.enter_context(nc.psum_tensor("c_WO", [128, 512], F32))
                        _ = None
                        Pb2 = ps6.enter_context(nc.sbuf_tensor("c_P2", [128, T], BF16))
                        Pb3 = ps6.enter_context(nc.sbuf_tensor("c_P3", [128, T], BF16))
                        Pn2 = ps6.enter_context(nc.sbuf_tensor("c_Pn2", [128, T], BF16))
                        Pn3 = ps6.enter_context(nc.sbuf_tensor("c_Pn3", [128, T], BF16))
                        PT2 = ps6.enter_context(nc.sbuf_tensor("c_PT2", [128, T], BF16))
                        PT3 = ps6.enter_context(nc.sbuf_tensor("c_PT3", [128, T], BF16))
                        Pb = [Pb0, Pb1, Pb2, Pb3]
                        Pn = [Pn0, Pn1, Pn2, Pn3]
                        PT = [PT0, PT1, PT2, PT3]
                        xr = [cxr0, cxr1]
                        xo = [cxo0, cxo1]

                        if "wo_c" in evs:
                            p.wait("sync", evs["wo_c"])
                            for c in range(8):
                                p.emit("sync", lambda o, c=c: o.dma_start(
                                    out=wo_b[:, c * D:(c + 1) * D],
                                    in_=wo_c[c * 128:(c + 1) * 128, :]),
                                    inc="dS", n=16)
                            e_wo = ("dS", p.cnt["dS"])
                        else:
                            for c in range(8):
                                p.emit("gpsimd", lambda o, c=c: o.dma_start(
                                    out=wo_b[:, c * D:(c + 1) * D],
                                    in_=woT[c * 128:(c + 1) * 128, :]), inc="dG", n=16)
                            e_wo = ("dG", p.cnt["dG"])

                        r_SH = [Res(), Res()]
                        r_Pb = [Res(), Res(), Res(), Res()]
                        r_Pn = [Res(), Res(), Res(), Res()]
                        r_PTps = Res()
                        r_PTsb = [Res(), Res(), Res(), Res()]
                        r_OT = Res()
                        r_ot = Res()
                        r_WO = Res()
                        r_xr = [Res(), Res()]
                        r_xo = [Res(), Res()]
                        r_linv = [Res(), Res()]
                        pend_wo = [None]
                        pend_gen = [None]

                        for i in range(NT):
                            ip = i % 2
                            ncols = (i + 1) * 128
                            r_xr[ip].begin_write(p, "sync")
                            if "P5" not in phases and "P3" in phases:
                                p.wait("sync", evs[f"f1_store_{i}"])
                            e_lr = p.emit("sync", lambda o, i=i, ip=ip: o.dma_start(
                                out=xr[ip][:], in_=xs[i * 128:(i + 1) * 128, :]),
                                inc="dS", n=16)
                            r_xr[ip].set_ready(e_lr)

                            def stage1(h, pump=None, i=i, ncols=ncols):
                                hp = h % 2
                                bi = h % 4
                                po = 64 * hp
                                pc = h // 2
                                pb = 32 + hp * 16
                                c_nm = pb + 5
                                c_lA, c_lB, c_li = pb + 6, pb + 7, pb + 9
                                c_l = pb + 8
                                colsA = min(ncols, 1024)
                                colsB = ncols - colsA
                                diag_lo = i * 128
                                nch = (ncols + 511) // 512
                                sbase = 1024 * (h % 2) if ncols <= 1024 else 0

                                lqv = qT_b[po:po + 64,
                                           pc * T + i * 128: pc * T + (i + 1) * 128]
                                cm_evs = []
                                e_s_half = {0: None, 1: None}
                                seen_half = set()
                                for ci in range(nch):
                                    c0 = ci * 512
                                    w = min(512, ncols - c0)
                                    half = (sbase + c0) // 1024
                                    if half not in seen_half:
                                        seen_half.add(half)
                                        r_SH[half].begin_write(p, "tensor")
                                    has_diag = c0 <= diag_lo < c0 + w

                                    def smm(t_, c0=c0, w=w, po=po, pc=pc, sbase=sbase,
                                            has_diag=has_diag, dl=diag_lo):
                                        last = t_.matmul(
                                            S_ps[:, sbase + c0:sbase + c0 + w], lqv,
                                            kT_b[po:po + 64, pc * T + c0: pc * T + c0 + w],
                                            start=True, stop=not has_diag)
                                        if has_diag:
                                            last = t_.matmul(
                                                S_ps[:, sbase + dl:sbase + dl + 128],
                                                ident_b[:], mask_b[:],
                                                start=False, stop=True)
                                        return last
                                    e_sc = p.emit("tensor", smm, inc="pe")
                                    e_s_half[half] = e_sc
                                    p.wait("vector", e_sc)
                                    e_cm = p.emit("vector", lambda v, ci=ci, c0=c0, w=w,
                                                  sbase=sbase:
                                                  v.reduce_max(st[:, pb + ci:pb + ci + 1],
                                                               S_ps[:, sbase + c0:
                                                                    sbase + c0 + w],
                                                               AX.X,
                                                               negate=True), inc="dve")
                                    cm_evs.append(e_cm)
                                for half in seen_half:
                                    r_SH[half].set_ready(e_s_half[half])
                                if pump is not None:
                                    pump()
                                p.wait("vector", cm_evs[-1])
                                e_nm = p.emit("vector", lambda v, nch=nch: v.tensor_reduce(
                                    st[:, c_nm:c_nm + 1], st[:, pb:pb + nch], AX.X,
                                    ALU.min), inc="dve")

                                r_Pb[bi].begin_write(p, "scalar")
                                p.wait("scalar", e_nm)
                                dstlA = c_lA if colsB else c_l
                                e_eA = p.emit("scalar", lambda o, bi=bi, colsA=colsA,
                                              c_nm=c_nm, d=dstlA, sbase=sbase:
                                              o.activation(
                                                  Pb[bi][:, 0:colsA],
                                                  S_ps[:, sbase:sbase + colsA],
                                                  AF.Exp, bias=st[:, c_nm:c_nm + 1],
                                                  accum_out=st[:, d:d + 1]), inc="act")
                                r_SH[sbase // 1024].add_consumer(e_eA)
                                e_eB = None
                                if colsB:
                                    e_eB = p.emit("scalar", lambda o, bi=bi, colsB=colsB,
                                                  c_nm=c_nm, d=c_lB: o.activation(
                                                      Pb[bi][:, 1024:1024 + colsB],
                                                      S_ps[:, 1024:1024 + colsB],
                                                      AF.Exp, bias=st[:, c_nm:c_nm + 1],
                                                      accum_out=st[:, d:d + 1]), inc="act")
                                    r_SH[1].add_consumer(e_eB)
                                r_Pb[bi].set_ready(e_eB or e_eA)

                                r_linv[hp].begin_write(p, "vector")
                                p.wait("vector", e_eB or e_eA)
                                if colsB:
                                    p.gemit("vector", lambda v, a=c_lA, bcol=c_lB, d=c_l:
                                            v.tensor_tensor(st[:, d:d + 1], st[:, a:a + 1],
                                                            st[:, bcol:bcol + 1], ALU.add))
                                e_li = p.gemit("vector", lambda v, scol=c_l, d=c_li:
                                               v.reciprocal(st[:, d:d + 1],
                                                            st[:, scol:scol + 1]))
                                r_linv[hp].set_ready(e_li)

                                r_Pn[bi].begin_write(p, "gpsimd")
                                p.wait("gpsimd", e_li)
                                e_pn = p.emit("gpsimd", lambda g, bi=bi, ncols=ncols,
                                              d=c_li: g.tensor_tensor(
                                                  Pn[bi][:, 0:ncols], Pb[bi][:, 0:ncols],
                                                  bcast_free(st[:, d:d + 1], ncols),
                                                  ALU.mult), inc="pool")
                                r_Pn[bi].set_ready(e_pn)
                                r_Pb[bi].add_consumer(e_pn)
                                r_linv[hp].add_consumer(e_pn)
                                if pump is not None:
                                    pump()

                            def stage2pair(hE, i=i, ncols=ncols):
                                nb = i + 1
                                banks = [PT_ps[:, :],
                                         WO_ps[:, :].bitcast(BF16)]
                                bank_res = [r_PTps, r_WO]
                                pv_done = {}
                                for hp in (0, 1):
                                    p.wait("tensor", r_Pn[(hE + hp) % 4].ready)
                                r_OT.begin_write(p, "tensor")
                                e_pv = None
                                for b0 in range(0, nb, 8):
                                    yield
                                    bw = min(8, nb - b0)
                                    ev_evac = {}
                                    for hp in (0, 1):
                                        bi = (hE + hp) % 4
                                        bank_res[hp].begin_write(p, "tensor")

                                        def tp_p(t_, bi=bi, b0=b0, bw=bw, hp=hp):
                                            last = None
                                            for c in range(bw):
                                                last = t_.transpose(
                                                    banks[hp][:, c * 128:(c + 1) * 128],
                                                    Pn[bi][:, (b0 + c) * 128:
                                                            (b0 + c + 1) * 128],
                                                    ident_b[:])
                                            return last
                                        e_t = p.emit("tensor", tp_p, inc="pe")
                                        bank_res[hp].set_ready(e_t)
                                        r_Pn[bi].add_consumer(e_t)
                                        eng = "scalar" if hp == 0 else "vector"
                                        if b0 == 0:
                                            r_PTsb[bi].begin_write(p, eng)
                                        p.wait(eng, e_t)
                                        if hp == 0:
                                            e_e = p.emit("scalar", lambda o, bi=bi, b0=b0,
                                                         bw=bw, hp=hp: o.copy(
                                                             PT[bi][:, b0 * 128:
                                                                    (b0 + bw) * 128],
                                                             banks[hp][:, 0:bw * 128]),
                                                         inc="act")
                                        else:
                                            e_e = p.emit("vector", lambda o, bi=bi, b0=b0,
                                                         bw=bw, hp=hp: o.tensor_copy(
                                                             PT[bi][:, b0 * 128:
                                                                    (b0 + bw) * 128],
                                                             banks[hp][:, 0:bw * 128]),
                                                         inc="dve")
                                        bank_res[hp].add_consumer(e_e)
                                        ev_evac[hp] = e_e
                                        if b0 == 0:
                                            pv_done[hp] = []
                                        pv_done[hp].append(e_e)

                                    p.wait_all("tensor", [ev_evac[0], ev_evac[1]])

                                    def pvb(t_, hE=hE, i=i, b0=b0, bw=bw):
                                        last = None
                                        for b in range(b0, b0 + bw):
                                            for hp in (0, 1):
                                                h = hE + hp
                                                bi = h % 4
                                                po = 64 * hp
                                                off = 512 * hp
                                                last = t_.matmul(
                                                    OT_ps[po:po + 64, off:off + 128],
                                                    v_b[:, b * D + h * HD:
                                                        b * D + (h + 1) * HD],
                                                    PT[bi][:, b * 128:(b + 1) * 128],
                                                    start=(b == 0), stop=(b == i),
                                                    tile_position=(0, po))
                                        return last
                                    e_pv = p.emit("tensor", pvb, inc="pe")
                                    for hp in (0, 1):
                                        r_PTsb[(hE + hp) % 4].add_consumer(e_pv)
                                r_OT.set_ready(e_pv)

                                pc = hE // 2
                                if pc == 0:
                                    r_ot.begin_write(p, "vector")
                                p.wait("vector", e_pv)
                                e_oe = p.emit("vector", lambda o, pc=pc: (
                                    o.tensor_copy(ot_sb[0:64, pc * 128:(pc + 1) * 128],
                                                  OT_ps[0:64, 0:128]),
                                    o.tensor_copy(ot_sb[64:128, pc * 128:(pc + 1) * 128],
                                                  OT_ps[64:128, 512:640]))[-1],
                                    inc="dve")
                                r_OT.add_consumer(e_oe)
                                if pc == 7:
                                    r_ot.set_ready(e_oe)

                            def wo_block(i2):
                                ip2 = i2 % 2
                                p.wait("tensor", r_ot.ready)
                                p.wait("tensor", e_wo)
                                r_xo[ip2].begin_write(p, "vector")
                                e_res = None
                                for g in range(2):
                                    r_WO.begin_write(p, "tensor")

                                    def womm(t_, g=g):
                                        last = None
                                        for c in range(8):
                                            last = t_.matmul(
                                                WO_ps[:, 0:512],
                                                ot_sb[:, c * 128:(c + 1) * 128],
                                                wo_b[:, c * D + g * 512:
                                                     c * D + (g + 1) * 512],
                                                start=(c == 0), stop=(c == 7))
                                        return last
                                    e_wmm = p.emit("tensor", womm, inc="pe")
                                    r_WO.set_ready(e_wmm)
                                    if g == 1:
                                        r_ot.add_consumer(e_wmm)
                                    p.wait("vector", e_wmm)
                                    p.wait("vector", r_xr[ip2].ready)
                                    e_res = p.emit("vector", lambda o, ip2=ip2, g=g:
                                                   o.tensor_tensor(
                                                       xo[ip2][:, g * 512:(g + 1) * 512],
                                                       WO_ps[:, 0:512],
                                                       xr[ip2][:, g * 512:(g + 1) * 512],
                                                       ALU.add), inc="dve")
                                    r_WO.add_consumer(e_res)
                                    r_xr[ip2].add_consumer(e_res)
                                r_xo[ip2].set_ready(e_res)

                                p.wait("scalar", e_res)
                                p6_dst = y_out if terminal == "P6" else xs
                                e_sto = p.emit("scalar", lambda o, i2=i2, ip2=ip2,
                                               d=p6_dst: o.dma_start(
                                                   out=d[i2 * 128:(i2 + 1) * 128, :],
                                                   in_=xo[ip2][:]), inc="dO", n=16)
                                r_xo[ip2].add_consumer(e_sto)
                                evs[f"a_store_{i2}"] = e_sto

                            for pr in range(NH // 2):
                                gen = (stage2pair(2 * pr - 2) if pr > 0
                                       else pend_gen[0])

                                def pump(g=gen):
                                    if g is not None:
                                        next(g, None)
                                stage1(2 * pr, pump)
                                stage1(2 * pr + 1, pump)
                                if gen is not None:
                                    for _ in gen:
                                        pass
                                if pr == 0:
                                    pend_gen[0] = None
                                    if pend_wo[0] is not None:
                                        wo_block(pend_wo[0])
                                        pend_wo[0] = None
                            pend_gen[0] = stage2pair(NH - 2)
                            pend_wo[0] = i
                        for _ in pend_gen[0]:
                            pass
                        wo_block(pend_wo[0])
                        p.flush_block("p6")

        if "P7" in phases:
            with (
                nc.sbuf_tensor("wqfc2", [128, 8 * HF], BF16) as wqfc2,
                nc.sbuf_tensor("wqpj2", [128, 32 * D], BF16) as wqpj2,
            ):
                if "P1" in phases:
                    p.wait("sync", evs["wq_stored"])
                    p.wait("scalar", evs["wq_stored"])
                for c in range(8):
                    q, qs = ("sync", "dS") if c % 2 == 0 else ("scalar", "dO")
                    p.emit(q, lambda o, c=c: o.dma_start(
                        out=wqfc2[:, c * HF:(c + 1) * HF],
                        in_=wqfc_s[c * 128:(c + 1) * 128, :]), inc=qs, n=16)
                for ci, c in enumerate(range(0, 32, 2)):
                    srcv = pair_view(wqpj_s, c, D)
                    q, qs = ("sync", "dS") if ci % 2 == 0 else ("scalar", "dO")
                    p.emit(q, lambda o, s=srcv, c=c: o.dma_start(
                        out=wqpj2[:, c * D:(c + 2) * D].rearrange("p (a n) -> p a n", a=2),
                        in_=s), inc=qs, n=16)
                p.wait("tensor", ("dS", p.cnt["dS"]))
                p.wait("tensor", ("dO", p.cnt["dO"]))
                src_evs = ([evs.get(f"a_store_{t}") for t in range(NT)]
                           if "P6" in phases else
                           ([evs.get(f"f1_store_{t}") for t in range(NT)]
                            if "P3" in phases else None))
                emit_ffn("f2", xs, y_out, wqfc2, wqpj2, src_wait_evs=src_evs)

        p.wait("sync", ("dO", p.cnt["dO"]))
        p.flush_block("fin")

    return nc


_NC_CACHE = {}


def _get_nc():
    if "full" not in _NC_CACHE:
        _NC_CACHE["full"] = build_program()
    return _NC_CACHE["full"]


def kernel(L0_attn_x, L0_attn_wqkv, L0_attn_wo, L0_attn_q_norm_w, L0_attn_k_norm_w,
           L0_ffn_W_fc, L0_ffn_ln_w, L0_ffn_ln_b, L0_ffn_W_proj):
    x = np.ascontiguousarray(np.asarray(L0_attn_x, dtype=np.float32))
    wqkvT = np.ascontiguousarray(np.asarray(L0_attn_wqkv, dtype=np.float32).T)
    woT = np.ascontiguousarray(np.asarray(L0_attn_wo, dtype=np.float32).T)
    wfcT = np.ascontiguousarray(np.asarray(L0_ffn_W_fc, dtype=np.float32).T)
    wpjT = np.ascontiguousarray(np.asarray(L0_ffn_W_proj, dtype=np.float32).T)
    qnw = np.ascontiguousarray(np.asarray(L0_attn_q_norm_w, dtype=np.float32).reshape(1, HD))
    knw = np.ascontiguousarray(np.asarray(L0_attn_k_norm_w, dtype=np.float32).reshape(1, HD))
    lnw = np.ascontiguousarray(np.asarray(L0_ffn_ln_w, dtype=np.float32).reshape(32, 128))
    lnb = np.ascontiguousarray(np.asarray(L0_ffn_ln_b, dtype=np.float32).reshape(32, 128))

    nc = _get_nc()
    in_maps = []
    for b in range(NCORES):
        in_maps.append({
            "x": np.ascontiguousarray(x[b]),
            "wqkvT": wqkvT, "woT": woT, "wfcT": wfcT, "wpjT": wpjT,
            "qnw": qnw, "knw": knw, "lnw": lnw, "lnb": lnb,
        })
    res = run_bass_kernel_spmd(nc, in_maps, core_ids=list(range(NCORES)))
    return np.stack([np.asarray(r["y"], dtype=np.float32) for r in res.results], axis=0)



# version 31 (final)
# speedup vs baseline: 1.7978x; 1.0080x over previous
import math
import contextlib
import numpy as np

import concourse.bass as bass
import concourse.mybir as mybir
from concourse.bass_utils import run_bass_kernel_spmd

F32 = mybir.dt.float32
BF16 = mybir.dt.bfloat16
I32 = mybir.dt.int32
AF = mybir.ActivationFunctionType
ALU = mybir.AluOpType
AX = mybir.AxisListType

T = 2048
D = 1024
HF = 4096
NH = 16
HD = 64
NT = T // 128
RS = NT * 32
NCORES = 8


def set_dims(nt):
    global T, NT, RS
    NT = nt
    T = nt * 128
    RS = NT * 32

ROPE_THETA = 10000.0
TWO_PI = 2 * math.pi

ENGINES = ("tensor", "vector", "scalar", "gpsimd", "sync")
DEBUG = False


def bcast_free(ap, n):
    return bass.AP(ap.tensor, ap.offset, [list(ap.ap[0]), [0, n]])


def head_view(ap, col_off, nh, w, stride):
    return bass.AP(ap.tensor, ap.offset + col_off, [list(ap.ap[0]), [stride, nh], [1, w]])


def head_bcast(ap, col_off, nh, w):
    return bass.AP(ap.tensor, ap.offset + col_off, [list(ap.ap[0]), [0, nh], [1, w]])


def pair_view(dram_ap, c, ncols):
    off = c * 128 * ncols
    return bass.AP(dram_ap.tensor, dram_ap.offset + off,
                   [[ncols, 128], [128 * ncols, 2], [1, ncols]])


class Prog:
    def __init__(self, nc):
        self.nc = nc
        self.cmds = {e: [] for e in ENGINES}
        self.sems = {}
        self.cnt = {}
        self.hw = {e: {} for e in ENGINES}
        self.blocks = 0

    def sem(self, name):
        h = self.nc.semaphore(name).__enter__()
        self.sems[name] = h
        self.cnt[name] = 0
        return h

    def emit(self, eng, fn, inc=None, n=1):
        ev = None
        if inc is not None:
            self.cnt[inc] += n
            ev = (inc, self.cnt[inc])
            sem = self.sems[inc]

            def cl(obj, fn=fn, sem=sem, n=n):
                ins = fn(obj)
                assert ins is not None, "emitted fn with inc must return an instruction"
                ins.then_inc(sem, n)
        else:
            def cl(obj, fn=fn):
                fn(obj)
        self.cmds[eng].append(cl)
        return ev

    ENG_SEM = {"tensor": "pe", "vector": "dve", "scalar": "act", "gpsimd": "pool"}

    def gemit(self, eng, fn, n=1):
        ev = self.emit(eng, fn, inc=self.ENG_SEM[eng], n=n)
        self.wait(eng, ev)
        return ev

    def wait(self, eng, ev):
        if ev is None:
            return
        s, v = ev
        assert v <= self.cnt[s], f"wait on future event {s}={v} > {self.cnt[s]}"
        if self.hw[eng].get(s, 0) >= v:
            return
        self.hw[eng][s] = v
        sem = self.sems[s]
        self.cmds[eng].append(lambda obj, sem=sem, v=v: obj.wait_ge(sem, v))

    def wait_all(self, eng, evs):
        best = {}
        for ev in evs:
            if ev is None:
                continue
            s, v = ev
            if best.get(s, 0) < v:
                best[s] = v
        for s, v in best.items():
            self.wait(eng, (s, v))

    def flush_block(self, name):
        cmds = self.cmds
        self.cmds = {e: [] for e in ENGINES}
        nonempty = [e for e in ENGINES if cmds[e]]
        if not nonempty:
            return
        with self.nc.Block(f"b{self.blocks}_{name}") as blk:
            for e in nonempty:
                def replay(obj, lst=cmds[e]):
                    for c in lst:
                        c(obj)
                getattr(blk, e)(replay)
        self.blocks += 1
        for s in ("pe", "dve", "act", "pool"):
            if s in self.cnt:
                for e in ENGINES:
                    self.hw[e][s] = self.cnt[s]


class Res:
    __slots__ = ("ready", "cons")

    def __init__(self):
        self.ready = None
        self.cons = []

    def begin_write(self, p, eng):
        p.wait_all(eng, self.cons)

    def set_ready(self, ev):
        self.ready = ev
        self.cons = []

    def add_consumer(self, ev):
        if ev is not None:
            self.cons.append(ev)


def build_program(phases=("P0", "P1", "P3", "P5", "P6", "P7")):
    terminal = "P7" if "P7" in phases else ("P6" if "P6" in phases else "P3")
    nc = bass.Bass("TRN2", target_bir_lowering=False)
    dbg_tensors = {}

    def dbg_decl(name, shape, dt):
        if not DEBUG:
            return None
        dbg_tensors[name] = nc.dram_tensor(f"dbg_{name}", shape, dt,
                                           kind="ExternalOutput").ap()
        return dbg_tensors[name]

    x_in = nc.dram_tensor("x", [T, D], F32, kind="ExternalInput").ap()
    wqkvT = nc.dram_tensor("wqkvT", [D, 3 * D], F32, kind="ExternalInput").ap()
    woT = nc.dram_tensor("woT", [D, D], F32, kind="ExternalInput").ap()
    wfcT = nc.dram_tensor("wfcT", [D, HF], F32, kind="ExternalInput").ap()
    wpjT = nc.dram_tensor("wpjT", [HF, D], F32, kind="ExternalInput").ap()
    qnw = nc.dram_tensor("qnw", [1, HD], F32, kind="ExternalInput").ap()
    knw = nc.dram_tensor("knw", [1, HD], F32, kind="ExternalInput").ap()
    lnw = nc.dram_tensor("lnw", [32, 128], F32, kind="ExternalInput").ap()
    lnb = nc.dram_tensor("lnb", [32, 128], F32, kind="ExternalInput").ap()
    y_out = nc.dram_tensor("y", [T, D], F32, kind="ExternalOutput").ap()

    xs = nc.dram_tensor("xs", [T, D], F32, kind="Internal").ap()
    wqfc_s = nc.dram_tensor("wqfc_s", [D, HF], BF16, kind="Internal").ap()
    wqpj_s = nc.dram_tensor("wqpj_s", [HF, D], BF16, kind="Internal").ap()
    wo_c = nc.dram_tensor("wo_c", [D, D], BF16, kind="Internal").ap()

    es = contextlib.ExitStack()
    with es:
        p = Prog(nc)
        for s in ("dS", "dG", "dO", "pe", "dve", "act", "pool"):
            p.sem(s)

        def dbg_dump(name, src_ap, shape, dt, wait_evs=(), res=None):
            if not DEBUG:
                return
            d = dbg_decl(name, shape, dt)
            p.wait_all("sync", [e for e in wait_evs if e is not None])
            ev = p.emit("sync", lambda o, d=d, s=src_ap: o.dma_start(out=d, in_=s),
                        inc="dO", n=16)
            if res is not None:
                res.add_consumer(ev)

        def sb(name, shape, dt):
            return es.enter_context(nc.sbuf_tensor(name, shape, dt))

        ident_b = sb("ident_b", [128, 128], BF16)
        ident_f = sb("ident_f", [128, 128], F32)
        mask_b = sb("mask_b", [128, 128], BF16)
        ones_row_b = sb("ones_row_b", [1, 128], BF16)
        ones_row_f = sb("ones_row_f", [1, 128], F32)
        ones_col_f = sb("ones_col_f", [128, 1], F32)
        ones_col_b = sb("ones_col_b", [128, 1], BF16)
        ropeq = sb("ropeq", [128, 4 * RS], F32)
        ropek = sb("ropek", [128, 4 * RS], F32)
        halfb = sb("halfb", [128, 1024], F32)
        halfc = sb("halfc", [128, 1024], F32)
        lnwT = sb("lnwT", [128, 32], F32)
        lnbT = sb("lnbT", [128, 32], BF16)
        thr2 = sb("thr2", [128, 2], F32)
        st = sb("st", [128, 64], F32)
        gst = sb("gst", [128, 8], F32)
        cst = sb("cst", [128, 2], F32)

        def init_cst(g):
            g.memset(st[:], 0.0)
            g.memset(gst[:], 0.0)
            g.memset(cst[:, 0:1], 1e-5)
            return g.memset(cst[:, 1:2], 1e-6)
        p.emit("gpsimd", init_cst, inc="pool")
        p.flush_block("cst")

        evs = {}

        if "P0" in phases:
            with contextlib.ExitStack() as ps0:
                qn_sb = ps0.enter_context(nc.sbuf_tensor("qn_sb", [1, HD], F32))
                kn_sb = ps0.enter_context(nc.sbuf_tensor("kn_sb", [1, HD], F32))
                lnw_r = ps0.enter_context(nc.sbuf_tensor("lnw_r", [32, 128], F32))
                lnb_r = ps0.enter_context(nc.sbuf_tensor("lnb_r", [32, 128], BF16))
                wbc = ps0.enter_context(nc.sbuf_tensor("wbc", [128, 128], F32))
                ang = ps0.enter_context(nc.sbuf_tensor("ang", [128, RS], F32))
                ang2 = ps0.enter_context(nc.sbuf_tensor("ang2", [128, RS], F32))
                rt1 = ps0.enter_context(nc.sbuf_tensor("rt1", [128, RS], F32))
                rk32 = ps0.enter_context(nc.sbuf_tensor("rk32", [128, RS], I32))
                rkf = ps0.enter_context(nc.sbuf_tensor("rkf", [128, RS], F32))
                sin_t = ps0.enter_context(nc.sbuf_tensor("sin_t", [128, RS], F32))
                cos_t = ps0.enter_context(nc.sbuf_tensor("cos_t", [128, RS], F32))
                invf = ps0.enter_context(nc.sbuf_tensor("invf", [128, 32], F32))
                pos = ps0.enter_context(nc.sbuf_tensor("pos", [128, NT], F32))
                p0_ps = ps0.enter_context(nc.psum_tensor("p0_ps", [128, 1024], F32))
                _ = None
                p.emit("sync", lambda o: o.dma_start(out=qn_sb[:], in_=qnw), inc="dS", n=16)
                p.emit("sync", lambda o: o.dma_start(out=kn_sb[:], in_=knw), inc="dS", n=16)
                e_lnw = p.emit("sync", lambda o: o.dma_start(out=lnw_r[:], in_=lnw), inc="dS", n=16)
                e_qkn = e_lnw
                e_lnb = p.emit("gpsimd", lambda o: o.dma_start(out=lnb_r[:], in_=lnb), inc="dG", n=16)

                def g0a(g):
                    g.memset(ones_row_b[:], 1.0)
                    g.memset(ones_row_f[:], 1.0)
                    g.memset(ones_col_f[:], 1.0)
                    g.memset(ones_col_b[:], 1.0)
                    g.memset(ident_b[:], 1.0)
                    g.memset(ident_f[:], 1.0)
                    return g.memset(mask_b[:], 0.0)
                p.gemit("gpsimd", g0a)

                def g0b(g):
                    g.affine_select(mask_b[:], mask_b[:], [[-1, 128]], ALU.is_ge, -1e9,
                                    base=0, channel_multiplier=1)
                    g.affine_select(ident_b[:], ident_b[:], [[-1, 128]], ALU.is_equal, 0.0,
                                    base=0, channel_multiplier=1)
                    g.affine_select(ident_f[:], ident_f[:], [[-1, 128]], ALU.is_equal, 0.0,
                                    base=0, channel_multiplier=1)
                    g.iota(invf[:], [[1, 32]], base=0, channel_multiplier=0,
                           allow_small_or_imprecise_dtypes=True)
                    return g.iota(pos[:], [[128, NT]], base=0, channel_multiplier=1,
                                  allow_small_or_imprecise_dtypes=True)
                e_g0 = p.gemit("gpsimd", g0b)

                p.wait("scalar", e_g0)
                e_invf = p.emit("scalar", lambda o: o.activation(
                    invf[:], invf[:], AF.Exp, bias=0.0,
                    scale=-2.0 * math.log(ROPE_THETA) / HD), inc="act")

                p.wait("vector", e_invf)

                def mk_ang(v):
                    in0 = bass.AP(pos[:].tensor, pos[:].offset,
                                  [list(pos[:].ap[0]), [1, NT], [0, 32]])
                    in1 = bass.AP(invf[:].tensor, invf[:].offset,
                                  [list(invf[:].ap[0]), [0, NT], [1, 32]])
                    v.tensor_tensor(ang[:].rearrange("p (a b) -> p a b", b=32),
                                    in0, in1, ALU.mult)
                    v.tensor_scalar(rt1[:], ang[:], 1.0 / TWO_PI, None, ALU.mult)
                    v.tensor_copy(rk32[:], rt1[:])
                    v.tensor_copy(rkf[:], rk32[:])
                    v.scalar_tensor_tensor(rt1[:], rkf[:], -TWO_PI, ang[:], ALU.mult, ALU.add)
                    v.tensor_scalar(rkf[:], rt1[:], math.pi, None, ALU.is_gt)
                    v.scalar_tensor_tensor(rt1[:], rkf[:], -TWO_PI, rt1[:], ALU.mult, ALU.add)
                    v.tensor_scalar(ang2[:], ang[:], math.pi / 2, None, ALU.add)
                    v.tensor_scalar(ang[:], ang2[:], 1.0 / TWO_PI, None, ALU.mult)
                    v.tensor_copy(rk32[:], ang[:])
                    v.tensor_copy(rkf[:], rk32[:])
                    v.scalar_tensor_tensor(ang2[:], rkf[:], -TWO_PI, ang2[:],
                                           ALU.mult, ALU.add)
                    v.tensor_scalar(rkf[:], ang2[:], math.pi, None, ALU.is_gt)
                    return v.scalar_tensor_tensor(ang2[:], rkf[:], -TWO_PI, ang2[:],
                                                  ALU.mult, ALU.add)
                e_rr = p.emit("vector", mk_ang, inc="dve")

                p.wait("scalar", e_rr)
                p.emit("scalar", lambda o: o.activation(sin_t[:], rt1[:], AF.Sin))
                e_sc = p.emit("scalar", lambda o: o.activation(cos_t[:], ang2[:], AF.Sin),
                              inc="act")

                p.wait("tensor", e_qkn)
                p.wait("tensor", e_lnw)
                p.wait("tensor", e_g0)

                def pe0(t_):
                    t_.matmul(p0_ps[:, 0:64], ones_row_f[0:1, :], qn_sb[0:1, :],
                              start=True, stop=True)
                    t_.matmul(p0_ps[:, 64:128], ones_row_f[0:1, :], kn_sb[0:1, :],
                              start=True, stop=True)
                    return t_.transpose(p0_ps[:, 512:544], lnw_r[:, :], ident_f[0:32, 0:32])
                e_pe0 = p.emit("tensor", pe0, inc="pe")
                p.wait("vector", e_pe0)

                def ev0(v):
                    v.tensor_copy(wbc[:], p0_ps[:, 0:128])
                    return v.tensor_copy(lnwT[:], p0_ps[:, 512:544])
                e_ev0 = p.emit("vector", ev0, inc="dve")

                p.wait("vector", e_sc)

                def tables(v):
                    last = None
                    for dst, wofs, src, scale in (
                        (ropeq[:, 0:RS], 0, cos_t, 0.125),
                        (ropeq[:, RS:2 * RS], 32, sin_t, 0.125),
                        (ropeq[:, 2 * RS:3 * RS], 0, sin_t, 0.125),
                        (ropeq[:, 3 * RS:4 * RS], 32, cos_t, 0.125),
                        (ropek[:, 0:RS], 64, cos_t, 1.0),
                        (ropek[:, RS:2 * RS], 96, sin_t, 1.0),
                        (ropek[:, 2 * RS:3 * RS], 64, sin_t, 1.0),
                        (ropek[:, 3 * RS:4 * RS], 96, cos_t, 1.0),
                    ):
                        last = v.scalar_tensor_tensor(
                            dst.rearrange("p (a b) -> p a b", b=32),
                            src[:].rearrange("p (a b) -> p a b", b=32),
                            scale, head_bcast(wbc[:], wofs, NT, 32), ALU.mult, ALU.mult)
                    return last
                p.emit("vector", tables)
                p.flush_block("p0a")

                with nc.psum_tensor("p0b_ps", [128, 32], BF16) as p0b_ps:
                    p.wait("tensor", e_lnb)
                    e_tb = p.emit("tensor", lambda o: o.transpose(
                        p0b_ps[:, :], lnb_r[:, :], ident_b[0:32, 0:32]), inc="pe")
                    p.wait("vector", e_tb)
                    e_lnbT = p.emit("vector", lambda o: o.tensor_copy(lnbT[:], p0b_ps[:, :]),
                                    inc="dve")
                    dbg_dump("lnwT", lnwT[:], [128, 32], F32, [e_ev0])
                    dbg_dump("lnbT", lnbT[:], [128, 32], BF16, [e_lnbT])
                    p.flush_block("p0b")

        def emit_ffn(tag, src_ap, dst_ap, wqfc, wqpj, src_wait_evs=None,
                     xin_pre=None):
            NSUP = NT // 2
            NCH = HF // 128
            with contextlib.ExitStack() as psf:
                if xin_pre is None:
                    xin0 = psf.enter_context(nc.sbuf_tensor(f"{tag}_xin0", [128, 2048], F32))
                    r_x0 = Res()
                else:
                    xin0, r_x0 = xin_pre
                xin1 = psf.enter_context(nc.sbuf_tensor(f"{tag}_xin1", [128, 2048], F32))
                r_xin = [r_x0, Res()]
                lnx = psf.enter_context(nc.sbuf_tensor(f"{tag}_lnx", [128, 2048], BF16))
                lxT0 = psf.enter_context(nc.sbuf_tensor(f"{tag}_lxT0", [128, 2048], BF16))
                lxT1 = psf.enter_context(nc.sbuf_tensor(f"{tag}_lxT1", [128, 2048], BF16))
                cmb = psf.enter_context(nc.sbuf_tensor(f"{tag}_cmb", [128, 2048], BF16))
                tmpv = psf.enter_context(nc.sbuf_tensor(f"{tag}_tmp", [128, 2048], F32))
                rowb = psf.enter_context(nc.sbuf_tensor(f"{tag}_row", [1, 512], F32))
                stg_ps = psf.enter_context(nc.psum_tensor(f"{tag}_stg", [128, 1024], BF16))
                hps0 = psf.enter_context(nc.psum_tensor(f"{tag}_h0", [128, 512], F32))
                hps1 = psf.enter_context(nc.psum_tensor(f"{tag}_h1", [128, 512], F32))
                o2a_ps = psf.enter_context(nc.psum_tensor(f"{tag}_o2a", [128, 1024], F32))
                o2b_ps = psf.enter_context(nc.psum_tensor(f"{tag}_o2b", [128, 1024], F32))
                sm_ps = psf.enter_context(nc.psum_tensor(f"{tag}_sm", [128, 512], F32))
                xin = [xin0, xin1]
                lxT = [lxT0, lxT1]
                hps = [hps0, hps1]
                o2 = [o2a_ps, o2b_ps]
                sm_f = sm_ps[:, :]

                r_lnx = Res()
                r_lxT = [Res(), Res()]
                r_stg = Res()
                r_hps = [Res(), Res()]
                r_cmb = [Res() for _ in range(4)]
                r_o2 = [Res(), Res()]
                r_sm = Res()
                lxT_evs = [[], []]
                gel_ev = {}
                sq_ev = {}
                mm2_ev = {}
                summ_ev = {}

                def newton_rstd(dst, src, tmp):
                    z = st[:, src:src + 2]
                    y = st[:, dst:dst + 2]
                    t = st[:, tmp:tmp + 2]
                    zi = z.bitcast(I32)
                    yi = y.bitcast(I32)
                    p.gemit("vector", lambda v, z=z: v.tensor_scalar(
                        z, z, 1e-5, None, ALU.add))
                    p.gemit("vector", lambda v, yi=yi, zi=zi: v.tensor_scalar(
                        yi, zi, 1, None, ALU.logical_shift_right))
                    p.gemit("vector", lambda v, yi=yi: v.tensor_scalar(
                        yi, yi, -1, None, ALU.bitwise_xor))
                    p.gemit("vector", lambda v, yi=yi: v.tensor_scalar(
                        yi, yi, 0x5F3759E0, None, ALU.add))
                    last = None
                    for _ in range(2):
                        p.gemit("vector", lambda v, t=t, y=y: v.tensor_tensor(
                            t, y, y, ALU.mult))
                        p.gemit("vector", lambda v, t=t, z=z: v.tensor_tensor(
                            t, t, z, ALU.mult))
                        p.gemit("vector", lambda v, t=t: v.tensor_scalar(
                            t, t, -0.5, 1.5, ALU.mult, ALU.add))
                        last = p.gemit("vector", lambda v, y=y, t=t:
                                       v.tensor_tensor(y, y, t, ALU.mult))
                    return last

                def emit_load(u):
                    par = u % 2
                    r_xin[par].begin_write(p, "sync")
                    if src_wait_evs is not None:
                        p.wait_all("sync",
                                   [src_wait_evs[2 * u], src_wait_evs[2 * u + 1]])
                    ev = p.emit("sync", lambda o, u=u, par=par: o.dma_start(
                        out=xin[par][:].rearrange("p (a n) -> p a n", a=2),
                        in_=pair_view(src_ap, 2 * u, D)), inc="dS", n=16)
                    r_xin[par].set_ready(ev)

                def emit_ln_stats(u):
                    par = u % 2
                    co = 32 * par
                    p.wait("vector", r_xin[par].ready)
                    r_lnx.begin_write(p, "vector")
                    x3 = xin[par][:].rearrange("p (a n) -> p a n", a=2)
                    p.gemit("vector", lambda v, x3=x3, co=co: v.reduce_sum(
                        st[:, co:co + 2], x3, AX.X))
                    for j in range(2):
                        p.gemit("vector", lambda v, par=par, j=j, co=co:
                                v.scalar_tensor_tensor(
                                    lnx[:, j * D:(j + 1) * D],
                                    xin[par][:, j * D:(j + 1) * D], 1.0,
                                    xin[par][:, j * D:(j + 1) * D], ALU.mult,
                                    ALU.mult,
                                    accum_out=st[:, co + 2 + j:co + 3 + j]))
                    p.gemit("vector", lambda v, co=co: v.tensor_scalar(
                        st[:, co + 4:co + 6], st[:, co:co + 2], 1.0 / D, None,
                        ALU.mult))
                    p.gemit("vector", lambda v, co=co: v.tensor_tensor(
                        st[:, co + 20:co + 22], st[:, co + 4:co + 6],
                        st[:, co + 4:co + 6], ALU.mult))
                    p.gemit("vector", lambda v, co=co: v.scalar_tensor_tensor(
                        st[:, co + 6:co + 8], st[:, co + 2:co + 4], 1.0 / D,
                        st[:, co + 20:co + 22], ALU.mult, ALU.subtract))

                def emit_rstd_apply(u):
                    par = u % 2
                    co = 32 * par
                    newton_rstd(co + 8, co + 6, co + 20)
                    ev = None
                    for j in range(2):
                        ev = p.emit("vector", lambda o, par=par, j=j, co=co:
                                    o.tensor_scalar(
                                        lnx[:, j * D:(j + 1) * D],
                                        xin[par][:, j * D:(j + 1) * D],
                                        st[:, co + 4 + j:co + 5 + j],
                                        st[:, co + 8 + j:co + 9 + j],
                                        ALU.subtract, ALU.mult), inc="dve")
                    r_lnx.set_ready(ev)

                def emit_tp(u, half):
                    par = u % 2
                    r_stg.begin_write(p, "tensor")
                    p.wait("tensor", r_lnx.ready)

                    def tp_ln(t_, half=half):
                        last = None
                        for k in range(8):
                            c = half * 4 + k // 2
                            j = k % 2
                            last = t_.transpose(
                                stg_ps[:, k * 128:(k + 1) * 128],
                                lnx[:, j * D + c * 128:j * D + (c + 1) * 128],
                                ident_b[:])
                        return last
                    e_tp = p.emit("tensor", tp_ln, inc="pe")
                    r_stg.set_ready(e_tp)
                    r_lnx.add_consumer(e_tp)
                    if half == 0:
                        r_lxT[par].begin_write(p, "scalar")
                        p.wait("scalar", e_tp)
                        e_te = p.emit("scalar", lambda o, par=par: o.copy(
                            lxT[par][:, 0:1024], stg_ps[:, :]), inc="act")
                        lxT_evs[par] = [e_te]
                    else:
                        r_lxT[par].begin_write(p, "vector")
                        p.wait("vector", e_tp)
                        e_te = p.emit("vector", lambda o, par=par: o.tensor_copy(
                            lxT[par][:, 1024:2048], stg_ps[:, :]), inc="dve")
                        lxT_evs[par] = lxT_evs[par] + [e_te]
                    r_stg.add_consumer(e_te)

                def emit_chunk(u, c):
                    par = u % 2
                    gb = c % 2
                    s = c % 4
                    r_hps[gb].begin_write(p, "tensor")
                    if c == 0:
                        p.wait_all("tensor", lxT_evs[par])
                        if "wqfc_q" in evs:
                            p.wait("tensor", evs["wqfc_q"])
                        if "wqpj_fold" in evs:
                            p.wait("tensor", evs["wqpj_fold"])

                    def mm1(t_, par=par, c=c, gb=gb):
                        last = None
                        for k in range(8):
                            last = t_.matmul(
                                hps[gb][:, 0:256],
                                wqfc[:, k * HF + c * 128:k * HF + (c + 1) * 128],
                                lxT[par][:, k * 256:(k + 1) * 256],
                                start=(k == 0), stop=(k == 7))
                        return last
                    e_mm1 = p.emit("tensor", mm1, inc="pe")
                    r_hps[gb].set_ready(e_mm1)
                    r_lxT[par].add_consumer(e_mm1)

                    r_cmb[s].begin_write(p, "scalar")
                    p.wait("scalar", e_mm1)
                    e_gel = p.emit("scalar", lambda o, s=s, gb=gb: o.activation(
                        cmb[:, s * 512:s * 512 + 256], hps[gb][:, 0:256],
                        AF.Gelu), inc="act")
                    r_hps[gb].add_consumer(e_gel)
                    gel_ev[c] = e_gel

                    p.wait("gpsimd", e_gel)
                    e_sq = p.emit("gpsimd", lambda o, s=s: o.tensor_tensor(
                        cmb[:, s * 512 + 256:s * 512 + 512],
                        cmb[:, s * 512:s * 512 + 256],
                        cmb[:, s * 512:s * 512 + 256], ALU.mult), inc="pool")
                    sq_ev[c] = e_sq

                def emit_mm2(u, c):
                    s = c % 4
                    if c == 0:
                        r_o2[0].begin_write(p, "tensor")
                        r_o2[1].begin_write(p, "tensor")
                        r_sm.begin_write(p, "tensor")
                    p.wait("tensor", gel_ev[c])

                    def mm2(t_, c=c, s=s):
                        last = None
                        for j in range(2):
                            for g in range(2):
                                last = t_.matmul(
                                    o2[j][:, g * 512:(g + 1) * 512],
                                    cmb[:, s * 512 + j * 128:s * 512 + (j + 1) * 128],
                                    wqpj[:, c * D + g * 512:c * D + (g + 1) * 512],
                                    start=(c == 0), stop=(c == NCH - 1))
                        return last
                    e_mm2 = p.emit("tensor", mm2, inc="pe")
                    mm2_ev[c] = e_mm2
                    p.wait("tensor", sq_ev[c])
                    e_summ = p.emit("tensor", lambda t_, s=s, c=c: t_.matmul(
                        sm_ps[0:1, 0:512], ones_col_b[:, 0:1],
                        cmb[:, s * 512:(s + 1) * 512],
                        start=(c == 0), stop=(c == NCH - 1)), inc="pe")
                    summ_ev[c] = e_summ
                    r_cmb[s].add_consumer(e_mm2)
                    r_cmb[s].add_consumer(e_summ)

                def emit_tail(u):
                    par = u % 2
                    co = 32 * par
                    e_s31 = summ_ev[NCH - 1]
                    e_m31 = mm2_ev[NCH - 1]
                    r_o2[0].set_ready(e_m31)
                    r_o2[1].set_ready(e_m31)
                    r_sm.set_ready(e_s31)
                    p.wait("vector", e_s31)
                    e_rows = p.gemit("vector", lambda v: v.tensor_copy(
                        rowb[0:1, :], sm_ps[0:1, 0:512]))
                    r_sm.add_consumer(e_rows)
                    p.wait("tensor", e_rows)

                    def row_tp(t_):
                        ins = None
                        for q in range(4):
                            ins = t_.matmul(sm_f[:, q:q + 1],
                                            rowb[0:1, q * 128:(q + 1) * 128],
                                            ones_row_f[0:1, 0:1],
                                            start=True, stop=True)
                        return ins
                    e_rtp = p.emit("tensor", row_tp, inc="pe")
                    p.wait("scalar", e_m31)
                    ev0a = p.emit("scalar", lambda o: o.activation(
                        tmpv[:, 0:1024], o2[0][:, :], AF.Copy, scale=0.5),
                        inc="act")
                    p.wait("vector", e_m31)
                    ev0b = p.emit("vector", lambda o: o.tensor_scalar(
                        tmpv[:, 1024:2048], o2[1][:, :], 0.5, None, ALU.mult),
                        inc="dve")
                    r_o2[0].add_consumer(ev0a)
                    r_o2[1].add_consumer(ev0b)
                    p.wait("vector", e_rtp)
                    p.gemit("vector", lambda v, co=co: v.tensor_scalar(
                        st[:, co + 12:co + 14], sm_f[:, 0:2], 1.0 / HF, None,
                        ALU.mult))
                    p.gemit("vector", lambda v, co=co: v.tensor_tensor(
                        st[:, co + 20:co + 22], st[:, co + 12:co + 14],
                        st[:, co + 12:co + 14], ALU.mult))
                    e_var = p.gemit("vector", lambda v, co=co:
                                    v.scalar_tensor_tensor(
                                        st[:, co + 14:co + 16], sm_f[:, 2:4],
                                        1.0 / HF, st[:, co + 20:co + 22],
                                        ALU.mult, ALU.subtract))
                    r_sm.add_consumer(e_var)
                    newton_rstd(co + 16, co + 14, co + 20)
                    p.gemit("vector", lambda v, co=co: v.tensor_tensor(
                        st[:, co + 18:co + 20], st[:, co + 12:co + 14],
                        st[:, co + 16:co + 18], ALU.mult))
                    p.gemit("vector", lambda v, co=co: v.tensor_scalar(
                        st[:, co + 18:co + 20], st[:, co + 18:co + 20], -1.0,
                        None, ALU.mult))
                    p.wait("vector", ev0a)
                    for j in range(2):
                        p.gemit("vector", lambda v, j=j, co=co:
                                v.scalar_tensor_tensor(
                                    tmpv[:, j * 1024:(j + 1) * 1024],
                                    tmpv[:, j * 1024:(j + 1) * 1024],
                                    st[:, co + 16 + j:co + 17 + j],
                                    halfb[:], ALU.mult, ALU.add))
                        p.gemit("vector", lambda v, j=j, co=co:
                                v.scalar_tensor_tensor(
                                    tmpv[:, j * 1024:(j + 1) * 1024],
                                    halfc[:],
                                    st[:, co + 18 + j:co + 19 + j],
                                    tmpv[:, j * 1024:(j + 1) * 1024],
                                    ALU.mult, ALU.add))
                        e_ev3 = p.gemit("vector", lambda v, par=par, j=j:
                                        v.tensor_tensor(
                                            xin[par][:, j * D:(j + 1) * D],
                                            xin[par][:, j * D:(j + 1) * D],
                                            tmpv[:, j * 1024:(j + 1) * 1024],
                                            ALU.add))
                        r_xin[par].add_consumer(e_ev3)
                        p.wait("sync", e_ev3)
                        e_sto = p.emit("sync", lambda o, u=u, j=j, par=par:
                                       o.dma_start(
                                           out=dst_ap[(2 * u + j) * 128:
                                                      (2 * u + j + 1) * 128, :],
                                           in_=xin[par][:, j * D:(j + 1) * D]),
                                       inc="dO", n=16)
                        r_xin[par].add_consumer(e_sto)
                        evs[f"{tag}_store_{2 * u + j}"] = e_sto

                if xin_pre is None:
                    emit_load(0)
                if NSUP > 1:
                    emit_load(1)
                emit_ln_stats(0)
                emit_rstd_apply(0)
                emit_tp(0, 0)
                emit_tp(0, 1)

                for u in range(NSUP):
                    nxt = u + 1
                    for c in range(NCH):
                        emit_chunk(u, c)
                        if c == 2 and nxt < NSUP:
                            emit_ln_stats(nxt)
                        if c == 8 and nxt < NSUP:
                            emit_rstd_apply(nxt)
                        if c >= 1:
                            emit_mm2(u, c - 1)
                        if c == 20 and nxt < NSUP:
                            emit_tp(nxt, 0)
                        if c == 26 and nxt < NSUP:
                            emit_tp(nxt, 1)
                    emit_mm2(u, NCH - 1)
                    emit_tail(u)
                    if nxt + 1 < NSUP:
                        emit_load(nxt + 1)
                if DEBUG:
                    dbg_dump("cmb", cmb[:], [128, 2048], BF16,
                             [gel_ev[NCH - 1], sq_ev[NCH - 1]])
                p.flush_block(tag)

        if "P1" in phases or "P3" in phases:
            with (
                nc.sbuf_tensor("wqfc", [128, 8 * HF], BF16) as wqfc,
                nc.sbuf_tensor("wqpj", [128, 32 * D], BF16) as wqpj,
                nc.sbuf_tensor("f1_xin0", [128, 2048], F32) as f1_xin0,
            ):
                r_xin_f1 = Res()
                if "P1" in phases:
                    with contextlib.ExitStack() as ps1:
                        wld0 = ps1.enter_context(nc.sbuf_tensor("wld0", [128, 2048], F32, side="right"))
                        wld1 = ps1.enter_context(nc.sbuf_tensor("wld1", [128, 2048], F32, side="right"))
                        wld2 = ps1.enter_context(nc.sbuf_tensor("wld2", [128, 2048], F32, side="right"))
                        wsq = ps1.enter_context(nc.sbuf_tensor("wsq", [128, 2048], F32, side="right"))
                        wsg = ps1.enter_context(nc.sbuf_tensor("wsg", [128, 2048], BF16, side="right"))
                        bvec_c = ps1.enter_context(nc.sbuf_tensor("bvec_c", [1, 1024], BF16, side="right"))
                        bvec_b = ps1.enter_context(nc.sbuf_tensor("bvec_b", [1, 1024], BF16, side="right"))
                        p1_ps = ps1.enter_context(nc.psum_tensor("p1_ps", [128, 2048], F32))
                        _ = None
                        wld = [wld0, wld1, wld2]
                        e_x = p.emit("scalar", lambda o: o.dma_start(
                            out=f1_xin0[:].rearrange("p (a n) -> p a n", a=2),
                            in_=pair_view(x_in, 0, D)), inc="dO", n=16)
                        r_xin_f1.set_ready(e_x)
                        halves = []
                        for c in range(8):
                            for hh in range(2):
                                halves.append(
                                    (wfcT[c * 128:(c + 1) * 128, hh * 2048:(hh + 1) * 2048],
                                     wqfc, c * HF + hh * 2048, 0))
                        for c in range(0, 32, 2):
                            halves.append((pair_view(wpjT, c, D), wqpj, c * D, 1))

                        p.emit("vector", lambda o: o.memset(gst[:, 0:8], 0.0))
                        buf_free = [Res(), Res(), Res()]
                        e_a = None
                        for idx, (src, _, _, mat) in enumerate(halves):
                            b = idx % 3
                            q = "sync" if idx % 2 == 0 else "scalar"
                            qs = "dS" if idx % 2 == 0 else "dO"
                            buf_free[b].begin_write(p, q)
                            e_l = p.emit(q, lambda o, s=src, b=b: o.dma_start(
                                out=wld[b][:], in_=s), inc=qs, n=16)
                            buf_free[b].set_ready(e_l)
                            col = mat * 4 + b
                            p.wait("vector", e_l)
                            p.gemit("vector", lambda v, b=b: v.reduce_sum(
                                st[:, b:b + 1], wld[b][:], AX.X,
                                apply_absolute_value=True))
                            e_a = p.gemit("vector", lambda v, col=col, b=b: v.tensor_tensor(
                                gst[:, col:col + 1], gst[:, col:col + 1],
                                st[:, b:b + 1], ALU.add))
                            buf_free[b].add_consumer(e_a)

                        p.gemit("vector", lambda v: v.reduce_sum(st[:, 0:1], gst[:, 0:4],
                                                                 AX.X))
                        e_a = p.gemit("vector", lambda v: v.reduce_sum(st[:, 1:2],
                                                                       gst[:, 4:8], AX.X))
                        p.wait("tensor", e_a)

                        def psum_gamma(t_):
                            t_.matmul(p1_ps[0:1, 0:1], st[:, 0:1], ones_col_f[:],
                                      start=True, stop=True)
                            return t_.matmul(p1_ps[0:1, 1:2], st[:, 1:2], ones_col_f[:],
                                             start=True, stop=True)
                        e_pg = p.emit("tensor", psum_gamma, inc="pe")
                        p.wait("vector", e_pg)

                        p.gemit("vector", lambda v: v.tensor_scalar(
                            st[0:1, 2:4], p1_ps[0:1, 0:2], 1.0 / (HF * D), None, ALU.mult))
                        p.gemit("vector", lambda v: v.tensor_scalar(
                            st[0:1, 2:4], st[0:1, 2:4], 1e-6, 0.5, ALU.add, ALU.mult))
                        e_thr = p.gemit("vector", lambda v: v.tensor_tensor(
                            st[0:1, 2:4], st[0:1, 2:4], st[0:1, 2:4], ALU.mult))
                        p.wait("tensor", e_thr)
                        e_bthr = p.emit("tensor", lambda o: o.matmul(
                            p1_ps[:, 512:514], ones_row_f[0:1, :], st[0:1, 2:4],
                            start=True, stop=True), inc="pe")
                        p.wait("vector", e_bthr)
                        e_thr2 = p.emit("vector", lambda o: o.tensor_copy(
                            thr2[:], p1_ps[:, 512:514]), inc="dve")
                        dbg_dump("thr2", thr2[:], [128, 2], F32, [e_thr2])
                        dbg_dump("gst", gst[:, 0:2], [128, 2], F32, [e_thr2])

                        wsq2 = [wsq, wsq]
                        wsg2 = [wsg, wsg]
                        _sq = Res()
                        _sg = Res()
                        sq_r = [_sq, _sq]
                        sg_r = [_sg, _sg]
                        for idx, (src, dst, dcol, mat) in enumerate(halves):
                            b = idx % 3
                            b2 = idx % 2
                            q = "sync" if idx % 2 == 0 else "gpsimd"
                            qs = "dS" if idx % 2 == 0 else "dG"
                            buf_free[b].begin_write(p, q)
                            e_l = p.emit(q, lambda o, s=src, b=b: o.dma_start(
                                out=wld[b][:], in_=s), inc=qs, n=16)
                            buf_free[b].set_ready(e_l)

                            sq_r[b2].begin_write(p, "gpsimd")
                            p.wait("gpsimd", e_l)
                            e_sq = p.emit("gpsimd", lambda o, b=b, b2=b2: o.tensor_tensor(
                                wsq2[b2][:], wld[b][:], wld[b][:], ALU.mult), inc="pool")
                            sq_r[b2].set_ready(e_sq)

                            sg_r[b2].begin_write(p, "scalar")
                            p.wait("scalar", e_l)
                            e_sg = p.emit("scalar", lambda o, b=b, b2=b2: o.activation(
                                wsg2[b2][:], wld[b][:], AF.Sign), inc="act")
                            sg_r[b2].set_ready(e_sg)
                            buf_free[b].add_consumer(e_sq)
                            buf_free[b].add_consumer(e_sg)

                            p.wait("vector", e_sq)
                            p.wait("vector", e_sg)
                            e_q = p.emit("vector", lambda o, dst=dst, dcol=dcol, mat=mat,
                                         b2=b2: o.scalar_tensor_tensor(
                                             dst[:, dcol:dcol + 2048], wsq2[b2][:],
                                             thr2[:, mat:mat + 1], wsg2[b2][:],
                                             ALU.is_gt, ALU.mult), inc="dve")
                            sq_r[b2].add_consumer(e_q)
                            sg_r[b2].add_consumer(e_q)
                            if idx == 15:
                                evs["wqfc_q"] = e_q
                        evs["wqpj_q"] = e_q
                        dbg_dump("wqfc0", wqfc[:, 0:1024], [128, 1024], BF16, [e_q])
                        dbg_dump("wqpj0", wqpj[:, 0:1024], [128, 1024], BF16, [e_q])

                        p.wait("tensor", evs["wqpj_q"])
                        p.wait("tensor", e_thr2)
                        e_b_list = []
                        for g in range(2):
                            def bmm(t_, g=g):
                                last = None
                                for c in range(32):
                                    last = t_.matmul(
                                        p1_ps[0:1, g * 512:(g + 1) * 512],
                                        lnbT[:, c:c + 1],
                                        wqpj[:, c * D + g * 512: c * D + (g + 1) * 512],
                                        start=(c == 0), stop=(c == 31))
                                return last
                            e_b = p.emit("tensor", bmm, inc="pe")
                            e_b_list.append(e_b)
                            p.wait("scalar", e_b)
                            e_bc = p.emit("scalar", lambda o, g=g: o.copy(
                                bvec_b[0:1, g * 512:(g + 1) * 512],
                                p1_ps[0:1, g * 512:(g + 1) * 512]), inc="act")

                        p.wait_all("vector", e_b_list)
                        def fold(v):
                            last = None
                            for c in range(32):
                                last = v.tensor_scalar(wqpj[:, c * D:(c + 1) * D],
                                                       wqpj[:, c * D:(c + 1) * D],
                                                       lnwT[:, c:c + 1], None, ALU.mult)
                            return last
                        e_fold = p.emit("vector", fold, inc="dve")
                        evs["wqpj_fold"] = e_fold

                        p.wait("tensor", e_fold)
                        p.wait("tensor", e_bc)

                        def cv_mm(t_):
                            last = None
                            for g in range(2):
                                for c in range(32):
                                    last = t_.matmul(
                                        p1_ps[0:1, g * 512:(g + 1) * 512],
                                        ones_col_b[:, 0:1],
                                        wqpj[:, c * D + g * 512: c * D + (g + 1) * 512],
                                        start=(c == 0), stop=(c == 31))
                            return last
                        e_cv = p.emit("tensor", cv_mm, inc="pe")
                        p.wait("scalar", e_cv)
                        e_cvc = p.emit("scalar", lambda o: o.copy(
                            bvec_c[0:1, :], p1_ps[0:1, 0:1024]), inc="act")

                        p.wait("tensor", e_bc)
                        p.wait("tensor", e_cvc)
                        def hb_mm(t_):
                            t_.matmul(p1_ps[:, 0:512], ones_row_b[0:1, :],
                                      bvec_b[0:1, 0:512], start=True, stop=True)
                            t_.matmul(p1_ps[:, 512:1024], ones_row_b[0:1, :],
                                      bvec_b[0:1, 512:1024], start=True, stop=True)
                            t_.matmul(p1_ps[:, 1024:1536], ones_row_b[0:1, :],
                                      bvec_c[0:1, 0:512], start=True, stop=True)
                            return t_.matmul(p1_ps[:, 1536:2048], ones_row_b[0:1, :],
                                             bvec_c[0:1, 512:1024], start=True, stop=True)
                        e_hb = p.emit("tensor", hb_mm, inc="pe")
                        p.wait("vector", e_hb)
                        e_halfb = p.emit("vector", lambda o: (
                            o.tensor_scalar(halfb[:], p1_ps[:, 0:1024], 0.5, None,
                                            ALU.mult),
                            o.tensor_scalar(halfc[:], p1_ps[:, 1024:2048], 0.5, None,
                                            ALU.mult))[-1], inc="dve")
                        dbg_dump("halfb", halfb[:], [128, 1024], F32, [e_halfb])
                        dbg_dump("wqpjf0", wqpj[:, 0:1024], [128, 1024], BF16, [e_halfb])

                        p.wait("scalar", ("dve", p.cnt["dve"]))
                        for c in range(8):
                            p.emit("scalar", lambda o, c=c: o.dma_start(
                                out=wqfc_s[c * 128:(c + 1) * 128, :],
                                in_=wqfc[:, c * HF:(c + 1) * HF]), inc="dO", n=16)
                        for c in range(0, 32, 2):
                            dstv = pair_view(wqpj_s, c, D)
                            p.emit("scalar", lambda o, d=dstv, c=c: o.dma_start(
                                out=d, in_=wqpj[:, c * D:(c + 2) * D].rearrange(
                                    "p (a n) -> p a n", a=2)), inc="dO", n=16)
                        evs["wq_stored"] = ("dO", p.cnt["dO"])
                        p.flush_block("p1")

                if "P3" in phases:
                    for c in range(8):
                        p.emit("gpsimd", lambda o, c=c: o.dma_start(
                            out=wo_c[c * 128:(c + 1) * 128, :],
                            in_=woT[c * 128:(c + 1) * 128, :]), inc="dG", n=16)
                    evs["wo_c"] = ("dG", p.cnt["dG"])
                    emit_ffn("f1", x_in, y_out if terminal == "P3" else xs,
                             wqfc, wqpj,
                             xin_pre=(f1_xin0, r_xin_f1) if "P1" in phases else None)

        if "P5" in phases or "P6" in phases:
            with (
                nc.sbuf_tensor("qT_b", [128, 8 * T], BF16) as qT_b,
                nc.sbuf_tensor("kT_b", [128, 8 * T], BF16) as kT_b,
                nc.sbuf_tensor("v_b", [128, NT * D], BF16) as v_b,
            ):
                if "P5" in phases:
                    with contextlib.ExitStack() as ps5:
                        wqkv_b = ps5.enter_context(nc.sbuf_tensor("a_wqkv", [128, 8 * 3 * D], BF16))
                        axin0 = ps5.enter_context(nc.sbuf_tensor("a_xin0", [128, D], F32))
                        axin1 = ps5.enter_context(nc.sbuf_tensor("a_xin1", [128, D], F32))
                        alnx = ps5.enter_context(nc.sbuf_tensor("a_lnx", [128, D], BF16))
                        alxT = ps5.enter_context(nc.sbuf_tensor("a_lxT", [128, D], BF16))
                        qr_f = ps5.enter_context(nc.sbuf_tensor("a_qr", [128, D], F32))
                        kr_f = ps5.enter_context(nc.sbuf_tensor("a_kr", [128, D], F32))
                        qrope = ps5.enter_context(nc.sbuf_tensor("a_qrope", [128, D], BF16))
                        krope = ps5.enter_context(nc.sbuf_tensor("a_krope", [128, D], BF16))
                        tq1 = ps5.enter_context(nc.sbuf_tensor("a_tq1", [128, 512], F32))
                        tq2 = ps5.enter_context(nc.sbuf_tensor("a_tq2", [128, 512], F32))
                        tk1 = ps5.enter_context(nc.sbuf_tensor("a_tk1", [128, 512], F32))
                        tk2 = ps5.enter_context(nc.sbuf_tensor("a_tk2", [128, 512], F32))
                        qkv_ps = ps5.enter_context(nc.psum_tensor("a_qkv", [128, 3 * D], F32))
                        stA_ps = ps5.enter_context(nc.psum_tensor("a_stA", [128, 1024], BF16))
                        stB_ps = ps5.enter_context(nc.psum_tensor("a_stB", [128, 1024], BF16))
                        _ = None
                        xin = [axin0, axin1]
                        r_xin = [Res(), Res()]
                        r_lnx = Res()
                        r_lxT = Res()
                        r_stA = Res()
                        r_stB = Res()
                        r_qkv = Res()
                        r_qr = Res()
                        r_kr = Res()
                        r_qrope = Res()
                        r_krope = Res()

                        for c in range(8):
                            p.emit("gpsimd", lambda o, c=c: o.dma_start(
                                out=wqkv_b[:, c * 3 * D:(c + 1) * 3 * D],
                                in_=wqkvT[c * 128:(c + 1) * 128, :]), inc="dG", n=16)
                        e_wqkv = ("dG", p.cnt["dG"])

                        for t in range(NT):
                            par = t % 2
                            sc = 8 * par

                            r_xin[par].begin_write(p, "sync")
                            if "P3" in phases:
                                p.wait("sync", evs[f"f1_store_{t}"])
                            e_ld = p.emit("sync", lambda o, t=t, par=par: o.dma_start(
                                out=xin[par][:], in_=xs[t * 128:(t + 1) * 128, :]),
                                inc="dS", n=16)
                            r_xin[par].set_ready(e_ld)

                            p.wait("vector", e_ld)
                            r_lnx.begin_write(p, "vector")

                            p.gemit("vector", lambda v, par=par, sc=sc: (
                                v.reduce_sum(st[:, sc + 0:sc + 1], xin[par][:], AX.X),
                                v.scalar_tensor_tensor(alnx[:], xin[par][:], 1.0,
                                                       xin[par][:], ALU.mult, ALU.mult,
                                                       accum_out=st[:, sc + 1:sc + 2]))[-1])
                            p.gemit("vector", lambda v, sc=sc: v.tensor_scalar(
                                st[:, sc + 2:sc + 3], st[:, sc + 0:sc + 1],
                                1.0 / D, None, ALU.mult))
                            p.gemit("vector", lambda v, sc=sc: v.tensor_tensor(
                                st[:, sc + 3:sc + 4], st[:, sc + 2:sc + 3],
                                st[:, sc + 2:sc + 3], ALU.mult))
                            e_st = p.gemit("vector", lambda v, sc=sc: v.scalar_tensor_tensor(
                                st[:, sc + 4:sc + 5], st[:, sc + 1:sc + 2], 1.0 / D,
                                st[:, sc + 3:sc + 4], ALU.mult, ALU.subtract))
                            p.wait("scalar", e_st)

                            def rstd(s, sc=sc):
                                s.activation(st[:, sc + 5:sc + 6], st[:, sc + 4:sc + 5],
                                             AF.Ln, bias=cst[:, 0:1])
                                return s.activation(st[:, sc + 6:sc + 7],
                                                    st[:, sc + 5:sc + 6], AF.Exp, scale=-0.5)
                            e_rs = p.emit("scalar", rstd, inc="act")
                            p.wait("vector", e_rs)
                            e_ln = p.emit("vector", lambda o, par=par, sc=sc: o.tensor_scalar(
                                alnx[:], xin[par][:], st[:, sc + 2:sc + 3],
                                st[:, sc + 6:sc + 7], ALU.subtract, ALU.mult), inc="dve")
                            r_lnx.set_ready(e_ln)
                            r_xin[par].add_consumer(e_ln)

                            r_stA.begin_write(p, "tensor")
                            p.wait("tensor", e_ln)

                            def tp_ln(t_):
                                last = None
                                for c in range(8):
                                    last = t_.transpose(stA_ps[:, c * 128:(c + 1) * 128],
                                                        alnx[:, c * 128:(c + 1) * 128],
                                                        ident_b[:])
                                return last
                            e_tp = p.emit("tensor", tp_ln, inc="pe")
                            r_stA.set_ready(e_tp)
                            r_lnx.add_consumer(e_tp)

                            r_lxT.begin_write(p, "scalar")
                            p.wait("scalar", e_tp)
                            e_lxT = p.emit("scalar", lambda o: o.copy(
                                alxT[:], stA_ps[:, :]), inc="act")
                            r_lxT.set_ready(e_lxT)
                            r_stA.add_consumer(e_lxT)

                            p.wait("tensor", e_lxT)
                            p.wait("tensor", e_wqkv)
                            r_qkv.begin_write(p, "tensor")

                            def qkv_mm(t_):
                                last = None
                                for g in range(6):
                                    for c in range(8):
                                        last = t_.matmul(
                                            qkv_ps[:, g * 512:(g + 1) * 512],
                                            alxT[:, c * 128:(c + 1) * 128],
                                            wqkv_b[:, c * 3 * D + g * 512:
                                                   c * 3 * D + (g + 1) * 512],
                                            start=(c == 0), stop=(c == 7))
                                return last
                            e_qkv = p.emit("tensor", qkv_mm, inc="pe")
                            r_qkv.set_ready(e_qkv)
                            r_lxT.add_consumer(e_qkv)

                            p.wait("vector", e_qkv)
                            r_qr.begin_write(p, "vector")
                            r_kr.begin_write(p, "vector")
                            e_cpq = p.gemit("vector", lambda v: v.tensor_copy(
                                qr_f[:], qkv_ps[:, 0:1024]))
                            p.wait("scalar", e_qkv)
                            e_cpk = p.emit("scalar", lambda o: o.copy(
                                kr_f[:], qkv_ps[:, 1024:2048]), inc="act")
                            e_cp = e_cpk
                            p.wait("gpsimd", e_cpk)
                            e_ksq = p.gemit("gpsimd", lambda g: (
                                g.tensor_tensor(tk1[:], kr_f[:, 0:512], kr_f[:, 0:512],
                                                ALU.mult),
                                g.tensor_tensor(tk2[:], kr_f[:, 512:1024],
                                                kr_f[:, 512:1024], ALU.mult))[-1])
                            p.gemit("vector", lambda v: (
                                v.tensor_tensor(tq1[:], qr_f[:, 0:512], qr_f[:, 0:512],
                                                ALU.mult),
                                v.tensor_tensor(tq2[:], qr_f[:, 512:1024],
                                                qr_f[:, 512:1024], ALU.mult))[-1])
                            p.wait("vector", e_ksq)

                            def sq_red(v):
                                v.reduce_sum(st[:, 16:24],
                                             tq1[:].rearrange("p (a b) -> p a b", b=64),
                                             AX.X)
                                v.reduce_sum(st[:, 24:32],
                                             tq2[:].rearrange("p (a b) -> p a b", b=64),
                                             AX.X)
                                v.reduce_sum(st[:, 32:40],
                                             tk1[:].rearrange("p (a b) -> p a b", b=64),
                                             AX.X)
                                return v.reduce_sum(
                                    st[:, 40:48],
                                    tk2[:].rearrange("p (a b) -> p a b", b=64), AX.X)
                            e_sqk = p.gemit("vector", sq_red)
                            r_qkv.add_consumer(e_cpq)
                            r_qkv.add_consumer(e_cpk)

                            p.wait("scalar", e_sqk)

                            def rms_rstd(s):
                                s.activation(st[:, 48:64], st[:, 16:32], AF.Ln,
                                             bias=cst[:, 1:2], scale=1.0 / HD)
                                s.activation(st[:, 48:64], st[:, 48:64], AF.Exp, scale=-0.5)
                                s.activation(st[:, 16:32], st[:, 32:48], AF.Ln,
                                             bias=cst[:, 1:2], scale=1.0 / HD)
                                return s.activation(st[:, 16:32], st[:, 16:32], AF.Exp,
                                                    scale=-0.5)
                            e_rms = p.emit("scalar", rms_rstd, inc="act")

                            p.wait("vector", e_rms)

                            def evq(v, t=t):
                                v.tensor_tensor(
                                    qr_f[:].rearrange("p (a b) -> p a b", b=64),
                                    qr_f[:].rearrange("p (a b) -> p a b", b=64),
                                    bass.AP(st[:].tensor, st[:].offset + 48,
                                            [list(st[:].ap[0]), [1, 16], [0, 64]]),
                                    ALU.mult)
                                return v.tensor_tensor(
                                    kr_f[:].rearrange("p (a b) -> p a b", b=64),
                                    kr_f[:].rearrange("p (a b) -> p a b", b=64),
                                    bass.AP(st[:].tensor, st[:].offset + 16,
                                            [list(st[:].ap[0]), [1, 16], [0, 64]]),
                                    ALU.mult)
                            e_vv = p.emit("scalar", lambda o, t=t: o.copy(
                                v_b[:, t * D:(t + 1) * D], qkv_ps[:, 2048:3072]),
                                inc="act")
                            e_ev = p.emit("vector", evq, inc="dve")
                            r_qr.set_ready(e_ev)
                            r_kr.set_ready(e_ev)
                            r_qkv.add_consumer(e_vv)

                            r_qrope.begin_write(p, "vector")

                            def rope_q(v, t=t):
                                A = head_bcast(ropeq[:], 0 + t * 32, 16, 32)
                                B = head_bcast(ropeq[:], RS + t * 32, 16, 32)
                                C = head_bcast(ropeq[:], 2 * RS + t * 32, 16, 32)
                                Dd = head_bcast(ropeq[:], 3 * RS + t * 32, 16, 32)
                                q1 = head_view(qr_f[:], 0, 16, 32, 64)
                                q2 = head_view(qr_f[:], 32, 16, 32, 64)
                                o1 = head_view(qrope[:], 0, 16, 32, 64)
                                o2 = head_view(qrope[:], 32, 16, 32, 64)
                                t1v = tq1[:].rearrange("p (a b) -> p a b", b=32)
                                t2v = tq2[:].rearrange("p (a b) -> p a b", b=32)
                                v.tensor_tensor(t1v, q1, A, ALU.mult)
                                v.tensor_tensor(t2v, q2, B, ALU.mult)
                                v.tensor_tensor(o1, t1v, t2v, ALU.subtract)
                                v.tensor_tensor(t1v, q1, C, ALU.mult)
                                v.tensor_tensor(t2v, q2, Dd, ALU.mult)
                                return v.tensor_tensor(o2, t1v, t2v, ALU.add)
                            e_rq = p.emit("vector", rope_q, inc="dve")
                            r_qrope.set_ready(e_rq)
                            r_qr.add_consumer(e_rq)

                            r_krope.begin_write(p, "gpsimd")
                            p.wait("gpsimd", e_ev)

                            def rope_k(g, t=t):
                                A = head_bcast(ropek[:], 0 + t * 32, 16, 32)
                                B = head_bcast(ropek[:], RS + t * 32, 16, 32)
                                C = head_bcast(ropek[:], 2 * RS + t * 32, 16, 32)
                                Dd = head_bcast(ropek[:], 3 * RS + t * 32, 16, 32)
                                k1 = head_view(kr_f[:], 0, 16, 32, 64)
                                k2 = head_view(kr_f[:], 32, 16, 32, 64)
                                o1 = head_view(krope[:], 0, 16, 32, 64)
                                o2 = head_view(krope[:], 32, 16, 32, 64)
                                t1v = tk1[:].rearrange("p (a b) -> p a b", b=32)
                                t2v = tk2[:].rearrange("p (a b) -> p a b", b=32)
                                g.tensor_tensor(t1v, k1, A, ALU.mult)
                                g.tensor_tensor(t2v, k2, B, ALU.mult)
                                g.tensor_tensor(o1, t1v, t2v, ALU.subtract)
                                g.tensor_tensor(t1v, k1, C, ALU.mult)
                                g.tensor_tensor(t2v, k2, Dd, ALU.mult)
                                return g.tensor_tensor(o2, t1v, t2v, ALU.add)
                            e_rk = p.emit("gpsimd", rope_k, inc="pool")
                            r_krope.set_ready(e_rk)
                            r_kr.add_consumer(e_rk)

                            r_stA.begin_write(p, "tensor")
                            p.wait("tensor", e_rq)

                            def tp_q(t_):
                                last = None
                                for hh in range(16):
                                    po = 64 * (hh % 2)
                                    pc = hh // 2
                                    last = t_.transpose(
                                        stA_ps[po:po + 64, pc * 128:(pc + 1) * 128],
                                        qrope[:, hh * 64:(hh + 1) * 64], ident_b[:])
                                return last
                            e_tq = p.emit("tensor", tp_q, inc="pe")
                            r_stA.set_ready(e_tq)
                            r_qrope.add_consumer(e_tq)

                            r_stB.begin_write(p, "tensor")
                            p.wait("tensor", e_rk)

                            def tp_k(t_):
                                last = None
                                for hh in range(16):
                                    po = 64 * (hh % 2)
                                    pc = hh // 2
                                    last = t_.transpose(
                                        stB_ps[po:po + 64, pc * 128:(pc + 1) * 128],
                                        krope[:, hh * 64:(hh + 1) * 64], ident_b[:])
                                return last
                            e_tk = p.emit("tensor", tp_k, inc="pe")
                            r_stB.set_ready(e_tk)
                            r_krope.add_consumer(e_tk)

                            p.wait("vector", e_tq)

                            p.wait("scalar", e_tq)

                            def evq2a(o, t=t):
                                iA = stA_ps[:, :].rearrange("p (a b) -> p a b", b=128)
                                oA = bass.AP(qT_b[:].tensor, qT_b[:].offset + t * 128,
                                             [list(qT_b[:].ap[0]), [T, 8], [1, 128]])
                                return o.copy(oA, iA)
                            e_ev2 = p.emit("scalar", evq2a, inc="act")
                            p.wait("scalar", e_tk)

                            def evq2b(o, t=t):
                                iB = stB_ps[:, :].rearrange("p (a b) -> p a b", b=128)
                                oB = bass.AP(kT_b[:].tensor, kT_b[:].offset + t * 128,
                                             [list(kT_b[:].ap[0]), [T, 8], [1, 128]])
                                return o.copy(oB, iB)
                            e_ev2b = p.emit("scalar", evq2b, inc="act")
                            r_stA.add_consumer(e_ev2)
                            r_stB.add_consumer(e_ev2b)
                        p.flush_block("p5")

                if "P6" in phases:
                    with contextlib.ExitStack() as ps6:
                        wo_b = ps6.enter_context(nc.sbuf_tensor("c_wo", [128, 8 * D], BF16))
                        Pb0 = ps6.enter_context(nc.sbuf_tensor("c_P0", [128, T], BF16))
                        Pb1 = ps6.enter_context(nc.sbuf_tensor("c_P1", [128, T], BF16))
                        Pn0 = ps6.enter_context(nc.sbuf_tensor("c_Pn0", [128, T], BF16))
                        Pn1 = ps6.enter_context(nc.sbuf_tensor("c_Pn1", [128, T], BF16))
                        PT0 = ps6.enter_context(nc.sbuf_tensor("c_PT0", [128, T], BF16))
                        PT1 = ps6.enter_context(nc.sbuf_tensor("c_PT1", [128, T], BF16))
                        ot_sb = ps6.enter_context(nc.sbuf_tensor("c_ot", [128, 1024], BF16))
                        cxr0 = ps6.enter_context(nc.sbuf_tensor("c_xr0", [128, D], F32))
                        cxr1 = ps6.enter_context(nc.sbuf_tensor("c_xr1", [128, D], F32))
                        cxo0 = ps6.enter_context(nc.sbuf_tensor("c_xo0", [128, D], F32))
                        cxo1 = ps6.enter_context(nc.sbuf_tensor("c_xo1", [128, D], F32))
                        S_ps = ps6.enter_context(nc.psum_tensor("c_S", [128, min(T, 2048)], F32))
                        PT_ps = ps6.enter_context(nc.psum_tensor("c_PT", [128, 1024], BF16))
                        OT_ps = ps6.enter_context(nc.psum_tensor("c_OT", [128, 1024], F32))
                        WO_ps = ps6.enter_context(nc.psum_tensor("c_WO", [128, 512], F32))
                        _ = None
                        Pb2 = ps6.enter_context(nc.sbuf_tensor("c_P2", [128, T], BF16))
                        Pb3 = ps6.enter_context(nc.sbuf_tensor("c_P3", [128, T], BF16))
                        Pn2 = ps6.enter_context(nc.sbuf_tensor("c_Pn2", [128, T], BF16))
                        Pn3 = ps6.enter_context(nc.sbuf_tensor("c_Pn3", [128, T], BF16))
                        PT2 = ps6.enter_context(nc.sbuf_tensor("c_PT2", [128, T], BF16))
                        PT3 = ps6.enter_context(nc.sbuf_tensor("c_PT3", [128, T], BF16))
                        Pb = [Pb0, Pb1, Pb2, Pb3]
                        Pn = [Pn0, Pn1, Pn2, Pn3]
                        PT = [PT0, PT1, PT2, PT3]
                        xr = [cxr0, cxr1]
                        xo = [cxo0, cxo1]

                        if "wo_c" in evs:
                            p.wait("sync", evs["wo_c"])
                            for c in range(8):
                                p.emit("sync", lambda o, c=c: o.dma_start(
                                    out=wo_b[:, c * D:(c + 1) * D],
                                    in_=wo_c[c * 128:(c + 1) * 128, :]),
                                    inc="dS", n=16)
                            e_wo = ("dS", p.cnt["dS"])
                        else:
                            for c in range(8):
                                p.emit("gpsimd", lambda o, c=c: o.dma_start(
                                    out=wo_b[:, c * D:(c + 1) * D],
                                    in_=woT[c * 128:(c + 1) * 128, :]), inc="dG", n=16)
                            e_wo = ("dG", p.cnt["dG"])

                        r_SH = [Res(), Res()]
                        r_Pb = [Res(), Res(), Res(), Res()]
                        r_Pn = [Res(), Res(), Res(), Res()]
                        r_PTps = Res()
                        r_PTsb = [Res(), Res(), Res(), Res()]
                        r_OT = Res()
                        r_ot = Res()
                        r_WO = Res()
                        r_xr = [Res(), Res()]
                        r_xo = [Res(), Res()]
                        r_linv = [Res(), Res()]
                        pend_wo = [None]
                        pend_gen = [None]

                        for i in range(NT):
                            ip = i % 2
                            ncols = (i + 1) * 128
                            r_xr[ip].begin_write(p, "sync")
                            if "P5" not in phases and "P3" in phases:
                                p.wait("sync", evs[f"f1_store_{i}"])
                            e_lr = p.emit("sync", lambda o, i=i, ip=ip: o.dma_start(
                                out=xr[ip][:], in_=xs[i * 128:(i + 1) * 128, :]),
                                inc="dS", n=16)
                            r_xr[ip].set_ready(e_lr)

                            def stage1(h, pump=None, i=i, ncols=ncols):
                                hp = h % 2
                                bi = h % 4
                                po = 64 * hp
                                pc = h // 2
                                pb = 32 + hp * 16
                                c_nm = pb + 5
                                c_lA, c_lB, c_li = pb + 6, pb + 7, pb + 9
                                c_l = pb + 8
                                colsA = min(ncols, 1024)
                                colsB = ncols - colsA
                                diag_lo = i * 128
                                nch = (ncols + 511) // 512
                                sbase = 1024 * (h % 2) if ncols <= 1024 else 0

                                lqv = qT_b[po:po + 64,
                                           pc * T + i * 128: pc * T + (i + 1) * 128]
                                cm_evs = []
                                e_s_half = {0: None, 1: None}
                                seen_half = set()
                                for ci in range(nch):
                                    c0 = ci * 512
                                    w = min(512, ncols - c0)
                                    half = (sbase + c0) // 1024
                                    if half not in seen_half:
                                        seen_half.add(half)
                                        r_SH[half].begin_write(p, "tensor")
                                    has_diag = c0 <= diag_lo < c0 + w

                                    def smm(t_, c0=c0, w=w, po=po, pc=pc, sbase=sbase,
                                            has_diag=has_diag, dl=diag_lo):
                                        last = t_.matmul(
                                            S_ps[:, sbase + c0:sbase + c0 + w], lqv,
                                            kT_b[po:po + 64, pc * T + c0: pc * T + c0 + w],
                                            start=True, stop=not has_diag)
                                        if has_diag:
                                            last = t_.matmul(
                                                S_ps[:, sbase + dl:sbase + dl + 128],
                                                ident_b[:], mask_b[:],
                                                start=False, stop=True)
                                        return last
                                    e_sc = p.emit("tensor", smm, inc="pe")
                                    e_s_half[half] = e_sc
                                    p.wait("vector", e_sc)
                                    e_cm = p.emit("vector", lambda v, ci=ci, c0=c0, w=w,
                                                  sbase=sbase:
                                                  v.reduce_max(st[:, pb + ci:pb + ci + 1],
                                                               S_ps[:, sbase + c0:
                                                                    sbase + c0 + w],
                                                               AX.X,
                                                               negate=True), inc="dve")
                                    cm_evs.append(e_cm)
                                for half in seen_half:
                                    r_SH[half].set_ready(e_s_half[half])
                                if pump is not None:
                                    pump()
                                p.wait("vector", cm_evs[-1])
                                e_nm = p.emit("vector", lambda v, nch=nch: v.tensor_reduce(
                                    st[:, c_nm:c_nm + 1], st[:, pb:pb + nch], AX.X,
                                    ALU.min), inc="dve")

                                r_Pb[bi].begin_write(p, "scalar")
                                p.wait("scalar", e_nm)
                                dstlA = c_lA if colsB else c_l
                                e_eA = p.emit("scalar", lambda o, bi=bi, colsA=colsA,
                                              c_nm=c_nm, d=dstlA, sbase=sbase:
                                              o.activation(
                                                  Pb[bi][:, 0:colsA],
                                                  S_ps[:, sbase:sbase + colsA],
                                                  AF.Exp, bias=st[:, c_nm:c_nm + 1],
                                                  accum_out=st[:, d:d + 1]), inc="act")
                                r_SH[sbase // 1024].add_consumer(e_eA)
                                e_eB = None
                                if colsB:
                                    e_eB = p.emit("scalar", lambda o, bi=bi, colsB=colsB,
                                                  c_nm=c_nm, d=c_lB: o.activation(
                                                      Pb[bi][:, 1024:1024 + colsB],
                                                      S_ps[:, 1024:1024 + colsB],
                                                      AF.Exp, bias=st[:, c_nm:c_nm + 1],
                                                      accum_out=st[:, d:d + 1]), inc="act")
                                    r_SH[1].add_consumer(e_eB)
                                r_Pb[bi].set_ready(e_eB or e_eA)

                                r_linv[hp].begin_write(p, "vector")
                                p.wait("vector", e_eB or e_eA)
                                if colsB:
                                    p.gemit("vector", lambda v, a=c_lA, bcol=c_lB, d=c_l:
                                            v.tensor_tensor(st[:, d:d + 1], st[:, a:a + 1],
                                                            st[:, bcol:bcol + 1], ALU.add))
                                e_li = p.gemit("vector", lambda v, scol=c_l, d=c_li:
                                               v.reciprocal(st[:, d:d + 1],
                                                            st[:, scol:scol + 1]))
                                r_linv[hp].set_ready(e_li)

                                r_Pn[bi].begin_write(p, "gpsimd")
                                p.wait("gpsimd", e_li)
                                e_pn = p.emit("gpsimd", lambda g, bi=bi, ncols=ncols,
                                              d=c_li: g.tensor_tensor(
                                                  Pn[bi][:, 0:ncols], Pb[bi][:, 0:ncols],
                                                  bcast_free(st[:, d:d + 1], ncols),
                                                  ALU.mult), inc="pool")
                                r_Pn[bi].set_ready(e_pn)
                                r_Pb[bi].add_consumer(e_pn)
                                r_linv[hp].add_consumer(e_pn)
                                if pump is not None:
                                    pump()

                            def stage2pair(hE, i=i, ncols=ncols):
                                nb = i + 1
                                banks = [PT_ps[:, :],
                                         WO_ps[:, :].bitcast(BF16)]
                                bank_res = [r_PTps, r_WO]
                                pv_done = {}
                                for hp in (0, 1):
                                    p.wait("tensor", r_Pn[(hE + hp) % 4].ready)
                                r_OT.begin_write(p, "tensor")
                                e_pv = None
                                for b0 in range(0, nb, 8):
                                    yield
                                    bw = min(8, nb - b0)
                                    ev_evac = {}
                                    for hp in (0, 1):
                                        bi = (hE + hp) % 4
                                        bank_res[hp].begin_write(p, "tensor")

                                        def tp_p(t_, bi=bi, b0=b0, bw=bw, hp=hp):
                                            last = None
                                            for c in range(bw):
                                                last = t_.transpose(
                                                    banks[hp][:, c * 128:(c + 1) * 128],
                                                    Pn[bi][:, (b0 + c) * 128:
                                                            (b0 + c + 1) * 128],
                                                    ident_b[:])
                                            return last
                                        e_t = p.emit("tensor", tp_p, inc="pe")
                                        bank_res[hp].set_ready(e_t)
                                        r_Pn[bi].add_consumer(e_t)
                                        eng = "scalar" if hp == 0 else "vector"
                                        if b0 == 0:
                                            r_PTsb[bi].begin_write(p, eng)
                                        p.wait(eng, e_t)
                                        if hp == 0:
                                            e_e = p.emit("scalar", lambda o, bi=bi, b0=b0,
                                                         bw=bw, hp=hp: o.copy(
                                                             PT[bi][:, b0 * 128:
                                                                    (b0 + bw) * 128],
                                                             banks[hp][:, 0:bw * 128]),
                                                         inc="act")
                                        else:
                                            e_e = p.emit("vector", lambda o, bi=bi, b0=b0,
                                                         bw=bw, hp=hp: o.tensor_copy(
                                                             PT[bi][:, b0 * 128:
                                                                    (b0 + bw) * 128],
                                                             banks[hp][:, 0:bw * 128]),
                                                         inc="dve")
                                        bank_res[hp].add_consumer(e_e)
                                        ev_evac[hp] = e_e
                                        if b0 == 0:
                                            pv_done[hp] = []
                                        pv_done[hp].append(e_e)

                                    p.wait_all("tensor", [ev_evac[0], ev_evac[1]])

                                    def pvb(t_, hE=hE, i=i, b0=b0, bw=bw):
                                        last = None
                                        for b in range(b0, b0 + bw):
                                            for hp in (0, 1):
                                                h = hE + hp
                                                bi = h % 4
                                                po = 64 * hp
                                                off = 512 * hp
                                                last = t_.matmul(
                                                    OT_ps[po:po + 64, off:off + 128],
                                                    v_b[:, b * D + h * HD:
                                                        b * D + (h + 1) * HD],
                                                    PT[bi][:, b * 128:(b + 1) * 128],
                                                    start=(b == 0), stop=(b == i),
                                                    tile_position=(0, po))
                                        return last
                                    e_pv = p.emit("tensor", pvb, inc="pe")
                                    for hp in (0, 1):
                                        r_PTsb[(hE + hp) % 4].add_consumer(e_pv)
                                r_OT.set_ready(e_pv)

                                pc = hE // 2
                                if pc == 0:
                                    r_ot.begin_write(p, "vector")
                                p.wait("vector", e_pv)
                                e_oe = p.emit("vector", lambda o, pc=pc: (
                                    o.tensor_copy(ot_sb[0:64, pc * 128:(pc + 1) * 128],
                                                  OT_ps[0:64, 0:128]),
                                    o.tensor_copy(ot_sb[64:128, pc * 128:(pc + 1) * 128],
                                                  OT_ps[64:128, 512:640]))[-1],
                                    inc="dve")
                                r_OT.add_consumer(e_oe)
                                if pc == 7:
                                    r_ot.set_ready(e_oe)

                            def wo_block(i2):
                                ip2 = i2 % 2
                                p.wait("tensor", r_ot.ready)
                                p.wait("tensor", e_wo)
                                r_xo[ip2].begin_write(p, "vector")
                                wdst = [WO_ps[:, 0:512],
                                        PT_ps[:, :].bitcast(F32)[:, 0:512]]
                                wres = [r_WO, r_PTps]
                                e_res = None
                                for g in range(2):
                                    wres[g].begin_write(p, "tensor")

                                    def womm(t_, g=g):
                                        last = None
                                        for c in range(8):
                                            last = t_.matmul(
                                                wdst[g],
                                                ot_sb[:, c * 128:(c + 1) * 128],
                                                wo_b[:, c * D + g * 512:
                                                     c * D + (g + 1) * 512],
                                                start=(c == 0), stop=(c == 7))
                                        return last
                                    e_wmm = p.emit("tensor", womm, inc="pe")
                                    wres[g].set_ready(e_wmm)
                                    if g == 1:
                                        r_ot.add_consumer(e_wmm)
                                    p.wait("vector", e_wmm)
                                    p.wait("vector", r_xr[ip2].ready)
                                    e_res = p.emit("vector", lambda o, ip2=ip2, g=g:
                                                   o.tensor_tensor(
                                                       xo[ip2][:, g * 512:(g + 1) * 512],
                                                       wdst[g],
                                                       xr[ip2][:, g * 512:(g + 1) * 512],
                                                       ALU.add), inc="dve")
                                    wres[g].add_consumer(e_res)
                                    r_xr[ip2].add_consumer(e_res)
                                r_xo[ip2].set_ready(e_res)

                                p.wait("scalar", e_res)
                                p6_dst = y_out if terminal == "P6" else xs
                                e_sto = p.emit("scalar", lambda o, i2=i2, ip2=ip2,
                                               d=p6_dst: o.dma_start(
                                                   out=d[i2 * 128:(i2 + 1) * 128, :],
                                                   in_=xo[ip2][:]), inc="dO", n=16)
                                r_xo[ip2].add_consumer(e_sto)
                                evs[f"a_store_{i2}"] = e_sto

                            for pr in range(NH // 2):
                                gen = (stage2pair(2 * pr - 2) if pr > 0
                                       else pend_gen[0])

                                def pump(g=gen):
                                    if g is not None:
                                        next(g, None)
                                stage1(2 * pr, pump)
                                stage1(2 * pr + 1, pump)
                                if gen is not None:
                                    for _ in gen:
                                        pass
                                if pr == 0:
                                    pend_gen[0] = None
                                    if pend_wo[0] is not None:
                                        wo_block(pend_wo[0])
                                        pend_wo[0] = None
                            pend_gen[0] = stage2pair(NH - 2)
                            pend_wo[0] = i
                        for _ in pend_gen[0]:
                            pass
                        wo_block(pend_wo[0])
                        p.flush_block("p6")

        if "P7" in phases:
            with (
                nc.sbuf_tensor("wqfc2", [128, 8 * HF], BF16) as wqfc2,
                nc.sbuf_tensor("wqpj2", [128, 32 * D], BF16) as wqpj2,
            ):
                if "P1" in phases:
                    p.wait("sync", evs["wq_stored"])
                    p.wait("scalar", evs["wq_stored"])
                for c in range(8):
                    q, qs = ("sync", "dS") if c % 2 == 0 else ("scalar", "dO")
                    p.emit(q, lambda o, c=c: o.dma_start(
                        out=wqfc2[:, c * HF:(c + 1) * HF],
                        in_=wqfc_s[c * 128:(c + 1) * 128, :]), inc=qs, n=16)
                for ci, c in enumerate(range(0, 32, 2)):
                    srcv = pair_view(wqpj_s, c, D)
                    q, qs = ("sync", "dS") if ci % 2 == 0 else ("scalar", "dO")
                    p.emit(q, lambda o, s=srcv, c=c: o.dma_start(
                        out=wqpj2[:, c * D:(c + 2) * D].rearrange("p (a n) -> p a n", a=2),
                        in_=s), inc=qs, n=16)
                p.wait("tensor", ("dS", p.cnt["dS"]))
                p.wait("tensor", ("dO", p.cnt["dO"]))
                src_evs = ([evs.get(f"a_store_{t}") for t in range(NT)]
                           if "P6" in phases else
                           ([evs.get(f"f1_store_{t}") for t in range(NT)]
                            if "P3" in phases else None))
                emit_ffn("f2", xs, y_out, wqfc2, wqpj2, src_wait_evs=src_evs)

        p.wait("sync", ("dO", p.cnt["dO"]))
        p.flush_block("fin")

    return nc


_NC_CACHE = {}


def _get_nc():
    if "full" not in _NC_CACHE:
        _NC_CACHE["full"] = build_program()
    return _NC_CACHE["full"]


def kernel(L0_attn_x, L0_attn_wqkv, L0_attn_wo, L0_attn_q_norm_w, L0_attn_k_norm_w,
           L0_ffn_W_fc, L0_ffn_ln_w, L0_ffn_ln_b, L0_ffn_W_proj):
    x = np.ascontiguousarray(np.asarray(L0_attn_x, dtype=np.float32))
    wqkvT = np.ascontiguousarray(np.asarray(L0_attn_wqkv, dtype=np.float32).T)
    woT = np.ascontiguousarray(np.asarray(L0_attn_wo, dtype=np.float32).T)
    wfcT = np.ascontiguousarray(np.asarray(L0_ffn_W_fc, dtype=np.float32).T)
    wpjT = np.ascontiguousarray(np.asarray(L0_ffn_W_proj, dtype=np.float32).T)
    qnw = np.ascontiguousarray(np.asarray(L0_attn_q_norm_w, dtype=np.float32).reshape(1, HD))
    knw = np.ascontiguousarray(np.asarray(L0_attn_k_norm_w, dtype=np.float32).reshape(1, HD))
    lnw = np.ascontiguousarray(np.asarray(L0_ffn_ln_w, dtype=np.float32).reshape(32, 128))
    lnb = np.ascontiguousarray(np.asarray(L0_ffn_ln_b, dtype=np.float32).reshape(32, 128))

    nc = _get_nc()
    in_maps = []
    for b in range(NCORES):
        in_maps.append({
            "x": np.ascontiguousarray(x[b]),
            "wqkvT": wqkvT, "woT": woT, "wfcT": wfcT, "wpjT": wpjT,
            "qnw": qnw, "knw": knw, "lnw": lnw, "lnb": lnb,
        })
    res = run_bass_kernel_spmd(nc, in_maps, core_ids=list(range(NCORES)))
    return np.stack([np.asarray(r["y"], dtype=np.float32) for r in res.results], axis=0)

